# version 01
# baseline (speedup 1.0000x reference)
import numpy as np
import ml_dtypes
from contextlib import ExitStack

import concourse.bass as bass
import concourse.mybir as mybir
from concourse.bass_utils import run_bass_kernel_spmd

F32 = mybir.dt.float32
BF16 = mybir.dt.bfloat16
AF = mybir.ActivationFunctionType
ALU = mybir.AluOpType
AX = mybir.AxisListType

NCORES = 8
B, T, D = 2, 16384, 1024
NH, HS = 16, 64
E, HID = 8, 2816
TPC = T * B // NCORES
TT = 1024
NTILE = TPC // TT
NTC = TT // 128
NKC = D // 128
NHC = HID // 128
HGROUPS = [(0, 8), (8, 16), (16, 22)]
HBLOCKS = [(0, 4), (4, 8), (8, 12), (12, 16), (16, 20), (20, 22)]
NCH = TPC // 128
LNEPS = 1e-5

ENGS = ("sync", "gpsimd", "vector", "tensor", "scalar")


class Sem:
    __slots__ = ("h", "count")

    def __init__(self, h):
        self.h = h
        self.count = 0


class Tr:

    def __init__(self, active, eng):
        self.a = active
        self.e = eng
        self.hist = {}

    def op(self, name, fn, *args, inc=None, incn=1, **kw):
        if self.a == name:
            inst = getattr(self.e, fn)(*args, **kw)
            if inc is not None:
                inst.then_inc(inc.h, incn)
        if inc is not None:
            inc.count += incn
            return inc.count
        return None

    def wait(self, name, sem, val):
        if val is None or val <= 0:
            return
        if self.a == name:
            self.e.wait_ge(sem.h, val)

    def dma(self, name, out, in_, sem, **kw):
        return self.op(name, "dma_start", inc=sem, incn=16, out=out, in_=in_, **kw)

    def opw(self, name, fn, *args, inc, incn=1, **kw):
        c = self.op(name, fn, *args, inc=inc, incn=incn, **kw)
        self.wait(name, inc, c)
        return c


def run_phase(nc, sems, emit):
    base = [(s, s.count) for s in sems]
    with nc.Block() as block:
        for name in ENGS:
            for s, c in base:
                s.count = c

            def body(eng, _n=name):
                emit(Tr(_n, eng))

            getattr(block, name)(body)


class NS:
    pass


def build_program(phases=("p0", "p1", "p2", "p3")):
    nc = bass.Bass("TRN2", target_bir_lowering=False)
    t = NS()

    x_in = nc.dram_tensor("x", [TPC, D], F32, kind="ExternalInput")
    wqkv_in = nc.dram_tensor("wqkv", [3 * D, D], F32, kind="ExternalInput")
    wout_in = nc.dram_tensor("wout", [D, D], F32, kind="ExternalInput")
    gw_in = nc.dram_tensor("gw", [E, D], F32, kind="ExternalInput")
    w1_in = nc.dram_tensor("w1", [E, HID, D], F32, kind="ExternalInput")
    w3_in = nc.dram_tensor("w3", [E, HID, D], F32, kind="ExternalInput")
    w2_in = nc.dram_tensor("w2", [E, D, HID], F32, kind="ExternalInput")
    identf_in = nc.dram_tensor("identf", [128, 128], F32, kind="ExternalInput")
    umask_in = nc.dram_tensor("umask", [128, 128], F32, kind="ExternalInput")
    prefm_in = nc.dram_tensor("prefm", [64, 8], F32, kind="ExternalInput")
    seqb_in = nc.dram_tensor("seqb", [64, 8], F32, kind="ExternalInput")
    y_out = nc.dram_tensor("y", [TPC, D], F32, kind="ExternalOutput")

    w1_bf = nc.dram_tensor("w1_bf", [E, HID, D], BF16, kind="Internal")
    w3_bf = nc.dram_tensor("w3_bf", [E, HID, D], BF16, kind="Internal")
    w2_bf = nc.dram_tensor("w2_bf", [E, D, HID], BF16, kind="Internal")
    wqkv_bf = nc.dram_tensor("wqkv_bf", [3 * D, D], BF16, kind="Internal")
    wout_bf = nc.dram_tensor("wout_bf", [D, D], BF16, kind="Internal")
    stream = nc.dram_tensor("stream", [TPC, D], F32, kind="Internal")
    kcm_d = nc.dram_tensor("kcm_d", [NCH, 64, NH * 128], F32, kind="Internal")
    qscm_d = nc.dram_tensor("qscm_d", [NCH, 64, NH * 128], F32, kind="Internal")
    vtm_d = nc.dram_tensor("vtm_d", [TPC, D], BF16, kind="Internal")
    schunks_d = nc.dram_tensor("schunks_d", [NCH, 64, NH * 128], F32, kind="Internal")
    CCW = 16 + NH * 128
    ccsrc_d = nc.dram_tensor("ccsrc_d", [64, CCW], F32, kind="Internal")
    ccdst_d = nc.dram_tensor("ccdst_d", [64 * NCORES, CCW], F32, kind="Internal")

    with ExitStack() as top:
        def sb(ctx, name, shape, dt=F32):
            return ctx.enter_context(nc.sbuf_tensor(name, shape, dt))

        def ps(ctx, name, shape, dt=F32):
            return ctx.enter_context(nc.psum_tensor(name, shape, dt))

        identf = sb(top, "identf_s", [128, 128])
        umaskf = sb(top, "umaskf_s", [128, 128])
        prefm = sb(top, "prefm_s", [64, 8])
        seqb = sb(top, "seqb_s", [64, 8])
        gwsb = sb(top, "gwsb", [8, D])
        gwT = sb(top, "gwT", [128, 8 * E])
        Stot = sb(top, "Stot", [64, NH * 128])
        kmaxloc = sb(top, "kmaxloc", [64, NH])
        escale = sb(top, "escale", [64, NH])
        negm = sb(top, "negm", [64, NH])

        semnames = ["d_cin", "d_wc", "d_w", "d_x", "d_st", "d_sp", "cc_s",
                    "s_tr", "s_tev", "s_amm", "s_silu", "s_hm", "s_bmm",
                    "s_bep", "s_w13c", "s_w2c", "s_gmm", "s_dv", "s_ac",
                    "s_pe2", "s_dv2", "s_ac2", "s_misc"]
        sems = {n: Sem(top.enter_context(nc.semaphore(n))) for n in semnames}
        S = NS()
        for n, s in sems.items():
            setattr(S, n, s)
        allsems = list(sems.values())

        if "p0" in phases:
            with ExitStack() as c0:
                pg0 = ps(c0, "pg0", [128, 16])
                pg1 = ps(c0, "pg1", [128, 16])

                def emit_p0(tr):
                    for ap_out, ap_in in [(identf[:, :], identf_in[:, :]),
                                          (umaskf[:, :], umask_in[:, :]),
                                          (prefm[:, :], prefm_in[:, :]),
                                          (seqb[:, :], seqb_in[:, :]),
                                          (gwsb[:, :], gw_in[:, :])]:
                        tr.dma("sync", ap_out, ap_in, S.d_cin)
                    cin_done = S.d_cin.count
                    for e in range(E):
                        tr.dma("gpsimd", w1_bf[e, :, :], w1_in[e, :, :], S.d_wc)
                        tr.dma("gpsimd", w3_bf[e, :, :], w3_in[e, :, :], S.d_wc)
                        tr.dma("gpsimd", w2_bf[e, :, :], w2_in[e, :, :], S.d_wc)
                    tr.dma("gpsimd", wqkv_bf[:, :], wqkv_in[:, :], S.d_wc)
                    tr.dma("gpsimd", wout_bf[:, :], wout_in[:, :], S.d_wc)
                    tr.wait("tensor", S.d_cin, cin_done)
                    for k in range(NKC):
                        pg = pg0 if k % 2 == 0 else pg1
                        if k >= 2:
                            tr.wait("tensor", S.s_tev, tr.hist[("gev", k - 2)])
                        c = tr.op("tensor", "transpose", pg[:, 0:8],
                                  gwsb[0:8, k * 128:(k + 1) * 128],
                                  identf[0:8, 0:8], inc=S.s_tr)
                        tr.hist[("gtr", k)] = c
                        tr.wait("vector", S.s_tr, tr.hist[("gtr", k)])
                        c = tr.op("vector", "tensor_copy",
                                  gwT[:, k * 8:(k + 1) * 8], pg[:, 0:8],
                                  inc=S.s_tev)
                        tr.hist[("gev", k)] = c
                    tr.op("vector", "memset", Stot[:, :], 0.0)
                    tr.op("vector", "memset", kmaxloc[:, :], -1e38)

                run_phase(nc, allsems, emit_p0)

        def make_ffn_tensors(ctx, pfx):
            f = NS()
            f.xres = sb(ctx, pfx + "xres", [128, NTC * D])
            f.xnbf = sb(ctx, pfx + "xnbf", [128, NKC * TT], BF16)
            f.cst = sb(ctx, pfx + "cst", [128, D])
            f.sqst = sb(ctx, pfx + "sqst", [128, D])
            f.xntm = sb(ctx, pfx + "xntm", [128, D])
            f.xnf = sb(ctx, pfx + "xnf", [128, D])
            f.hbuf = sb(ctx, pfx + "hbuf", [128, 8 * TT], BF16)
            f.outacc = sb(ctx, pfx + "outacc", [128, NTC * D])
            f.w1b = [sb(ctx, pfx + "w1b%d" % i, [128, NKC * 512], BF16)
                     for i in range(2)]
            f.w3b = [sb(ctx, pfx + "w3b%d" % i, [128, NKC * 512], BF16)
                     for i in range(2)]
            f.w2b = sb(ctx, pfx + "w2b", [128, 8 * D], BF16)
            f.silu = [sb(ctx, pfx + "silu%d" % i, [128, 512]) for i in range(2)]
            f.coef = sb(ctx, pfx + "coef", [128, NTC * E])
            f.r1 = sb(ctx, pfx + "r1", [128, 1])
            f.epsb = sb(ctx, pfx + "epsb", [128, 1])
            f.nmean = sb(ctx, pfx + "nmean", [128, 1])
            f.ssq = sb(ctx, pfx + "ssq", [128, 1])
            f.sd = sb(ctx, pfx + "sd", [128, 1])
            f.rstd = sb(ctx, pfx + "rstd", [128, 1])
            f.m1n = sb(ctx, pfx + "m1n", [128, 1])
            f.m2 = sb(ctx, pfx + "m2", [128, 1])
            f.d1 = sb(ctx, pfx + "d1", [128, 1])
            f.r2 = sb(ctx, pfx + "r2", [128, 1])
            f.gst = sb(ctx, pfx + "gst", [128, 8])
            f.ge1 = sb(ctx, pfx + "ge1", [128, 8])
            f.tmp8 = sb(ctx, pfx + "tmp8", [128, 8])
            f.gm = sb(ctx, pfx + "gm", [128, 8])
            f.pH1 = [ps(ctx, pfx + "pH1%d" % i, [128, 512]) for i in range(2)]
            f.pH3 = [ps(ctx, pfx + "pH3%d" % i, [128, 512]) for i in range(2)]
            f.pB = [ps(ctx, pfx + "pB%d" % i, [128, 512]) for i in range(2)]
            f.pT = [ps(ctx, pfx + "pT%d" % i, [128, 128]) for i in range(2)]
            return f

        def emit_ffn(tr, f, src, dst):
            h = tr.hist
            w13c0 = S.s_w13c.count
            w2c0 = S.s_w2c.count
            tr.op("vector", "memset", f.epsb[:, :], LNEPS, inc=S.s_dv)
            astep = 0
            bstep = 0
            blkidx = 0
            grpidx = 0
            trk = 0

            for tau in range(NTILE):
                for tc in range(NTC):
                    if tau > 0:
                        tr.wait("gpsimd", S.s_dv, h[("xfree", tau - 1)])
                        tr.wait("gpsimd", S.d_st, h[("st", tau - 1)])
                    r0 = tau * TT + tc * 128
                    c = tr.dma("gpsimd", f.xres[:, tc * D:(tc + 1) * D],
                               src[r0:r0 + 128, :], S.d_x)
                    h[("xld", tau, tc)] = c

                if tau > 0:
                    tr.wait("vector", S.s_amm, h[("aend", tau - 1)])
                for tc in range(NTC):
                    xs = f.xres[:, tc * D:(tc + 1) * D]
                    tr.wait("vector", S.d_x, h[("xld", tau, tc)])
                    tr.opw("vector", "reduce_sum", f.r1[:, :], xs, axis=AX.X,
                           inc=S.s_dv)
                    tr.opw("vector", "tensor_scalar", f.nmean[:, :],
                           f.r1[:, :], -1.0 / D, None, ALU.mult, inc=S.s_dv)
                    c = tr.op("vector", "tensor_scalar", f.cst[:, :], xs,
                              f.nmean[:, :], None, ALU.add, inc=S.s_dv)
                    tr.wait("scalar", S.s_dv, c)
                    tr.opw("scalar", "activation", f.sqst[:, :], f.cst[:, :],
                           AF.Square, accum_out=f.ssq[:, :], inc=S.s_ac)
                    c = tr.op("scalar", "activation", f.sd[:, :], f.ssq[:, :],
                              AF.Sqrt, scale=1.0 / D, bias=f.epsb[:, :],
                              inc=S.s_ac)
                    tr.wait("vector", S.s_ac, c)
                    tr.opw("vector", "reciprocal", f.rstd[:, :], f.sd[:, :],
                           inc=S.s_dv)
                    cxn = tr.op("vector", "tensor_scalar", f.xntm[:, :],
                                f.cst[:, :], f.rstd[:, :], None, ALU.mult,
                                inc=S.s_dv)
                    tr.wait("tensor", S.s_dv, cxn)
                    if tc > 0:
                        tr.wait("vector", S.s_gmm, h[("g", tau, tc - 1)])
                    for k in range(NKC):
                        slot = trk % 2
                        if trk >= 2:
                            tr.wait("tensor", S.s_tev, h[("tev", trk - 2)])
                        c = tr.op("tensor", "transpose", f.pT[slot][:, :],
                                  f.xntm[:, k * 128:(k + 1) * 128],
                                  identf[:, :], inc=S.s_tr)
                        h[("tr", trk)] = c
                        tr.wait("vector", S.s_tr, h[("tr", trk)])
                        tr.op("vector", "tensor_copy",
                              f.xnf[:, k * 128:(k + 1) * 128], f.pT[slot][:, :])
                        c = tr.op("vector", "tensor_copy",
                                  f.xnbf[:, k * TT + tc * 128:
                                         k * TT + (tc + 1) * 128],
                                  f.pT[slot][:, :], inc=S.s_tev)
                        h[("tev", trk)] = c
                        trk += 1
                    tr.wait("tensor", S.s_tev, h[("tev", trk - 1)])
                    if tau > 0 and tc == 0:
                        tr.wait("tensor", S.s_bep, h[("bdone", tau - 1)])
                    if tc > 0:
                        tr.wait("tensor", S.s_ac, h[("gex", tau, tc - 1)])
                    for k in range(NKC):
                        c = tr.op("tensor", "matmul", f.pB[0][:, 0:8],
                                  f.xnf[:, k * 128:(k + 1) * 128],
                                  gwT[:, k * 8:(k + 1) * 8],
                                  start=(k == 0), stop=(k == NKC - 1),
                                  skip_group_check=True,
                                  inc=S.s_gmm if k == NKC - 1 else None)
                    h[("g", tau, tc)] = S.s_gmm.count
                    tr.wait("vector", S.s_gmm, h[("g", tau, tc)])
                    c = tr.op("vector", "reduce_max", f.m1n[:, :],
                              f.pB[0][:, 0:8], axis=AX.X, negate=True,
                              inc=S.s_dv)
                    tr.wait("scalar", S.s_dv, c)
                    c = tr.op("scalar", "activation", f.gst[:, :],
                              f.pB[0][:, 0:8], AF.Exp, bias=f.m1n[:, :],
                              inc=S.s_ac)
                    h[("gex", tau, tc)] = c
                    tr.wait("vector", S.s_ac, c)
                    tr.opw("vector", "tensor_single_scalar", f.ge1[:, :],
                           f.gst[:, :], 1.0, ALU.is_ge, inc=S.s_dv)
                    tr.opw("vector", "scalar_tensor_tensor", f.tmp8[:, :],
                           f.ge1[:, :], -1e30, f.gst[:, :], ALU.mult, ALU.add,
                           inc=S.s_dv)
                    tr.opw("vector", "reduce_max", f.m2[:, :], f.tmp8[:, :],
                           axis=AX.X, inc=S.s_dv)
                    tr.opw("vector", "tensor_single_scalar", f.tmp8[:, :],
                           f.gst[:, :], f.m2[:, :], ALU.is_ge, inc=S.s_dv)
                    tr.opw("vector", "tensor_tensor", f.gm[:, :], f.gst[:, :],
                           f.tmp8[:, :], ALU.mult, inc=S.s_dv)
                    tr.opw("vector", "tensor_scalar", f.d1[:, :], f.m2[:, :],
                           1.0, None, ALU.add, inc=S.s_dv)
                    tr.opw("vector", "reciprocal", f.r2[:, :], f.d1[:, :],
                           inc=S.s_dv)
                    c = tr.op("vector", "tensor_scalar",
                              f.coef[:, tc * E:(tc + 1) * E], f.gm[:, :],
                              f.r2[:, :], 0.5, ALU.mult, ALU.mult, inc=S.s_dv)
                    h[("gd", tau, tc)] = c
                h[("gdone", tau)] = S.s_dv.count

                for e in range(E):
                    for gi, (g0, g1) in enumerate(HGROUPS):
                        for bl in (2 * gi, 2 * gi + 1):
                            hb0, hb1 = HBLOCKS[bl]
                            nh = hb1 - hb0
                            if blkidx >= 2:
                                tr.wait("sync", S.s_w13c, w13c0 + blkidx - 1)
                            for k in range(NKC):
                                tr.dma("sync",
                                       f.w1b[blkidx % 2][:, k * 512:
                                                         k * 512 + nh * 128],
                                       w1_bf[e, hb0 * 128:hb1 * 128,
                                             k * 128:(k + 1) * 128],
                                       S.d_w, transpose=True)
                                c = tr.dma("sync",
                                           f.w3b[blkidx % 2][:, k * 512:
                                                             k * 512 + nh * 128],
                                           w3_bf[e, hb0 * 128:hb1 * 128,
                                                 k * 128:(k + 1) * 128],
                                           S.d_w, transpose=True)
                            h[("wblk", blkidx)] = c
                            blkidx += 1
                        if grpidx >= 1:
                            tr.wait("sync", S.s_w2c, w2c0 + grpidx)
                        for hl, hh in enumerate(range(g0, g1)):
                            c = tr.dma("sync", f.w2b[:, hl * D:(hl + 1) * D],
                                       w2_bf[e, 0:D, hh * 128:(hh + 1) * 128],
                                       S.d_w, transpose=True)
                        h[("w2g", grpidx)] = c
                        grpidx += 1

                        for hh in range(g0, g1):
                            bl = hh // 4
                            gblk = (tau * E + e) * 6 + bl
                            par = blkidx_par = gblk % 2
                            hloc = hh - HBLOCKS[bl][0]
                            for n in range(2):
                                p = astep % 2
                                tr.wait("tensor", S.d_w, h[("wblk", gblk)])
                                if astep >= 2:
                                    tr.wait("tensor", S.s_silu,
                                            h[("silu", astep - 2)])
                                    tr.wait("tensor", S.s_hm,
                                            h[("hm", astep - 2)])
                                for k in range(NKC):
                                    tr.op("tensor", "matmul", f.pH1[p][:, :],
                                          f.w1b[par][:, k * 512 + hloc * 128:
                                                     k * 512 + (hloc + 1) * 128],
                                          f.xnbf[:, k * TT + n * 512:
                                                 k * TT + (n + 1) * 512],
                                          start=(k == 0), stop=(k == NKC - 1))
                                for k in range(NKC):
                                    c = tr.op("tensor", "matmul",
                                              f.pH3[p][:, :],
                                              f.w3b[par][:, k * 512 + hloc * 128:
                                                         k * 512 + (hloc + 1) * 128],
                                              f.xnbf[:, k * TT + n * 512:
                                                     k * TT + (n + 1) * 512],
                                              start=(k == 0),
                                              stop=(k == NKC - 1),
                                              inc=S.s_amm
                                              if k == NKC - 1 else None)
                                h[("amm", astep)] = S.s_amm.count
                                tr.wait("scalar", S.s_amm, h[("amm", astep)])
                                if astep >= 2:
                                    tr.wait("scalar", S.s_hm,
                                            h[("hm", astep - 2)])
                                c = tr.op("scalar", "activation",
                                          f.silu[p][:, :], f.pH1[p][:, :],
                                          AF.Silu, inc=S.s_silu)
                                h[("silu", astep)] = c
                                tr.wait("vector", S.s_silu, h[("silu", astep)])
                                c = tr.op("vector", "tensor_tensor",
                                          f.hbuf[:, (hh - g0) * TT + n * 512:
                                                 (hh - g0) * TT + (n + 1) * 512],
                                          f.silu[p][:, :], f.pH3[p][:, :],
                                          ALU.mult, inc=S.s_hm)
                                h[("hm", astep)] = c
                                astep += 1
                            if hh == HBLOCKS[bl][1] - 1:
                                tr.op("tensor", "nop", inc=S.s_w13c)

                        nhg = g1 - g0
                        first_b = True
                        for tc in range(NTC):
                            for half in range(2):
                                p = bstep % 2
                                if first_b:
                                    tr.wait("tensor", S.s_hm,
                                            h[("hm", astep - 1)])
                                    tr.wait("tensor", S.d_w,
                                            h[("w2g", grpidx - 1)])
                                    tr.wait("tensor", S.s_dv,
                                            h[("gdone", tau)])
                                    first_b = False
                                if bstep >= 2:
                                    tr.wait("tensor", S.s_bep,
                                            h[("bep", bstep - 2)])
                                for hl in range(nhg):
                                    c = tr.op("tensor", "matmul", f.pB[p][:, :],
                                              f.hbuf[:, hl * TT + tc * 128:
                                                     hl * TT + (tc + 1) * 128],
                                              f.w2b[:, hl * D + half * 512:
                                                    hl * D + (half + 1) * 512],
                                              start=(hl == 0),
                                              stop=(hl == nhg - 1),
                                              skip_group_check=True,
                                              inc=S.s_bmm
                                              if hl == nhg - 1 else None)
                                h[("bmm", bstep)] = S.s_bmm.count
                                tr.wait("vector", S.s_bmm, h[("bmm", bstep)])
                                if tau > 0 and e == 0 and gi == 0 \
                                        and tc == 0 and half == 0:
                                    tr.wait("vector", S.d_st,
                                            h[("st", tau - 1)])
                                oslc = f.outacc[:, tc * D + half * 512:
                                                tc * D + (half + 1) * 512]
                                cf = f.coef[:, tc * E + e:tc * E + e + 1]
                                if e == 0 and gi == 0:
                                    c = tr.op("vector", "tensor_scalar", oslc,
                                              f.pB[p][:, :], cf, None,
                                              ALU.mult, inc=S.s_bep)
                                else:
                                    c = tr.op("vector", "scalar_tensor_tensor",
                                              oslc, f.pB[p][:, :], cf, oslc,
                                              ALU.mult, ALU.add, inc=S.s_bep)
                                h[("bep", bstep)] = c
                                bstep += 1
                        tr.op("tensor", "nop", inc=S.s_w2c)
                h[("aend", tau)] = S.s_amm.count
                h[("bdone", tau)] = S.s_bep.count

                for tc in range(NTC):
                    tr.wait("vector", S.s_bep, h[("bdone", tau)])
                    c = tr.op("vector", "tensor_tensor",
                              f.outacc[:, tc * D:(tc + 1) * D],
                              f.xres[:, tc * D:(tc + 1) * D],
                              f.outacc[:, tc * D:(tc + 1) * D],
                              ALU.add, inc=S.s_dv)
                    h[("add", tau, tc)] = c
                    tr.wait("gpsimd", S.s_dv, h[("add", tau, tc)])
                    r0 = tau * TT + tc * 128
                    c = tr.dma("gpsimd", dst[r0:r0 + 128, :],
                               f.outacc[:, tc * D:(tc + 1) * D], S.d_st)
                h[("xfree", tau)] = S.s_dv.count
                h[("st", tau)] = S.d_st.count

            tr.wait("gpsimd", S.d_st, S.d_st.count)

        if "p1" in phases:
            with ExitStack() as c1:
                f1 = make_ffn_tensors(c1, "f1_")
                run_phase(nc, allsems,
                          lambda tr: emit_ffn(tr, f1, x_in, stream))

        if "p2" in phases:
            with ExitStack() as c2:
                a = NS()
                a.wqkvT = sb(c2, "a_wqkvT", [128, NKC * 3 * D], BF16)
                a.xres1 = sb(c2, "a_xres1", [128, D])
                a.cst = sb(c2, "a_cst", [128, D])
                a.sqst = sb(c2, "a_sqst", [128, D])
                a.xntm = sb(c2, "a_xntm", [128, D])
                a.xnbf1 = sb(c2, "a_xnbf1", [128, D], BF16)
                a.qstage = sb(c2, "a_qstage", [128, D])
                a.qsm = sb(c2, "a_qsm", [128, D])
                a.kstage = sb(c2, "a_kstage", [128, D])
                a.kexp = sb(c2, "a_kexp", [128, D])
                a.kexpb = sb(c2, "a_kexpb", [128, D], BF16)
                a.vbf = sb(c2, "a_vbf", [128, D], BF16)
                a.kcm = sb(c2, "a_kcm", [64, NH * 128])
                a.qscm = sb(c2, "a_qscm", [64, NH * 128])
                a.schst = [sb(c2, "a_schst%d" % i, [64, NH * 128])
                           for i in range(2)]
                a.kmaxc = sb(c2, "a_kmaxc", [64, NH])
                a.onesb = sb(c2, "a_onesb", [128, 1], BF16)
                a.r1 = sb(c2, "a_r1", [128, 1])
                a.nmean = sb(c2, "a_nmean", [128, 1])
                a.ssq = sb(c2, "a_ssq", [128, 1])
                a.sd = sb(c2, "a_sd", [128, 1])
                a.rstd = sb(c2, "a_rstd", [128, 1])
                a.qm = sb(c2, "a_qm", [128, 1])
                a.epsb = sb(c2, "a_epsb", [128, 1])
                a.qsum = sb(c2, "a_qsum", [128, 1])
                a.qr = sb(c2, "a_qr", [128, 1])
                a.pQ = [ps(c2, "a_pQ%d" % i, [128, 512]) for i in range(2)]
                a.pT = [ps(c2, "a_pT%d" % i, [128, 128]) for i in range(2)]
                a.pS = ps(c2, "a_pS", [64, NH * 128])

                def emit_p2a(tr):
                    h = tr.hist
                    for k in range(NKC):
                        for s2 in range(2):
                            c = tr.dma("sync",
                                       a.wqkvT[:, k * 3 * D + s2 * 1536:
                                               k * 3 * D + (s2 + 1) * 1536],
                                       wqkv_bf[s2 * 1536:(s2 + 1) * 1536,
                                               k * 128:(k + 1) * 128],
                                       S.d_w, transpose=True)
                    h["wq"] = c
                    tr.op("vector", "memset", a.onesb[:, :], 1.0)
                    tr.op("vector", "memset", a.epsb[:, :], LNEPS)
                    c0 = tr.op("vector", "memset", a.pS[:, :], 0.0,
                               inc=S.s_dv2)
                    trk = 0
                    for n in range(NCH):
                        if n > 0:
                            tr.wait("gpsimd", S.s_dv2, h[("xdone", n - 1)])
                        c = tr.dma("gpsimd", a.xres1[:, :],
                                   stream[n * 128:(n + 1) * 128, :], S.d_x)
                        h[("xld", n)] = c
                        tr.wait("vector", S.d_x, h[("xld", n)])
                        if n > 0:
                            tr.wait("vector", S.s_pe2, h[("smm", n - 1)])
                            tr.wait("vector", S.d_sp, h[("spst", n - 1)])
                            tr.wait("vector", S.s_tr, h[("trlast", n - 1)])
                        tr.opw("vector", "reduce_sum", a.r1[:, :],
                               a.xres1[:, :], axis=AX.X, inc=S.s_dv2)
                        tr.opw("vector", "tensor_scalar", a.nmean[:, :],
                               a.r1[:, :], -1.0 / D, None, ALU.mult,
                               inc=S.s_dv2)
                        c = tr.op("vector", "tensor_scalar", a.cst[:, :],
                                  a.xres1[:, :], a.nmean[:, :], None, ALU.add,
                                  inc=S.s_dv2)
                        h[("xdone", n)] = c
                        tr.wait("scalar", S.s_dv2, c)
                        tr.opw("scalar", "activation", a.sqst[:, :],
                               a.cst[:, :], AF.Square, accum_out=a.ssq[:, :],
                               inc=S.s_ac2)
                        c = tr.op("scalar", "activation", a.sd[:, :],
                                  a.ssq[:, :], AF.Sqrt, scale=1.0 / D,
                                  bias=a.epsb[:, :], inc=S.s_ac2)
                        tr.wait("vector", S.s_ac2, c)
                        tr.opw("vector", "reciprocal", a.rstd[:, :],
                               a.sd[:, :], inc=S.s_dv2)
                        cxn = tr.op("vector", "tensor_scalar", a.xntm[:, :],
                                    a.cst[:, :], a.rstd[:, :], None, ALU.mult,
                                    inc=S.s_dv2)
                        tr.wait("tensor", S.s_dv2, cxn)
                        for k in range(NKC):
                            slot = trk % 2
                            if trk >= 2:
                                tr.wait("tensor", S.s_tev, h[("tev", trk - 2)])
                            c = tr.op("tensor", "transpose", a.pT[slot][:, :],
                                      a.xntm[:, k * 128:(k + 1) * 128],
                                      identf[:, :], inc=S.s_tr)
                            h[("tr", trk)] = c
                            tr.wait("vector", S.s_tr, h[("tr", trk)])
                            c = tr.op("vector", "tensor_copy",
                                      a.xnbf1[:, k * 128:(k + 1) * 128],
                                      a.pT[slot][:, :], inc=S.s_tev)
                            h[("tev", trk)] = c
                            trk += 1
                        tr.wait("tensor", S.s_tev, h[("tev", trk - 1)])
                        if n == 0:
                            tr.wait("tensor", S.d_w, h["wq"])
                        for qs in range(6):
                            p = qs % 2
                            if n > 0 or qs >= 2:
                                tr.wait("tensor", S.s_dv2,
                                        h[("qev", n if qs >= 2 else n - 1,
                                           qs - 2 if qs >= 2 else qs + 4)])
                            for k in range(NKC):
                                c = tr.op("tensor", "matmul", a.pQ[p][:, :],
                                          a.xnbf1[:, k * 128:(k + 1) * 128],
                                          a.wqkvT[:, k * 3 * D + qs * 512:
                                                  k * 3 * D + (qs + 1) * 512],
                                          start=(k == 0), stop=(k == NKC - 1),
                                          inc=S.s_pe2
                                          if k == NKC - 1 else None)
                            h[("qkv", n, qs)] = S.s_pe2.count
                            tr.wait("vector", S.s_pe2, h[("qkv", n, qs)])
                            if qs < 2:
                                dstap = a.qstage[:, qs * 512:(qs + 1) * 512]
                            elif qs < 4:
                                dstap = a.kstage[:, (qs - 2) * 512:
                                                 (qs - 1) * 512]
                            else:
                                dstap = a.vbf[:, (qs - 4) * 512:(qs - 3) * 512]
                            c = tr.op("vector", "tensor_copy", dstap,
                                      a.pQ[p][:, :], inc=S.s_dv2)
                            h[("qev", n, qs)] = c
                        for hd in range(NH):
                            sl = a.qstage[:, hd * HS:(hd + 1) * HS]
                            c = tr.op("vector", "reduce_max", a.qm[:, :], sl,
                                      axis=AX.X, negate=True, inc=S.s_dv2)
                            tr.wait("scalar", S.s_dv2, c)
                            c = tr.op("scalar", "activation",
                                      a.qsm[:, hd * HS:(hd + 1) * HS], sl,
                                      AF.Exp, bias=a.qm[:, :],
                                      accum_out=a.qsum[:, :], inc=S.s_ac2)
                            tr.wait("vector", S.s_ac2, c)
                            tr.opw("vector", "reciprocal", a.qr[:, :],
                                   a.qsum[:, :], inc=S.s_dv2)
                            tr.opw("vector", "tensor_scalar",
                                   a.qsm[:, hd * HS:(hd + 1) * HS],
                                   a.qsm[:, hd * HS:(hd + 1) * HS],
                                   a.qr[:, :], 0.125, ALU.mult, ALU.mult,
                                   inc=S.s_dv2)
                        c = tr.op("vector", "nop", inc=S.s_dv2)
                        tr.wait("scalar", S.s_dv2, c)
                        c = tr.op("scalar", "activation", a.kexp[:, :],
                                  a.kstage[:, :], AF.Exp, inc=S.s_ac2)
                        tr.wait("vector", S.s_ac2, c)
                        cvb = tr.op("vector", "tensor_copy", a.kexpb[:, :],
                                    a.kexp[:, :], inc=S.s_dv2)
                        tr.wait("tensor", S.s_dv2, cvb)
                        if n > 0:
                            tr.wait("tensor", S.s_dv2, h[("sdone", n - 1)])
                        else:
                            tr.wait("tensor", S.s_dv2, c0)
                        for hd in range(NH):
                            tr.op("tensor", "matmul",
                                  a.pS[0:64, hd * 128:hd * 128 + 64],
                                  a.kexpb[:, hd * HS:(hd + 1) * HS],
                                  a.vbf[:, hd * HS:(hd + 1) * HS],
                                  start=(hd % 4 == 0), stop=False,
                                  skip_group_check=True)
                            c = tr.op("tensor", "matmul",
                                      a.pS[0:64, hd * 128 + 64:hd * 128 + 65],
                                      a.kexpb[:, hd * HS:(hd + 1) * HS],
                                      a.onesb[:, :], start=False,
                                      stop=(hd % 4 == 3),
                                      skip_group_check=True,
                                      inc=S.s_pe2 if hd == NH - 1 else None)
                        h[("smm", n)] = S.s_pe2.count
                        for hd in range(NH):
                            for which, srcap, dstt in (
                                    (0, a.kstage[:, hd * HS:(hd + 1) * HS],
                                     a.kcm),
                                    (1, a.qsm[:, hd * HS:(hd + 1) * HS],
                                     a.qscm)):
                                slot = trk % 2
                                if trk >= 2:
                                    tr.wait("tensor", S.s_tev,
                                            h[("tev", trk - 2)])
                                c = tr.op("tensor", "transpose",
                                          a.pT[slot][0:64, :], srcap,
                                          identf[:, :], inc=S.s_tr)
                                h[("tr", trk)] = c
                                tr.wait("vector", S.s_tr, h[("tr", trk)])
                                c = tr.op("vector", "tensor_copy",
                                          dstt[:, hd * 128:(hd + 1) * 128],
                                          a.pT[slot][0:64, :], inc=S.s_tev)
                                h[("tev", trk)] = c
                                trk += 1
                        cktr = S.s_tev.count
                        h[("trlast", n)] = S.s_tr.count
                        for hd in range(NH):
                            cr = tr.op("vector", "reduce_max",
                                       a.kmaxc[:, hd:hd + 1],
                                       a.kcm[:, hd * 128:(hd + 1) * 128],
                                       axis=AX.X,
                                       inc=S.s_dv2 if hd == NH - 1 else None)
                        tr.wait("vector", S.s_dv2, cr)
                        tr.opw("vector", "tensor_tensor", kmaxloc[:, :],
                               kmaxloc[:, :], a.kmaxc[:, :], ALU.max,
                               inc=S.s_dv2)
                        p2 = n % 2
                        tr.wait("vector", S.s_pe2, h[("smm", n)])
                        if n >= 2:
                            tr.wait("vector", S.d_sp, h[("spst", n - 2)])
                        tr.opw("vector", "tensor_copy", a.schst[p2][:, :],
                               a.pS[:, :], inc=S.s_dv2)
                        c = tr.op("vector", "tensor_tensor", Stot[:, :],
                                  Stot[:, :], a.schst[p2][:, :], ALU.add,
                                  inc=S.s_dv2)
                        h[("sdone", n)] = c
                        tr.wait("gpsimd", S.s_tev, cktr)
                        tr.dma("gpsimd", kcm_d[n, :, :], a.kcm[:, :], S.d_sp)
                        tr.dma("gpsimd", qscm_d[n, :, :], a.qscm[:, :], S.d_sp)
                        tr.wait("gpsimd", S.s_dv2, h[("qev", n, 5)])
                        tr.dma("gpsimd", vtm_d[n * 128:(n + 1) * 128, :],
                               a.vbf[:, :], S.d_sp)
                        tr.wait("gpsimd", S.s_dv2, h[("sdone", n)])
                        c = tr.dma("gpsimd", schunks_d[n, :, :],
                                   a.schst[p2][:, :], S.d_sp)
                        h[("spst", n)] = c
                    tr.wait("gpsimd", S.d_sp, S.d_sp.count)
                    cl = tr.op("vector", "nop", inc=S.s_dv2)
                    tr.wait("gpsimd", S.s_dv2, cl)
                    tr.dma("gpsimd", ccsrc_d[:, 0:16], kmaxloc[:, :], S.d_sp)
                    c = tr.dma("gpsimd", ccsrc_d[:, 16:CCW], Stot[:, :],
                               S.d_sp)
                    tr.wait("gpsimd", S.d_sp, c)
                    tr.op("gpsimd", "collective_compute", "AllGather",
                          ALU.bypass,
                          replica_groups=[list(range(NCORES))],
                          ins=[ccsrc_d.ap().opt()], outs=[ccdst_d.ap().opt()],
                          inc=S.cc_s)
                    tr.wait("gpsimd", S.cc_s, 1)

                run_phase(nc, allsems, emit_p2a)

            with ExitStack() as c3:
                b2 = NS()
                b2.woutT = sb(c3, "b_woutT", [128, NKC * D], BF16)
                b2.gath = sb(c3, "b_gath", [64, CCW])
                b2.Scar = sb(c3, "b_Scar", [64, NH * 128])
                b2.Scarb = sb(c3, "b_Scarb", [64, NH * HS], BF16)
                b2.kcm = sb(c3, "b_kcm", [64, NH * 128])
                b2.kexp = sb(c3, "b_kexp", [64, NH * 128])
                b2.kexpb = sb(c3, "b_kexpb", [64, NH * 128], BF16)
                b2.qscm = sb(c3, "b_qscm", [64, NH * 128])
                b2.qtb = sb(c3, "b_qtb", [64, NH * 128], BF16)
                b2.zfull = sb(c3, "b_zfull", [64, NH * 128])
                b2.ztmp = sb(c3, "b_ztmp", [64, NH * 128])
                b2.schk = sb(c3, "b_schk", [64, NH * 128])
                b2.vchunk = sb(c3, "b_vchunk", [128, D], BF16)
                b2.abf = [sb(c3, "b_abf%d" % i, [128, 128], BF16)
                          for i in range(2)]
                b2.ofm = sb(c3, "b_ofm", [128, D], BF16)
                b2.xres1 = sb(c3, "b_xres1", [128, D])
                b2.ones64 = sb(c3, "b_ones64", [64, 128])
                b2.pA = [ps(c3, "b_pA%d" % i, [128, 128]) for i in range(2)]
                b2.pO = ps(c3, "b_pO", [64, NH * 128])
                b2.pXO = [ps(c3, "b_pXO%d" % i, [128, 512]) for i in range(2)]

                def emit_p2b(tr):
                    h = tr.hist
                    for k in range(NKC):
                        c = tr.dma("sync", b2.woutT[:, k * D:(k + 1) * D],
                                   wout_bf[0:D, k * 128:(k + 1) * 128],
                                   S.d_w, transpose=True)
                    h["wo"] = c
                    tr.op("vector", "memset", b2.ones64[:, :], 1.0)
                    tr.op("vector", "memset", b2.Scar[:, :], 0.0)
                    tr.op("vector", "memset", kmaxloc[:, :], -1e38)
                    cm = tr.op("vector", "nop", inc=S.s_dv2)
                    for r in range(NCORES):
                        tr.wait("gpsimd", S.s_dv2,
                                cm if r == 0 else h[("gproc", r - 1)])
                        c = tr.dma("gpsimd", b2.gath[:, :],
                                   ccdst_d[r * 64:(r + 1) * 64, :], S.d_x)
                        h[("gld", r)] = c
                        tr.wait("vector", S.d_x, h[("gld", r)])
                        tr.opw("vector", "tensor_scalar", b2.kcm[:, 0:16],
                               b2.gath[:, 0:16], seqb[:, r:r + 1], None,
                               ALU.add, inc=S.s_dv2)
                        tr.opw("vector", "tensor_tensor", kmaxloc[:, :],
                               kmaxloc[:, :], b2.kcm[:, 0:16], ALU.max,
                               inc=S.s_dv2)
                        c = tr.opw("vector", "scalar_tensor_tensor",
                                   b2.Scar[:, :], b2.gath[:, 16:CCW],
                                   prefm[:, r:r + 1], b2.Scar[:, :],
                                   ALU.mult, ALU.add, inc=S.s_dv2)
                        h[("gproc", r)] = c
                    c = tr.op("vector", "nop", inc=S.s_dv2)
                    tr.wait("scalar", S.s_dv2, c)
                    c = tr.op("scalar", "activation", escale[:, :],
                              kmaxloc[:, :], AF.Exp, scale=-1.0, inc=S.s_ac2)
                    tr.wait("vector", S.s_ac2, c)
                    tr.opw("vector", "tensor_scalar", negm[:, :],
                           kmaxloc[:, :], -1.0, None, ALU.mult, inc=S.s_dv2)
                    for hd in range(NH):
                        crs = tr.op("vector", "tensor_scalar",
                                    b2.Scar[:, hd * 128:(hd + 1) * 128],
                                    b2.Scar[:, hd * 128:(hd + 1) * 128],
                                    escale[:, hd:hd + 1], None, ALU.mult,
                                    inc=S.s_dv2 if hd == NH - 1 else None)
                    tr.wait("vector", S.s_dv2, crs)
                    for hd in range(NH):
                        c = tr.op("vector", "tensor_copy",
                                  b2.Scarb[:, hd * HS:(hd + 1) * HS],
                                  b2.Scar[:, hd * 128:hd * 128 + 64],
                                  inc=S.s_dv2 if hd == NH - 1 else None)
                    h[("scb", -1)] = c

                    for n in range(NCH):
                        if n == 0:
                            tr.wait("gpsimd", S.s_dv2,
                                    h[("gproc", NCORES - 1)])
                        if n > 0:
                            tr.wait("gpsimd", S.s_ac2, h[("kexp", n - 1)])
                        c = tr.dma("gpsimd", b2.kcm[:, :], kcm_d[n, :, :],
                                   S.d_x)
                        if n > 0:
                            tr.wait("gpsimd", S.s_dv2, h[("qtb", n - 1)])
                        c = tr.dma("gpsimd", b2.qscm[:, :], qscm_d[n, :, :],
                                   S.d_x)
                        if n > 0:
                            tr.wait("gpsimd", S.s_dv2, h[("sup", n - 1)])
                        c = tr.dma("gpsimd", b2.schk[:, :],
                                   schunks_d[n, :, :], S.d_x)
                        if n > 0:
                            tr.wait("gpsimd", S.s_pe2, h[("omm", n - 1)])
                        c = tr.dma("gpsimd", b2.vchunk[:, :],
                                   vtm_d[n * 128:(n + 1) * 128, :], S.d_x)
                        if n > 0:
                            tr.wait("gpsimd", S.d_st, h[("stc", n - 1)])
                        c = tr.dma("gpsimd", b2.xres1[:, :],
                                   stream[n * 128:(n + 1) * 128, :], S.d_x)
                        h[("ld", n)] = c
                        tr.wait("scalar", S.d_x, h[("ld", n)])
                        if n == 0:
                            tr.wait("scalar", S.s_dv2, h[("scb", -1)])
                        for hd in range(NH):
                            c = tr.op("scalar", "activation",
                                      b2.kexp[:, hd * 128:(hd + 1) * 128],
                                      b2.kcm[:, hd * 128:(hd + 1) * 128],
                                      AF.Exp, bias=negm[:, hd:hd + 1],
                                      inc=S.s_ac2 if hd == NH - 1 else None)
                        h[("kexp", n)] = c
                        tr.wait("vector", S.s_ac2, h[("kexp", n)])
                        ckb = tr.op("vector", "tensor_copy", b2.kexpb[:, :],
                                    b2.kexp[:, :], inc=S.s_dv2)
                        for hd in range(NH):
                            csc = tr.op("vector", "tensor_tensor_scan",
                                        b2.zfull[:, hd * 128:(hd + 1) * 128],
                                        b2.ones64[:, :],
                                        b2.kexp[:, hd * 128:(hd + 1) * 128],
                                        b2.Scar[:, hd * 128 + 64:
                                                hd * 128 + 65],
                                        ALU.mult, ALU.add,
                                        inc=S.s_dv2 if hd == NH - 1 else None)
                        tr.wait("vector", S.s_dv2, csc)
                        tr.opw("vector", "tensor_scalar", b2.ztmp[:, :],
                               b2.zfull[:, :], 1e-9, None, ALU.add,
                               inc=S.s_dv2)
                        tr.opw("vector", "reciprocal", b2.ztmp[:, :],
                               b2.ztmp[:, :], inc=S.s_dv2)
                        tr.wait("vector", S.d_x, h[("ld", n)])
                        cqt = tr.op("vector", "tensor_tensor", b2.qtb[:, :],
                                    b2.qscm[:, :], b2.ztmp[:, :], ALU.mult,
                                    inc=S.s_dv2)
                        h[("qtb", n)] = cqt
                        for hd in range(NH):
                            slot = hd % 2
                            tr.wait("tensor", S.s_dv2, cqt)
                            if hd >= 2:
                                tr.wait("tensor", S.s_dv2,
                                        h[("am", n, hd - 2)])
                            c = tr.op("tensor", "matmul", b2.pA[slot][:, :],
                                      b2.kexpb[:, hd * 128:(hd + 1) * 128],
                                      b2.qtb[:, hd * 128:(hd + 1) * 128],
                                      start=True, stop=True,
                                      skip_group_check=True, inc=S.s_pe2)
                            h[("amm", n, hd)] = c
                            tr.wait("vector", S.s_pe2, h[("amm", n, hd)])
                            c = tr.op("vector", "tensor_tensor",
                                      b2.abf[slot][:, :], b2.pA[slot][:, :],
                                      umaskf[:, :], ALU.mult, inc=S.s_dv2)
                            h[("am", n, hd)] = c
                            tr.wait("tensor", S.s_dv2, h[("am", n, hd)])
                            if n > 0 and hd == 0:
                                tr.wait("tensor", S.s_dv2,
                                        h[("ofm", n - 1, NH - 1)])
                            tr.op("tensor", "matmul",
                                  b2.pO[0:64, hd * 128:(hd + 1) * 128],
                                  b2.Scarb[:, hd * HS:(hd + 1) * HS],
                                  b2.qtb[:, hd * 128:(hd + 1) * 128],
                                  start=(hd % 4 == 0), stop=False,
                                  skip_group_check=True)
                            c = tr.op("tensor", "matmul",
                                      b2.pO[0:64, hd * 128:(hd + 1) * 128],
                                      b2.vchunk[:, hd * HS:(hd + 1) * HS],
                                      b2.abf[slot][:, :],
                                      start=False, stop=(hd % 4 == 3),
                                      skip_group_check=True, inc=S.s_pe2)
                            h[("opair", n, hd)] = c
                            tr.wait("vector", S.s_pe2, h[("opair", n, hd)])
                            sub = hd % 2
                            kd = hd // 2
                            c = tr.op("vector", "tensor_copy",
                                      b2.ofm[sub * 64:(sub + 1) * 64,
                                             kd * 128:(kd + 1) * 128],
                                      b2.pO[0:64, hd * 128:(hd + 1) * 128],
                                      inc=S.s_dv2)
                            h[("ofm", n, hd)] = c
                            tr.op("vector", "scalar_tensor_tensor",
                                  b2.Scar[:, hd * 128:(hd + 1) * 128],
                                  b2.schk[:, hd * 128:(hd + 1) * 128],
                                  escale[:, hd:hd + 1],
                                  b2.Scar[:, hd * 128:(hd + 1) * 128],
                                  ALU.mult, ALU.add)
                        h[("omm", n)] = S.s_pe2.count
                        cupd = tr.op("vector", "nop", inc=S.s_dv2)
                        tr.wait("vector", S.s_dv2, cupd)
                        for hd in range(NH):
                            c = tr.op("vector", "tensor_copy",
                                      b2.Scarb[:, hd * HS:(hd + 1) * HS],
                                      b2.Scar[:, hd * 128:hd * 128 + 64],
                                      inc=S.s_dv2 if hd == NH - 1 else None)
                        h[("sup", n)] = c
                        if n == 0:
                            tr.wait("tensor", S.d_w, h["wo"])
                        for half in range(2):
                            tr.wait("tensor", S.s_dv2, h[("ofm", n, NH - 1)])
                            if n > 0:
                                tr.wait("tensor", S.s_dv2,
                                        h[("xadd", n - 1, half)])
                            for kd in range(NKC):
                                c = tr.op("tensor", "matmul",
                                          b2.pXO[half][:, :],
                                          b2.ofm[:, kd * 128:(kd + 1) * 128],
                                          b2.woutT[:, kd * D + half * 512:
                                                   kd * D + (half + 1) * 512],
                                          start=(kd == 0),
                                          stop=(kd == NKC - 1),
                                          inc=S.s_pe2
                                          if kd == NKC - 1 else None)
                            h[("xomm", n, half)] = S.s_pe2.count
                            tr.wait("vector", S.s_pe2, h[("xomm", n, half)])
                            c = tr.op("vector", "tensor_tensor",
                                      b2.xres1[:, half * 512:
                                               (half + 1) * 512],
                                      b2.xres1[:, half * 512:
                                               (half + 1) * 512],
                                      b2.pXO[half][:, :], ALU.add,
                                      inc=S.s_dv2)
                            h[("xadd", n, half)] = c
                        tr.wait("gpsimd", S.s_dv2, h[("xadd", n, 1)])
                        c = tr.dma("gpsimd", stream[n * 128:(n + 1) * 128, :],
                                   b2.xres1[:, :], S.d_st)
                        h[("stc", n)] = c
                    tr.wait("gpsimd", S.d_st, S.d_st.count)

                run_phase(nc, allsems, emit_p2b)

        if "p3" in phases:
            with ExitStack() as c4:
                f3 = make_ffn_tensors(c4, "f3_")
                run_phase(nc, allsems,
                          lambda tr: emit_ffn(tr, f3, stream, y_out))
        elif "copy" in phases:
            def emit_copy(tr):
                c = tr.dma("gpsimd", y_out[:, :], stream[:, :], S.d_st)
                tr.wait("gpsimd", S.d_st, c)

            run_phase(nc, allsems, emit_copy)

    return nc


def host_consts():
    identf = np.eye(128, dtype=np.float32)
    umask = np.triu(np.ones((128, 128), dtype=np.float32))
    prefm = []
    seqb = []
    for c in range(NCORES):
        pm = np.zeros((64, 8), dtype=np.float32)
        sb_ = np.full((64, 8), -1e38, dtype=np.float32)
        for r in range(NCORES):
            if r // 4 == c // 4:
                sb_[:, r] = 0.0
                if r % 4 < c % 4:
                    pm[:, r] = 1.0
        prefm.append(pm)
        seqb.append(sb_)
    return identf, umask, prefm, seqb


def kernel(L0_attn_x, L0_attn_W_qkv, L0_attn_W_out, L0_ffn_gate_w,
           L0_ffn_w1, L0_ffn_w3, L0_ffn_w2):
    x = np.ascontiguousarray(np.asarray(L0_attn_x, dtype=np.float32))
    wqkv = np.ascontiguousarray(np.asarray(L0_attn_W_qkv, dtype=np.float32))
    wout = np.ascontiguousarray(np.asarray(L0_attn_W_out, dtype=np.float32))
    gw = np.ascontiguousarray(np.asarray(L0_ffn_gate_w, dtype=np.float32))
    w1 = np.ascontiguousarray(np.asarray(L0_ffn_w1, dtype=np.float32))
    w3 = np.ascontiguousarray(np.asarray(L0_ffn_w3, dtype=np.float32))
    w2 = np.ascontiguousarray(np.asarray(L0_ffn_w2, dtype=np.float32))

    identf, umask, prefm, seqb = host_consts()
    nc = build_program()
    in_maps = []
    for c in range(NCORES):
        b = c // 4
        t0 = (c % 4) * TPC
        in_maps.append({
            "x": np.ascontiguousarray(x[b, t0:t0 + TPC, :]),
            "wqkv": wqkv, "wout": wout, "gw": gw,
            "w1": w1, "w3": w3, "w2": w2,
            "identf": identf, "umask": umask,
            "prefm": prefm[c], "seqb": seqb[c],
        })
    res = run_bass_kernel_spmd(nc, in_maps, core_ids=list(range(NCORES)))
    out = np.empty((B, T, D), dtype=np.float32)
    for c in range(NCORES):
        b = c // 4
        t0 = (c % 4) * TPC
        out[b, t0:t0 + TPC, :] = np.asarray(res.results[c]["y"],
                                            dtype=np.float32)
    return out


# version 02
# speedup vs baseline: 1.5177x; 1.5177x over previous
import numpy as np
import ml_dtypes
from contextlib import ExitStack

import concourse.bass as bass
import concourse.mybir as mybir
from concourse.bass_utils import run_bass_kernel_spmd

F32 = mybir.dt.float32
BF16 = mybir.dt.bfloat16
AF = mybir.ActivationFunctionType
ALU = mybir.AluOpType
AX = mybir.AxisListType

NCORES = 8
B, T, D = 2, 16384, 1024
NH, HS = 16, 64
E, HID = 8, 2816
TPC = T * B // NCORES
TT = 1024
NTILE = TPC // TT
NTC = TT // 128
NKC = D // 128
NHC = HID // 128
HGROUPS = [(0, 8), (8, 16), (16, 22)]
HBLOCKS = [(0, 4), (4, 8), (8, 12), (12, 16), (16, 20), (20, 22)]
NCH = TPC // 128
LNEPS = 1e-5
CAP = 1280
ZPAD = 2048
SENT = E * CAP
I32 = mybir.dt.int32

ENGS = ("sync", "gpsimd", "vector", "tensor", "scalar")


class Sem:
    __slots__ = ("h", "count")

    def __init__(self, h):
        self.h = h
        self.count = 0


class Tr:

    def __init__(self, active, eng):
        self.a = active
        self.e = eng
        self.hist = {}

    def op(self, name, fn, *args, inc=None, incn=1, **kw):
        if self.a == name:
            inst = getattr(self.e, fn)(*args, **kw)
            if inc is not None:
                inst.then_inc(inc.h, incn)
        if inc is not None:
            inc.count += incn
            return inc.count
        return None

    def wait(self, name, sem, val):
        if val is None or val <= 0:
            return
        if self.a == name:
            self.e.wait_ge(sem.h, val)

    def dma(self, name, out, in_, sem, **kw):
        return self.op(name, "dma_start", inc=sem, incn=16, out=out, in_=in_, **kw)

    def opw(self, name, fn, *args, inc, incn=1, **kw):
        c = self.op(name, fn, *args, inc=inc, incn=incn, **kw)
        self.wait(name, inc, c)
        return c


def run_phase(nc, sems, emit):
    base = [(s, s.count) for s in sems]
    with nc.Block() as block:
        for name in ENGS:
            for s, c in base:
                s.count = c

            def body(eng, _n=name):
                emit(Tr(_n, eng))

            getattr(block, name)(body)


class NS:
    pass


def build_program(phases=("p0", "p1", "p2", "p3")):
    nc = bass.Bass("TRN2", target_bir_lowering=False)
    t = NS()

    x_in = nc.dram_tensor("x", [TPC, D], F32, kind="ExternalInput")
    wqkv_in = nc.dram_tensor("wqkv", [3 * D, D], F32, kind="ExternalInput")
    wout_in = nc.dram_tensor("wout", [D, D], F32, kind="ExternalInput")
    gw_in = nc.dram_tensor("gw", [E, D], F32, kind="ExternalInput")
    w1_in = nc.dram_tensor("w1", [E, HID, D], F32, kind="ExternalInput")
    w3_in = nc.dram_tensor("w3", [E, HID, D], F32, kind="ExternalInput")
    w2_in = nc.dram_tensor("w2", [E, D, HID], F32, kind="ExternalInput")
    identf_in = nc.dram_tensor("identf", [128, 128], F32, kind="ExternalInput")
    umask_in = nc.dram_tensor("umask", [128, 128], F32, kind="ExternalInput")
    prefm_in = nc.dram_tensor("prefm", [64, 8], F32, kind="ExternalInput")
    seqb_in = nc.dram_tensor("seqb", [64, 8], F32, kind="ExternalInput")
    slotb_in = nc.dram_tensor("slotb", [128, 8], F32, kind="ExternalInput")
    y_out = nc.dram_tensor("y", [TPC, D], F32, kind="ExternalOutput")

    w1_bf = nc.dram_tensor("w1_bf", [E, HID, D], BF16, kind="Internal")
    w3_bf = nc.dram_tensor("w3_bf", [E, HID, D], BF16, kind="Internal")
    w2_bf = nc.dram_tensor("w2_bf", [E, D, HID], BF16, kind="Internal")
    wqkv_bf = nc.dram_tensor("wqkv_bf", [3 * D, D], BF16, kind="Internal")
    wout_bf = nc.dram_tensor("wout_bf", [D, D], BF16, kind="Internal")
    stream = nc.dram_tensor("stream", [TPC, D], F32, kind="Internal")
    kcm_d = nc.dram_tensor("kcm_d", [NCH, 64, NH * 128], F32, kind="Internal")
    qscm_d = nc.dram_tensor("qscm_d", [NCH, 64, NH * 128], F32, kind="Internal")
    vtm_d = nc.dram_tensor("vtm_d", [TPC, D], BF16, kind="Internal")
    schunks_d = nc.dram_tensor("schunks_d", [NCH, 64, NH * 128], F32, kind="Internal")
    CCW = 16 + NH * 128
    ccsrc_d = nc.dram_tensor("ccsrc_d", [64, CCW], F32, kind="Internal")
    ccdst_d = nc.dram_tensor("ccdst_d", [64 * NCORES, CCW], F32, kind="Internal")
    xnbf_d = nc.dram_tensor("xnbf_d", [TPC, D], F32, kind="Internal")
    xg_d = nc.dram_tensor("xg_d", [E * CAP + 1, D], F32, kind="Internal")
    y_d = nc.dram_tensor("y_d", [E * CAP + ZPAD, D], F32, kind="Internal")

    with ExitStack() as top:
        def sb(ctx, name, shape, dt=F32):
            return ctx.enter_context(nc.sbuf_tensor(name, shape, dt))

        def ps(ctx, name, shape, dt=F32):
            return ctx.enter_context(nc.psum_tensor(name, shape, dt))

        identf = sb(top, "identf_s", [128, 128])
        umaskf = sb(top, "umaskf_s", [128, 128])
        prefm = sb(top, "prefm_s", [64, 8])
        seqb = sb(top, "seqb_s", [64, 8])
        gwsb = sb(top, "gwsb", [8, D])
        slotbase = sb(top, "slotbase_s", [128, 8])
        gwT = sb(top, "gwT", [128, 8 * E])
        Stot = sb(top, "Stot", [64, NH * 128])
        kmaxloc = sb(top, "kmaxloc", [64, NH])
        escale = sb(top, "escale", [64, NH])
        negm = sb(top, "negm", [64, NH])

        semnames = ["d_cin", "d_wc", "d_w", "d_x", "d_st", "d_sp", "cc_s",
                    "s_tr", "s_tev", "s_amm", "s_silu", "s_hm", "s_bmm",
                    "s_bep", "s_w13c", "s_w2c", "s_gmm", "s_dv", "s_ac",
                    "s_pe2", "s_dv2", "s_ac2", "s_misc"]
        sems = {n: Sem(top.enter_context(nc.semaphore(n))) for n in semnames}
        S = NS()
        for n, s in sems.items():
            setattr(S, n, s)
        allsems = list(sems.values())

        if "p0" in phases:
            with ExitStack() as c0:
                pg0 = ps(c0, "pg0", [128, 16])
                pg1 = ps(c0, "pg1", [128, 16])
                z0 = sb(c0, "z0", [128, D])

                def emit_p0(tr):
                    for ap_out, ap_in in [(identf[:, :], identf_in[:, :]),
                                          (umaskf[:, :], umask_in[:, :]),
                                          (prefm[:, :], prefm_in[:, :]),
                                          (seqb[:, :], seqb_in[:, :]),
                                          (gwsb[:, :], gw_in[:, :]),
                                          (slotbase[:, :], slotb_in[:, :])]:
                        tr.dma("sync", ap_out, ap_in, S.d_cin)
                    cin_done = S.d_cin.count
                    for e in range(E):
                        tr.dma("gpsimd", w1_bf[e, :, :], w1_in[e, :, :], S.d_wc)
                        tr.dma("gpsimd", w3_bf[e, :, :], w3_in[e, :, :], S.d_wc)
                        tr.dma("gpsimd", w2_bf[e, :, :], w2_in[e, :, :], S.d_wc)
                    tr.dma("gpsimd", wqkv_bf[:, :], wqkv_in[:, :], S.d_wc)
                    tr.dma("gpsimd", wout_bf[:, :], wout_in[:, :], S.d_wc)
                    tr.wait("tensor", S.d_cin, cin_done)
                    for k in range(NKC):
                        pg = pg0 if k % 2 == 0 else pg1
                        if k >= 2:
                            tr.wait("tensor", S.s_tev, tr.hist[("gev", k - 2)])
                        c = tr.op("tensor", "transpose", pg[:, 0:8],
                                  gwsb[0:8, k * 128:(k + 1) * 128],
                                  identf[0:8, 0:8], inc=S.s_tr)
                        tr.hist[("gtr", k)] = c
                        tr.wait("vector", S.s_tr, tr.hist[("gtr", k)])
                        c = tr.op("vector", "tensor_copy",
                                  gwT[:, k * 8:(k + 1) * 8], pg[:, 0:8],
                                  inc=S.s_tev)
                        tr.hist[("gev", k)] = c
                    tr.op("vector", "memset", Stot[:, :], 0.0)
                    tr.op("vector", "memset", kmaxloc[:, :], -1e38)
                    cz = tr.op("vector", "memset", z0[:, :], 0.0, inc=S.s_dv)
                    tr.wait("gpsimd", S.s_dv, cz)
                    for i in range(ZPAD // 128):
                        tr.dma("gpsimd",
                               y_d[E * CAP + i * 128:E * CAP + (i + 1) * 128, :],
                               z0[:, :], S.d_st)

                run_phase(nc, allsems, emit_p0)

        def make_ffn_tensors(ctx, pfx):
            f = NS()
            f.xstg = sb(ctx, pfx + "xstg", [128, D])
            f.cst = sb(ctx, pfx + "cst", [128, D])
            f.sqst = sb(ctx, pfx + "sqst", [128, D])
            f.xntm = sb(ctx, pfx + "xntm", [128, D])
            f.xnf = sb(ctx, pfx + "xnf", [128, D])
            f.coef = sb(ctx, pfx + "coef", [128, NCH * E])
            f.maskcm = sb(ctx, pfx + "maskcm", [8, TPC])
            f.zeros8 = sb(ctx, pfx + "zeros8", [8, 1024])
            f.postm = sb(ctx, pfx + "postm", [128, 8])
            f.mtm = sb(ctx, pfx + "mtm", [128, 8])
            f.ovf = sb(ctx, pfx + "ovf", [128, 8])
            f.m2t = sb(ctx, pfx + "m2t", [128, 8])
            f.slf = sb(ctx, pfx + "slf", [128, 8])
            f.sl2 = sb(ctx, pfx + "sl2", [128, 8])
            f.mm1 = sb(ctx, pfx + "mm1", [128, 8])
            f.cm1 = sb(ctx, pfx + "cm1", [128, 8])
            f.o1 = sb(ctx, pfx + "o1", [128, 1])
            f.o2 = sb(ctx, pfx + "o2", [128, 1])
            f.oc = sb(ctx, pfx + "oc", [128, 1])
            f.c12 = sb(ctx, pfx + "c12", [128, 2 * NCH])
            f.sIdx = sb(ctx, pfx + "sIdx", [128, 2 * NCH], I32)
            f.gIdx = sb(ctx, pfx + "gIdx", [128, 2 * NCH], I32)
            f.xgld = sb(ctx, pfx + "xgld", [128, D])
            f.xgfm = sb(ctx, pfx + "xgfm", [128, NKC * CAP], BF16)
            f.hbuf = sb(ctx, pfx + "hbuf", [128, 8 * CAP], BF16)
            f.outacc = sb(ctx, pfx + "outacc", [128, (CAP // 128) * D])
            f.w1b = [sb(ctx, pfx + "w1b%d" % i, [128, NKC * 512], BF16)
                     for i in range(2)]
            f.w3b = [sb(ctx, pfx + "w3b%d" % i, [128, NKC * 512], BF16)
                     for i in range(2)]
            f.w2b = sb(ctx, pfx + "w2b", [128, 8 * D], BF16)
            f.silu = [sb(ctx, pfx + "silu%d" % i, [128, 512]) for i in range(2)]
            f.g1g = sb(ctx, pfx + "g1g", [128, D])
            f.g2g = sb(ctx, pfx + "g2g", [128, D])
            f.ostg = sb(ctx, pfx + "ostg", [128, D])
            f.r1 = sb(ctx, pfx + "r1", [128, 1])
            f.epsb = sb(ctx, pfx + "epsb", [128, 1])
            f.nmean = sb(ctx, pfx + "nmean", [128, 1])
            f.ssq = sb(ctx, pfx + "ssq", [128, 1])
            f.sd = sb(ctx, pfx + "sd", [128, 1])
            f.rstd = sb(ctx, pfx + "rstd", [128, 1])
            f.m1n = sb(ctx, pfx + "m1n", [128, 1])
            f.m2 = sb(ctx, pfx + "m2", [128, 1])
            f.d1 = sb(ctx, pfx + "d1", [128, 1])
            f.r2 = sb(ctx, pfx + "r2", [128, 1])
            f.gst = sb(ctx, pfx + "gst", [128, 8])
            f.ge1 = sb(ctx, pfx + "ge1", [128, 8])
            f.tmp8 = sb(ctx, pfx + "tmp8", [128, 8])
            f.gm = sb(ctx, pfx + "gm", [128, 8])
            f.pH1 = [ps(ctx, pfx + "pH1%d" % i, [128, 512]) for i in range(2)]
            f.pH3 = [ps(ctx, pfx + "pH3%d" % i, [128, 512]) for i in range(2)]
            f.pB = [ps(ctx, pfx + "pB%d" % i, [128, 512]) for i in range(2)]
            f.pT = [ps(ctx, pfx + "pT%d" % i, [128, 128]) for i in range(2)]
            return f

        def emit_ffn(tr, f, src, dst):
            h = tr.hist
            w13c0 = S.s_w13c.count
            w2c0 = S.s_w2c.count
            xgf0 = S.s_misc.count
            tr.op("vector", "memset", f.epsb[:, :], LNEPS, inc=S.s_dv)
            tr.op("vector", "memset", f.zeros8[:, :], 0.0, inc=S.s_dv)
            trk = 0

            for tc in range(NCH):
                if tc > 0:
                    tr.wait("gpsimd", S.s_dv, h[("lnd", tc - 1)])
                c = tr.dma("gpsimd", f.xstg[:, :],
                           src[tc * 128:(tc + 1) * 128, :], S.d_x)
                h[("xld", tc)] = c
                tr.wait("vector", S.d_x, h[("xld", tc)])
                tr.opw("vector", "reduce_sum", f.r1[:, :], f.xstg[:, :],
                       axis=AX.X, inc=S.s_dv)
                tr.opw("vector", "tensor_scalar", f.nmean[:, :], f.r1[:, :],
                       -1.0 / D, None, ALU.mult, inc=S.s_dv)
                c = tr.op("vector", "tensor_scalar", f.cst[:, :], f.xstg[:, :],
                          f.nmean[:, :], None, ALU.add, inc=S.s_dv)
                h[("lnd", tc)] = c
                tr.wait("scalar", S.s_dv, c)
                tr.opw("scalar", "activation", f.sqst[:, :], f.cst[:, :],
                       AF.Square, accum_out=f.ssq[:, :], inc=S.s_ac)
                c = tr.op("scalar", "activation", f.sd[:, :], f.ssq[:, :],
                          AF.Sqrt, scale=1.0 / D, bias=f.epsb[:, :],
                          inc=S.s_ac)
                tr.wait("vector", S.s_ac, c)
                tr.opw("vector", "reciprocal", f.rstd[:, :], f.sd[:, :],
                       inc=S.s_dv)
                if tc > 0:
                    tr.wait("vector", S.s_tr, h[("trl", tc - 1)])
                    tr.wait("vector", S.d_sp, h[("xnst", tc - 1)])
                cxn = tr.op("vector", "tensor_scalar", f.xntm[:, :],
                            f.cst[:, :], f.rstd[:, :], None, ALU.mult,
                            inc=S.s_dv)
                tr.wait("gpsimd", S.s_dv, cxn)
                c = tr.dma("gpsimd", xnbf_d[tc * 128:(tc + 1) * 128, :],
                           f.xntm[:, :], S.d_sp)
                h[("xnst", tc)] = c
                tr.wait("tensor", S.s_dv, cxn)
                for k in range(NKC):
                    slot = trk % 2
                    if trk >= 2:
                        tr.wait("tensor", S.s_tev, h[("tev", trk - 2)])
                    c = tr.op("tensor", "transpose", f.pT[slot][:, :],
                              f.xntm[:, k * 128:(k + 1) * 128],
                              identf[:, :], inc=S.s_tr)
                    h[("tr", trk)] = c
                    tr.wait("vector", S.s_tr, h[("tr", trk)])
                    c = tr.op("vector", "tensor_copy",
                              f.xnf[:, k * 128:(k + 1) * 128],
                              f.pT[slot][:, :], inc=S.s_tev)
                    h[("tev", trk)] = c
                    trk += 1
                tr.wait("tensor", S.s_tev, h[("tev", trk - 1)])
                if tc > 0:
                    tr.wait("tensor", S.s_ac, h[("gex", tc - 1)])
                for k in range(NKC):
                    tr.op("tensor", "matmul", f.pB[0][:, 0:8],
                          f.xnf[:, k * 128:(k + 1) * 128],
                          gwT[:, k * 8:(k + 1) * 8],
                          start=(k == 0), stop=(k == NKC - 1),
                          skip_group_check=True,
                          inc=S.s_gmm if k == NKC - 1 else None)
                h[("g", tc)] = S.s_gmm.count
                tr.wait("vector", S.s_gmm, h[("g", tc)])
                c = tr.op("vector", "reduce_max", f.m1n[:, :],
                          f.pB[0][:, 0:8], axis=AX.X, negate=True, inc=S.s_dv)
                tr.wait("scalar", S.s_dv, c)
                c = tr.op("scalar", "activation", f.gst[:, :],
                          f.pB[0][:, 0:8], AF.Exp, bias=f.m1n[:, :],
                          inc=S.s_ac)
                h[("gex", tc)] = c
                tr.wait("vector", S.s_ac, c)
                tr.opw("vector", "tensor_single_scalar", f.ge1[:, :],
                       f.gst[:, :], 1.0, ALU.is_ge, inc=S.s_dv)
                tr.opw("vector", "scalar_tensor_tensor", f.tmp8[:, :],
                       f.ge1[:, :], -1e30, f.gst[:, :], ALU.mult, ALU.add,
                       inc=S.s_dv)
                tr.opw("vector", "reduce_max", f.m2[:, :], f.tmp8[:, :],
                       axis=AX.X, inc=S.s_dv)
                tr.opw("vector", "tensor_single_scalar", f.tmp8[:, :],
                       f.gst[:, :], f.m2[:, :], ALU.is_ge, inc=S.s_dv)
                tr.opw("vector", "tensor_tensor", f.gm[:, :], f.gst[:, :],
                       f.tmp8[:, :], ALU.mult, inc=S.s_dv)
                tr.opw("vector", "tensor_scalar", f.d1[:, :], f.m2[:, :],
                       1.0, None, ALU.add, inc=S.s_dv)
                tr.opw("vector", "reciprocal", f.r2[:, :], f.d1[:, :],
                       inc=S.s_dv)
                tr.opw("vector", "tensor_scalar",
                       f.coef[:, tc * E:(tc + 1) * E], f.gm[:, :],
                       f.r2[:, :], 0.5, ALU.mult, ALU.mult, inc=S.s_dv)
                cmk = tr.opw("vector", "tensor_single_scalar", f.gm[:, :],
                             f.coef[:, tc * E:(tc + 1) * E], 0.0, ALU.is_gt,
                             inc=S.s_dv)
                slot = trk % 2
                tr.wait("tensor", S.s_dv, cmk)
                if trk >= 2:
                    tr.wait("tensor", S.s_tev, h[("tev", trk - 2)])
                c = tr.op("tensor", "transpose", f.pT[slot][0:8, 0:128],
                          f.gm[:, :], identf[:, :], inc=S.s_tr)
                h[("tr", trk)] = c
                h[("trl", tc)] = c
                tr.wait("vector", S.s_tr, h[("tr", trk)])
                c = tr.op("vector", "tensor_copy",
                          f.maskcm[:, tc * 128:(tc + 1) * 128],
                          f.pT[slot][0:8, 0:128], inc=S.s_tev)
                h[("tev", trk)] = c
                trk += 1
            h[("g1_done",)] = S.s_dv.count

            tr.wait("vector", S.s_tev, h[("tev", trk - 1)])
            for b in range(4):
                init = (f.zeros8[:, 0:1] if b == 0
                        else f.maskcm[:, b * 1024 - 1:b * 1024])
                tr.opw("vector", "tensor_tensor_scan",
                       f.maskcm[:, b * 1024:(b + 1) * 1024],
                       f.zeros8[:, :],
                       f.maskcm[:, b * 1024:(b + 1) * 1024],
                       init, ALU.add, ALU.add, inc=S.s_dv)
            for tc in range(NCH):
                slot = trk % 2
                if trk >= 2:
                    tr.wait("tensor", S.s_tev, h[("tev", trk - 2)])
                tr.wait("tensor", S.s_dv, S.s_dv.count)
                c = tr.op("tensor", "transpose", f.pT[slot][:, 0:8],
                          f.maskcm[:, tc * 128:(tc + 1) * 128],
                          identf[0:8, 0:8], inc=S.s_tr)
                h[("tr", trk)] = c
                tr.wait("vector", S.s_tr, h[("tr", trk)])
                c = tr.op("vector", "tensor_copy", f.postm[:, :],
                          f.pT[slot][:, 0:8], inc=S.s_tev)
                h[("tev", trk)] = c
                trk += 1
                tr.wait("vector", S.s_tev, h[("tev", trk - 1)])
                tr.opw("vector", "tensor_single_scalar", f.mtm[:, :],
                       f.coef[:, tc * E:(tc + 1) * E], 0.0, ALU.is_gt,
                       inc=S.s_dv)
                tr.opw("vector", "tensor_single_scalar", f.ovf[:, :],
                       f.postm[:, :], float(CAP), ALU.is_le, inc=S.s_dv)
                tr.opw("vector", "tensor_tensor", f.m2t[:, :], f.mtm[:, :],
                       f.ovf[:, :], ALU.mult, inc=S.s_dv)
                tr.opw("vector", "tensor_tensor", f.slf[:, :], f.postm[:, :],
                       slotbase[:, :], ALU.add, inc=S.s_dv)
                tr.opw("vector", "tensor_tensor", f.slf[:, :], f.slf[:, :],
                       f.m2t[:, :], ALU.mult, inc=S.s_dv)
                tr.opw("vector", "tensor_scalar", f.sl2[:, :], f.m2t[:, :],
                       -float(SENT), float(SENT), ALU.mult, ALU.add,
                       inc=S.s_dv)
                tr.opw("vector", "tensor_tensor", f.slf[:, :], f.slf[:, :],
                       f.sl2[:, :], ALU.add, inc=S.s_dv)
                tr.opw("vector", "tensor_reduce", f.o1[:, :], f.slf[:, :],
                       axis=AX.X, op=ALU.min, inc=S.s_dv)
                tr.opw("vector", "tensor_single_scalar", f.mm1[:, :],
                       f.slf[:, :], f.o1[:, :], ALU.is_equal, inc=S.s_dv)
                tr.opw("vector", "scalar_tensor_tensor", f.sl2[:, :],
                       f.mm1[:, :], float(SENT), f.slf[:, :], ALU.mult,
                       ALU.add, inc=S.s_dv)
                tr.opw("vector", "tensor_reduce", f.o2[:, :], f.sl2[:, :],
                       axis=AX.X, op=ALU.min, inc=S.s_dv)
                tr.opw("vector", "tensor_tensor", f.cm1[:, :], f.mm1[:, :],
                       f.coef[:, tc * E:(tc + 1) * E], ALU.mult, inc=S.s_dv)
                tr.opw("vector", "reduce_sum", f.c12[:, 2 * tc:2 * tc + 1],
                       f.cm1[:, :], axis=AX.X, inc=S.s_dv)
                tr.opw("vector", "tensor_single_scalar", f.mm1[:, :],
                       f.sl2[:, :], f.o2[:, :], ALU.is_equal, inc=S.s_dv)
                tr.opw("vector", "tensor_tensor", f.cm1[:, :], f.mm1[:, :],
                       f.coef[:, tc * E:(tc + 1) * E], ALU.mult, inc=S.s_dv)
                tr.opw("vector", "reduce_sum", f.c12[:, 2 * tc + 1:2 * tc + 2],
                       f.cm1[:, :], axis=AX.X, inc=S.s_dv)
                tr.opw("vector", "tensor_copy", f.sIdx[:, 2 * tc:2 * tc + 1],
                       f.o1[:, :], inc=S.s_dv)
                tr.opw("vector", "tensor_copy",
                       f.sIdx[:, 2 * tc + 1:2 * tc + 2], f.o2[:, :],
                       inc=S.s_dv)
                tr.opw("vector", "tensor_single_scalar", f.oc[:, :],
                       f.o1[:, :], float(E * CAP), ALU.min, inc=S.s_dv)
                tr.opw("vector", "tensor_copy", f.gIdx[:, 2 * tc:2 * tc + 1],
                       f.oc[:, :], inc=S.s_dv)
                tr.opw("vector", "tensor_single_scalar", f.oc[:, :],
                       f.o2[:, :], float(E * CAP), ALU.min, inc=S.s_dv)
                c = tr.opw("vector", "tensor_copy",
                           f.gIdx[:, 2 * tc + 1:2 * tc + 2], f.oc[:, :],
                           inc=S.s_dv)
                h[("idx", tc)] = c
            h[("g2_done",)] = S.s_dv.count

            tr.wait("gpsimd", S.s_dv, h[("g2_done",)])
            tr.wait("gpsimd", S.d_sp, h[("xnst", NCH - 1)])
            for tc in range(NCH):
                if tc > 0:
                    tr.wait("gpsimd", S.d_sp, h[("sc", tc - 1)])
                c = tr.dma("gpsimd", f.xgld[:, :],
                           xnbf_d[tc * 128:(tc + 1) * 128, :], S.d_x)
                tr.wait("gpsimd", S.d_x, c)
                for kk in range(2):
                    c = tr.op("gpsimd", "indirect_dma_start",
                              out=xg_d[:, :],
                              out_offset=bass.IndirectOffsetOnAxis(
                                  f.sIdx[:, 2 * tc + kk:2 * tc + kk + 1], 0),
                              in_=f.xgld[:, :], in_offset=None,
                              inc=S.d_sp, incn=16)
                h[("sc", tc)] = c
            h[("scat_done",)] = c
            tr.wait("gpsimd", S.d_sp, h[("scat_done",)])

            astep = 0
            bstep = 0
            blkidx = 0
            grpidx = 0
            NTE = CAP // 128
            ASL = [(0, 512), (512, 1024), (1024, CAP)]
            for e in range(E):
                for te in range(NTE):
                    if (e, te) != (0, 0):
                        tr.wait("gpsimd", S.s_tr, h[("xgtr_prev",)])
                    c = tr.dma("gpsimd", f.xgld[:, :],
                               xg_d[e * CAP + te * 128:
                                    e * CAP + (te + 1) * 128, :], S.d_x)
                    h[("xg", e, te)] = c
                    tr.wait("tensor", S.d_x, h[("xg", e, te)])
                    if e > 0 and te == 0:
                        tr.wait("tensor", S.s_amm, h[("aend", e - 1)])
                    for k in range(NKC):
                        slot = trk % 2
                        if trk >= 2:
                            tr.wait("tensor", S.s_tev, h[("tev", trk - 2)])
                        c = tr.op("tensor", "transpose", f.pT[slot][:, :],
                                  f.xgld[:, k * 128:(k + 1) * 128],
                                  identf[:, :], inc=S.s_tr)
                        h[("tr", trk)] = c
                        tr.wait("vector", S.s_tr, h[("tr", trk)])
                        c = tr.op("vector", "tensor_copy",
                                  f.xgfm[:, k * CAP + te * 128:
                                         k * CAP + (te + 1) * 128],
                                  f.pT[slot][:, :], inc=S.s_tev)
                        h[("tev", trk)] = c
                        trk += 1
                    h[("xgtr_prev",)] = S.s_tr.count
                tr.op("vector", "nop", inc=S.s_misc)
                h[("xgfm", e)] = S.s_misc.count
                for gi, (g0, g1) in enumerate(HGROUPS):
                    for bl in (2 * gi, 2 * gi + 1):
                        hb0, hb1 = HBLOCKS[bl]
                        nh2 = hb1 - hb0
                        if blkidx >= 2:
                            tr.wait("sync", S.s_w13c, w13c0 + blkidx - 1)
                        for k in range(NKC):
                            tr.dma("sync",
                                   f.w1b[blkidx % 2][:, k * 512:
                                                     k * 512 + nh2 * 128],
                                   w1_bf[e, hb0 * 128:hb1 * 128,
                                         k * 128:(k + 1) * 128],
                                   S.d_w, transpose=True)
                            c = tr.dma("sync",
                                       f.w3b[blkidx % 2][:, k * 512:
                                                         k * 512 + nh2 * 128],
                                       w3_bf[e, hb0 * 128:hb1 * 128,
                                             k * 128:(k + 1) * 128],
                                       S.d_w, transpose=True)
                        h[("wblk", blkidx)] = c
                        blkidx += 1
                    if grpidx >= 1:
                        tr.wait("sync", S.s_w2c, w2c0 + grpidx)
                    for hl, hh in enumerate(range(g0, g1)):
                        c = tr.dma("sync", f.w2b[:, hl * D:(hl + 1) * D],
                                   w2_bf[e, 0:D, hh * 128:(hh + 1) * 128],
                                   S.d_w, transpose=True)
                    h[("w2g", grpidx)] = c
                    grpidx += 1

                    for hh in range(g0, g1):
                        bl = hh // 4
                        gblk = e * 6 + bl
                        par = gblk % 2
                        hloc = hh - HBLOCKS[bl][0]
                        for n, (n0, n1) in enumerate(ASL):
                            p = astep % 2
                            tr.wait("tensor", S.d_w, h[("wblk", gblk)])
                            if astep == 0:
                                tr.wait("tensor", S.s_misc, h[("xgfm", e)])
                            if astep >= 2:
                                tr.wait("tensor", S.s_silu,
                                        h[("silu", astep - 2)])
                                tr.wait("tensor", S.s_hm, h[("hm", astep - 2)])
                            for k in range(NKC):
                                tr.op("tensor", "matmul",
                                      f.pH1[p][:, 0:n1 - n0],
                                      f.w1b[par][:, k * 512 + hloc * 128:
                                                 k * 512 + (hloc + 1) * 128],
                                      f.xgfm[:, k * CAP + n0:k * CAP + n1],
                                      start=(k == 0), stop=(k == NKC - 1))
                            for k in range(NKC):
                                tr.op("tensor", "matmul",
                                      f.pH3[p][:, 0:n1 - n0],
                                      f.w3b[par][:, k * 512 + hloc * 128:
                                                 k * 512 + (hloc + 1) * 128],
                                      f.xgfm[:, k * CAP + n0:k * CAP + n1],
                                      start=(k == 0), stop=(k == NKC - 1),
                                      inc=S.s_amm if k == NKC - 1 else None)
                            h[("amm", astep)] = S.s_amm.count
                            tr.wait("scalar", S.s_amm, h[("amm", astep)])
                            if astep >= 2:
                                tr.wait("scalar", S.s_hm, h[("hm", astep - 2)])
                            c = tr.op("scalar", "activation",
                                      f.silu[p][:, 0:n1 - n0],
                                      f.pH1[p][:, 0:n1 - n0], AF.Silu,
                                      inc=S.s_silu)
                            h[("silu", astep)] = c
                            tr.wait("vector", S.s_silu, h[("silu", astep)])
                            c = tr.op("vector", "tensor_tensor",
                                      f.hbuf[:, (hh - g0) * CAP + n0:
                                             (hh - g0) * CAP + n1],
                                      f.silu[p][:, 0:n1 - n0],
                                      f.pH3[p][:, 0:n1 - n0],
                                      ALU.mult, inc=S.s_hm)
                            h[("hm", astep)] = c
                            astep += 1
                        if hh == HBLOCKS[bl][1] - 1:
                            tr.op("tensor", "nop", inc=S.s_w13c)
                    nhg = g1 - g0
                    first_b = True
                    for te in range(NTE):
                        for half in range(2):
                            p = bstep % 2
                            if first_b:
                                tr.wait("tensor", S.s_hm, h[("hm", astep - 1)])
                                tr.wait("tensor", S.d_w,
                                        h[("w2g", grpidx - 1)])
                                first_b = False
                            if bstep >= 2:
                                tr.wait("tensor", S.s_bep,
                                        h[("bep", bstep - 2)])
                            for hl in range(nhg):
                                tr.op("tensor", "matmul", f.pB[p][:, :],
                                      f.hbuf[:, hl * CAP + te * 128:
                                             hl * CAP + (te + 1) * 128],
                                      f.w2b[:, hl * D + half * 512:
                                            hl * D + (half + 1) * 512],
                                      start=(hl == 0), stop=(hl == nhg - 1),
                                      skip_group_check=True,
                                      inc=S.s_bmm if hl == nhg - 1 else None)
                            h[("bmm", bstep)] = S.s_bmm.count
                            tr.wait("vector", S.s_bmm, h[("bmm", bstep)])
                            if e > 0 and gi == 0 and te == 0 and half == 0:
                                tr.wait("vector", S.d_st, h[("yst", e - 1)])
                            oslc = f.outacc[:, te * D + half * 512:
                                            te * D + (half + 1) * 512]
                            if gi == 0:
                                c = tr.op("vector", "tensor_copy", oslc,
                                          f.pB[p][:, :], inc=S.s_bep)
                            else:
                                c = tr.op("vector", "tensor_tensor", oslc,
                                          f.pB[p][:, :], oslc, ALU.add,
                                          inc=S.s_bep)
                            h[("bep", bstep)] = c
                            bstep += 1
                    tr.op("tensor", "nop", inc=S.s_w2c)
                h[("aend", e)] = S.s_amm.count
                for te in range(NTE):
                    tr.wait("gpsimd", S.s_bep, h[("bep", bstep - 1)])
                    c = tr.dma("gpsimd",
                               y_d[e * CAP + te * 128:
                                   e * CAP + (te + 1) * 128, :],
                               f.outacc[:, te * D:(te + 1) * D], S.d_st)
                h[("yst", e)] = c

            tr.wait("gpsimd", S.d_st, h[("yst", E - 1)])
            for tc in range(NCH):
                if tc > 0:
                    tr.wait("gpsimd", S.s_dv, h[("res", tc - 1)])
                c = tr.op("gpsimd", "indirect_dma_start",
                          out=f.g1g[:, :], out_offset=None,
                          in_=y_d[:, :],
                          in_offset=bass.IndirectOffsetOnAxis(
                              f.gIdx[:, 2 * tc:2 * tc + 1], 0),
                          inc=S.d_x, incn=16)
                c = tr.op("gpsimd", "indirect_dma_start",
                          out=f.g2g[:, :], out_offset=None,
                          in_=y_d[:, :],
                          in_offset=bass.IndirectOffsetOnAxis(
                              f.gIdx[:, 2 * tc + 1:2 * tc + 2], 0),
                          inc=S.d_x, incn=16)
                c = tr.dma("gpsimd", f.xstg[:, :],
                           src[tc * 128:(tc + 1) * 128, :], S.d_x)
                h[("rld", tc)] = c
                tr.wait("vector", S.d_x, h[("rld", tc)])
                tr.opw("vector", "scalar_tensor_tensor", f.ostg[:, :],
                       f.g1g[:, :], f.c12[:, 2 * tc:2 * tc + 1],
                       f.xstg[:, :], ALU.mult, ALU.add, inc=S.s_dv)
                c = tr.op("vector", "scalar_tensor_tensor", f.ostg[:, :],
                          f.g2g[:, :], f.c12[:, 2 * tc + 1:2 * tc + 2],
                          f.ostg[:, :], ALU.mult, ALU.add, inc=S.s_dv)
                h[("res", tc)] = c
                tr.wait("gpsimd", S.s_dv, h[("res", tc)])
                c = tr.dma("gpsimd", dst[tc * 128:(tc + 1) * 128, :],
                           f.ostg[:, :], S.d_st)
            tr.wait("gpsimd", S.d_st, S.d_st.count)


        if "p1" in phases:
            with ExitStack() as c1:
                f1 = make_ffn_tensors(c1, "f1_")
                run_phase(nc, allsems,
                          lambda tr: emit_ffn(tr, f1, x_in, stream))

        if "p2" in phases:
            with ExitStack() as c2:
                a = NS()
                a.wqkvT = sb(c2, "a_wqkvT", [128, NKC * 3 * D], BF16)
                a.xres1 = sb(c2, "a_xres1", [128, D])
                a.cst = sb(c2, "a_cst", [128, D])
                a.sqst = sb(c2, "a_sqst", [128, D])
                a.xntm = sb(c2, "a_xntm", [128, D])
                a.xnbf1 = sb(c2, "a_xnbf1", [128, D], BF16)
                a.qstage = sb(c2, "a_qstage", [128, D])
                a.qsm = sb(c2, "a_qsm", [128, D])
                a.kstage = sb(c2, "a_kstage", [128, D])
                a.kexp = sb(c2, "a_kexp", [128, D])
                a.kexpb = sb(c2, "a_kexpb", [128, D], BF16)
                a.vbf = sb(c2, "a_vbf", [128, D], BF16)
                a.kcm = sb(c2, "a_kcm", [64, NH * 128])
                a.qscm = sb(c2, "a_qscm", [64, NH * 128])
                a.schst = [sb(c2, "a_schst%d" % i, [64, NH * 128])
                           for i in range(2)]
                a.kmaxc = sb(c2, "a_kmaxc", [64, NH])
                a.onesb = sb(c2, "a_onesb", [128, 1], BF16)
                a.r1 = sb(c2, "a_r1", [128, 1])
                a.nmean = sb(c2, "a_nmean", [128, 1])
                a.ssq = sb(c2, "a_ssq", [128, 1])
                a.sd = sb(c2, "a_sd", [128, 1])
                a.rstd = sb(c2, "a_rstd", [128, 1])
                a.qm = sb(c2, "a_qm", [128, 1])
                a.epsb = sb(c2, "a_epsb", [128, 1])
                a.qsum = sb(c2, "a_qsum", [128, 1])
                a.qr = sb(c2, "a_qr", [128, 1])
                a.pQ = [ps(c2, "a_pQ%d" % i, [128, 512]) for i in range(2)]
                a.pT = [ps(c2, "a_pT%d" % i, [128, 128]) for i in range(2)]
                a.pS = ps(c2, "a_pS", [64, NH * 128])

                def emit_p2a(tr):
                    h = tr.hist
                    for k in range(NKC):
                        for s2 in range(2):
                            c = tr.dma("sync",
                                       a.wqkvT[:, k * 3 * D + s2 * 1536:
                                               k * 3 * D + (s2 + 1) * 1536],
                                       wqkv_bf[s2 * 1536:(s2 + 1) * 1536,
                                               k * 128:(k + 1) * 128],
                                       S.d_w, transpose=True)
                    h["wq"] = c
                    tr.op("vector", "memset", a.onesb[:, :], 1.0)
                    tr.op("vector", "memset", a.epsb[:, :], LNEPS)
                    c0 = tr.op("vector", "memset", a.pS[:, :], 0.0,
                               inc=S.s_dv2)
                    trk = 0
                    for n in range(NCH):
                        if n > 0:
                            tr.wait("gpsimd", S.s_dv2, h[("xdone", n - 1)])
                        c = tr.dma("gpsimd", a.xres1[:, :],
                                   stream[n * 128:(n + 1) * 128, :], S.d_x)
                        h[("xld", n)] = c
                        tr.wait("vector", S.d_x, h[("xld", n)])
                        if n > 0:
                            tr.wait("vector", S.s_pe2, h[("smm", n - 1)])
                            tr.wait("vector", S.d_sp, h[("spst", n - 1)])
                            tr.wait("vector", S.s_tr, h[("trlast", n - 1)])
                        tr.opw("vector", "reduce_sum", a.r1[:, :],
                               a.xres1[:, :], axis=AX.X, inc=S.s_dv2)
                        tr.opw("vector", "tensor_scalar", a.nmean[:, :],
                               a.r1[:, :], -1.0 / D, None, ALU.mult,
                               inc=S.s_dv2)
                        c = tr.op("vector", "tensor_scalar", a.cst[:, :],
                                  a.xres1[:, :], a.nmean[:, :], None, ALU.add,
                                  inc=S.s_dv2)
                        h[("xdone", n)] = c
                        tr.wait("scalar", S.s_dv2, c)
                        tr.opw("scalar", "activation", a.sqst[:, :],
                               a.cst[:, :], AF.Square, accum_out=a.ssq[:, :],
                               inc=S.s_ac2)
                        c = tr.op("scalar", "activation", a.sd[:, :],
                                  a.ssq[:, :], AF.Sqrt, scale=1.0 / D,
                                  bias=a.epsb[:, :], inc=S.s_ac2)
                        tr.wait("vector", S.s_ac2, c)
                        tr.opw("vector", "reciprocal", a.rstd[:, :],
                               a.sd[:, :], inc=S.s_dv2)
                        cxn = tr.op("vector", "tensor_scalar", a.xntm[:, :],
                                    a.cst[:, :], a.rstd[:, :], None, ALU.mult,
                                    inc=S.s_dv2)
                        tr.wait("tensor", S.s_dv2, cxn)
                        for k in range(NKC):
                            slot = trk % 2
                            if trk >= 2:
                                tr.wait("tensor", S.s_tev, h[("tev", trk - 2)])
                            c = tr.op("tensor", "transpose", a.pT[slot][:, :],
                                      a.xntm[:, k * 128:(k + 1) * 128],
                                      identf[:, :], inc=S.s_tr)
                            h[("tr", trk)] = c
                            tr.wait("vector", S.s_tr, h[("tr", trk)])
                            c = tr.op("vector", "tensor_copy",
                                      a.xnbf1[:, k * 128:(k + 1) * 128],
                                      a.pT[slot][:, :], inc=S.s_tev)
                            h[("tev", trk)] = c
                            trk += 1
                        tr.wait("tensor", S.s_tev, h[("tev", trk - 1)])
                        if n == 0:
                            tr.wait("tensor", S.d_w, h["wq"])
                        for qs in range(6):
                            p = qs % 2
                            if n > 0 or qs >= 2:
                                tr.wait("tensor", S.s_dv2,
                                        h[("qev", n if qs >= 2 else n - 1,
                                           qs - 2 if qs >= 2 else qs + 4)])
                            for k in range(NKC):
                                c = tr.op("tensor", "matmul", a.pQ[p][:, :],
                                          a.xnbf1[:, k * 128:(k + 1) * 128],
                                          a.wqkvT[:, k * 3 * D + qs * 512:
                                                  k * 3 * D + (qs + 1) * 512],
                                          start=(k == 0), stop=(k == NKC - 1),
                                          inc=S.s_pe2
                                          if k == NKC - 1 else None)
                            h[("qkv", n, qs)] = S.s_pe2.count
                            tr.wait("vector", S.s_pe2, h[("qkv", n, qs)])
                            if qs < 2:
                                dstap = a.qstage[:, qs * 512:(qs + 1) * 512]
                            elif qs < 4:
                                dstap = a.kstage[:, (qs - 2) * 512:
                                                 (qs - 1) * 512]
                            else:
                                dstap = a.vbf[:, (qs - 4) * 512:(qs - 3) * 512]
                            c = tr.op("vector", "tensor_copy", dstap,
                                      a.pQ[p][:, :], inc=S.s_dv2)
                            h[("qev", n, qs)] = c
                        for hd in range(NH):
                            sl = a.qstage[:, hd * HS:(hd + 1) * HS]
                            c = tr.op("vector", "reduce_max", a.qm[:, :], sl,
                                      axis=AX.X, negate=True, inc=S.s_dv2)
                            tr.wait("scalar", S.s_dv2, c)
                            c = tr.op("scalar", "activation",
                                      a.qsm[:, hd * HS:(hd + 1) * HS], sl,
                                      AF.Exp, bias=a.qm[:, :],
                                      accum_out=a.qsum[:, :], inc=S.s_ac2)
                            tr.wait("vector", S.s_ac2, c)
                            tr.opw("vector", "reciprocal", a.qr[:, :],
                                   a.qsum[:, :], inc=S.s_dv2)
                            tr.opw("vector", "tensor_scalar",
                                   a.qsm[:, hd * HS:(hd + 1) * HS],
                                   a.qsm[:, hd * HS:(hd + 1) * HS],
                                   a.qr[:, :], 0.125, ALU.mult, ALU.mult,
                                   inc=S.s_dv2)
                        c = tr.op("vector", "nop", inc=S.s_dv2)
                        tr.wait("scalar", S.s_dv2, c)
                        c = tr.op("scalar", "activation", a.kexp[:, :],
                                  a.kstage[:, :], AF.Exp, inc=S.s_ac2)
                        tr.wait("vector", S.s_ac2, c)
                        cvb = tr.op("vector", "tensor_copy", a.kexpb[:, :],
                                    a.kexp[:, :], inc=S.s_dv2)
                        tr.wait("tensor", S.s_dv2, cvb)
                        if n > 0:
                            tr.wait("tensor", S.s_dv2, h[("sdone", n - 1)])
                        else:
                            tr.wait("tensor", S.s_dv2, c0)
                        for hd in range(NH):
                            tr.op("tensor", "matmul",
                                  a.pS[0:64, hd * 128:hd * 128 + 64],
                                  a.kexpb[:, hd * HS:(hd + 1) * HS],
                                  a.vbf[:, hd * HS:(hd + 1) * HS],
                                  start=(hd % 4 == 0), stop=False,
                                  skip_group_check=True)
                            c = tr.op("tensor", "matmul",
                                      a.pS[0:64, hd * 128 + 64:hd * 128 + 65],
                                      a.kexpb[:, hd * HS:(hd + 1) * HS],
                                      a.onesb[:, :], start=False,
                                      stop=(hd % 4 == 3),
                                      skip_group_check=True,
                                      inc=S.s_pe2 if hd == NH - 1 else None)
                        h[("smm", n)] = S.s_pe2.count
                        for hd in range(NH):
                            for which, srcap, dstt in (
                                    (0, a.kstage[:, hd * HS:(hd + 1) * HS],
                                     a.kcm),
                                    (1, a.qsm[:, hd * HS:(hd + 1) * HS],
                                     a.qscm)):
                                slot = trk % 2
                                if trk >= 2:
                                    tr.wait("tensor", S.s_tev,
                                            h[("tev", trk - 2)])
                                c = tr.op("tensor", "transpose",
                                          a.pT[slot][0:64, :], srcap,
                                          identf[:, :], inc=S.s_tr)
                                h[("tr", trk)] = c
                                tr.wait("vector", S.s_tr, h[("tr", trk)])
                                c = tr.op("vector", "tensor_copy",
                                          dstt[:, hd * 128:(hd + 1) * 128],
                                          a.pT[slot][0:64, :], inc=S.s_tev)
                                h[("tev", trk)] = c
                                trk += 1
                        cktr = S.s_tev.count
                        h[("trlast", n)] = S.s_tr.count
                        for hd in range(NH):
                            cr = tr.op("vector", "reduce_max",
                                       a.kmaxc[:, hd:hd + 1],
                                       a.kcm[:, hd * 128:(hd + 1) * 128],
                                       axis=AX.X,
                                       inc=S.s_dv2 if hd == NH - 1 else None)
                        tr.wait("vector", S.s_dv2, cr)
                        tr.opw("vector", "tensor_tensor", kmaxloc[:, :],
                               kmaxloc[:, :], a.kmaxc[:, :], ALU.max,
                               inc=S.s_dv2)
                        p2 = n % 2
                        tr.wait("vector", S.s_pe2, h[("smm", n)])
                        if n >= 2:
                            tr.wait("vector", S.d_sp, h[("spst", n - 2)])
                        tr.opw("vector", "tensor_copy", a.schst[p2][:, :],
                               a.pS[:, :], inc=S.s_dv2)
                        c = tr.op("vector", "tensor_tensor", Stot[:, :],
                                  Stot[:, :], a.schst[p2][:, :], ALU.add,
                                  inc=S.s_dv2)
                        h[("sdone", n)] = c
                        tr.wait("gpsimd", S.s_tev, cktr)
                        tr.dma("gpsimd", kcm_d[n, :, :], a.kcm[:, :], S.d_sp)
                        tr.dma("gpsimd", qscm_d[n, :, :], a.qscm[:, :], S.d_sp)
                        tr.wait("gpsimd", S.s_dv2, h[("qev", n, 5)])
                        tr.dma("gpsimd", vtm_d[n * 128:(n + 1) * 128, :],
                               a.vbf[:, :], S.d_sp)
                        tr.wait("gpsimd", S.s_dv2, h[("sdone", n)])
                        c = tr.dma("gpsimd", schunks_d[n, :, :],
                                   a.schst[p2][:, :], S.d_sp)
                        h[("spst", n)] = c
                    tr.wait("gpsimd", S.d_sp, S.d_sp.count)
                    cl = tr.op("vector", "nop", inc=S.s_dv2)
                    tr.wait("gpsimd", S.s_dv2, cl)
                    tr.dma("gpsimd", ccsrc_d[:, 0:16], kmaxloc[:, :], S.d_sp)
                    c = tr.dma("gpsimd", ccsrc_d[:, 16:CCW], Stot[:, :],
                               S.d_sp)
                    tr.wait("gpsimd", S.d_sp, c)
                    tr.op("gpsimd", "collective_compute", "AllGather",
                          ALU.bypass,
                          replica_groups=[list(range(NCORES))],
                          ins=[ccsrc_d.ap().opt()], outs=[ccdst_d.ap().opt()],
                          inc=S.cc_s)
                    tr.wait("gpsimd", S.cc_s, 1)

                run_phase(nc, allsems, emit_p2a)

            with ExitStack() as c3:
                b2 = NS()
                b2.woutT = sb(c3, "b_woutT", [128, NKC * D], BF16)
                b2.gath = sb(c3, "b_gath", [64, CCW])
                b2.Scar = sb(c3, "b_Scar", [64, NH * 128])
                b2.Scarb = sb(c3, "b_Scarb", [64, NH * HS], BF16)
                b2.kcm = sb(c3, "b_kcm", [64, NH * 128])
                b2.kexp = sb(c3, "b_kexp", [64, NH * 128])
                b2.kexpb = sb(c3, "b_kexpb", [64, NH * 128], BF16)
                b2.qscm = sb(c3, "b_qscm", [64, NH * 128])
                b2.qtb = sb(c3, "b_qtb", [64, NH * 128], BF16)
                b2.zfull = sb(c3, "b_zfull", [64, NH * 128])
                b2.ztmp = sb(c3, "b_ztmp", [64, NH * 128])
                b2.schk = sb(c3, "b_schk", [64, NH * 128])
                b2.vchunk = sb(c3, "b_vchunk", [128, D], BF16)
                b2.abf = [sb(c3, "b_abf%d" % i, [128, 128], BF16)
                          for i in range(2)]
                b2.ofm = sb(c3, "b_ofm", [128, D], BF16)
                b2.xres1 = sb(c3, "b_xres1", [128, D])
                b2.ones64 = sb(c3, "b_ones64", [64, 128])
                b2.pA = [ps(c3, "b_pA%d" % i, [128, 128]) for i in range(2)]
                b2.pO = ps(c3, "b_pO", [64, NH * 128])
                b2.pXO = [ps(c3, "b_pXO%d" % i, [128, 512]) for i in range(2)]

                def emit_p2b(tr):
                    h = tr.hist
                    for k in range(NKC):
                        c = tr.dma("sync", b2.woutT[:, k * D:(k + 1) * D],
                                   wout_bf[0:D, k * 128:(k + 1) * 128],
                                   S.d_w, transpose=True)
                    h["wo"] = c
                    tr.op("vector", "memset", b2.ones64[:, :], 1.0)
                    tr.op("vector", "memset", b2.Scar[:, :], 0.0)
                    tr.op("vector", "memset", kmaxloc[:, :], -1e38)
                    cm = tr.op("vector", "nop", inc=S.s_dv2)
                    for r in range(NCORES):
                        tr.wait("gpsimd", S.s_dv2,
                                cm if r == 0 else h[("gproc", r - 1)])
                        c = tr.dma("gpsimd", b2.gath[:, :],
                                   ccdst_d[r * 64:(r + 1) * 64, :], S.d_x)
                        h[("gld", r)] = c
                        tr.wait("vector", S.d_x, h[("gld", r)])
                        tr.opw("vector", "tensor_scalar", b2.kcm[:, 0:16],
                               b2.gath[:, 0:16], seqb[:, r:r + 1], None,
                               ALU.add, inc=S.s_dv2)
                        tr.opw("vector", "tensor_tensor", kmaxloc[:, :],
                               kmaxloc[:, :], b2.kcm[:, 0:16], ALU.max,
                               inc=S.s_dv2)
                        c = tr.opw("vector", "scalar_tensor_tensor",
                                   b2.Scar[:, :], b2.gath[:, 16:CCW],
                                   prefm[:, r:r + 1], b2.Scar[:, :],
                                   ALU.mult, ALU.add, inc=S.s_dv2)
                        h[("gproc", r)] = c
                    c = tr.op("vector", "nop", inc=S.s_dv2)
                    tr.wait("scalar", S.s_dv2, c)
                    c = tr.op("scalar", "activation", escale[:, :],
                              kmaxloc[:, :], AF.Exp, scale=-1.0, inc=S.s_ac2)
                    tr.wait("vector", S.s_ac2, c)
                    tr.opw("vector", "tensor_scalar", negm[:, :],
                           kmaxloc[:, :], -1.0, None, ALU.mult, inc=S.s_dv2)
                    for hd in range(NH):
                        crs = tr.op("vector", "tensor_scalar",
                                    b2.Scar[:, hd * 128:(hd + 1) * 128],
                                    b2.Scar[:, hd * 128:(hd + 1) * 128],
                                    escale[:, hd:hd + 1], None, ALU.mult,
                                    inc=S.s_dv2 if hd == NH - 1 else None)
                    tr.wait("vector", S.s_dv2, crs)
                    for hd in range(NH):
                        c = tr.op("vector", "tensor_copy",
                                  b2.Scarb[:, hd * HS:(hd + 1) * HS],
                                  b2.Scar[:, hd * 128:hd * 128 + 64],
                                  inc=S.s_dv2 if hd == NH - 1 else None)
                    h[("scb", -1)] = c

                    for n in range(NCH):
                        if n == 0:
                            tr.wait("gpsimd", S.s_dv2,
                                    h[("gproc", NCORES - 1)])
                        if n > 0:
                            tr.wait("gpsimd", S.s_ac2, h[("kexp", n - 1)])
                        c = tr.dma("gpsimd", b2.kcm[:, :], kcm_d[n, :, :],
                                   S.d_x)
                        if n > 0:
                            tr.wait("gpsimd", S.s_dv2, h[("qtb", n - 1)])
                        c = tr.dma("gpsimd", b2.qscm[:, :], qscm_d[n, :, :],
                                   S.d_x)
                        if n > 0:
                            tr.wait("gpsimd", S.s_dv2, h[("sup", n - 1)])
                        c = tr.dma("gpsimd", b2.schk[:, :],
                                   schunks_d[n, :, :], S.d_x)
                        if n > 0:
                            tr.wait("gpsimd", S.s_pe2, h[("omm", n - 1)])
                        c = tr.dma("gpsimd", b2.vchunk[:, :],
                                   vtm_d[n * 128:(n + 1) * 128, :], S.d_x)
                        if n > 0:
                            tr.wait("gpsimd", S.d_st, h[("stc", n - 1)])
                        c = tr.dma("gpsimd", b2.xres1[:, :],
                                   stream[n * 128:(n + 1) * 128, :], S.d_x)
                        h[("ld", n)] = c
                        tr.wait("scalar", S.d_x, h[("ld", n)])
                        if n == 0:
                            tr.wait("scalar", S.s_dv2, h[("scb", -1)])
                        for hd in range(NH):
                            c = tr.op("scalar", "activation",
                                      b2.kexp[:, hd * 128:(hd + 1) * 128],
                                      b2.kcm[:, hd * 128:(hd + 1) * 128],
                                      AF.Exp, bias=negm[:, hd:hd + 1],
                                      inc=S.s_ac2 if hd == NH - 1 else None)
                        h[("kexp", n)] = c
                        tr.wait("vector", S.s_ac2, h[("kexp", n)])
                        ckb = tr.op("vector", "tensor_copy", b2.kexpb[:, :],
                                    b2.kexp[:, :], inc=S.s_dv2)
                        for hd in range(NH):
                            csc = tr.op("vector", "tensor_tensor_scan",
                                        b2.zfull[:, hd * 128:(hd + 1) * 128],
                                        b2.ones64[:, :],
                                        b2.kexp[:, hd * 128:(hd + 1) * 128],
                                        b2.Scar[:, hd * 128 + 64:
                                                hd * 128 + 65],
                                        ALU.mult, ALU.add,
                                        inc=S.s_dv2 if hd == NH - 1 else None)
                        tr.wait("vector", S.s_dv2, csc)
                        tr.opw("vector", "tensor_scalar", b2.ztmp[:, :],
                               b2.zfull[:, :], 1e-9, None, ALU.add,
                               inc=S.s_dv2)
                        tr.opw("vector", "reciprocal", b2.ztmp[:, :],
                               b2.ztmp[:, :], inc=S.s_dv2)
                        tr.wait("vector", S.d_x, h[("ld", n)])
                        cqt = tr.op("vector", "tensor_tensor", b2.qtb[:, :],
                                    b2.qscm[:, :], b2.ztmp[:, :], ALU.mult,
                                    inc=S.s_dv2)
                        h[("qtb", n)] = cqt
                        for hd in range(NH):
                            slot = hd % 2
                            tr.wait("tensor", S.s_dv2, cqt)
                            if hd >= 2:
                                tr.wait("tensor", S.s_dv2,
                                        h[("am", n, hd - 2)])
                            c = tr.op("tensor", "matmul", b2.pA[slot][:, :],
                                      b2.kexpb[:, hd * 128:(hd + 1) * 128],
                                      b2.qtb[:, hd * 128:(hd + 1) * 128],
                                      start=True, stop=True,
                                      skip_group_check=True, inc=S.s_pe2)
                            h[("amm", n, hd)] = c
                            tr.wait("vector", S.s_pe2, h[("amm", n, hd)])
                            c = tr.op("vector", "tensor_tensor",
                                      b2.abf[slot][:, :], b2.pA[slot][:, :],
                                      umaskf[:, :], ALU.mult, inc=S.s_dv2)
                            h[("am", n, hd)] = c
                            tr.wait("tensor", S.s_dv2, h[("am", n, hd)])
                            if n > 0 and hd == 0:
                                tr.wait("tensor", S.s_dv2,
                                        h[("ofm", n - 1, NH - 1)])
                            tr.op("tensor", "matmul",
                                  b2.pO[0:64, hd * 128:(hd + 1) * 128],
                                  b2.Scarb[:, hd * HS:(hd + 1) * HS],
                                  b2.qtb[:, hd * 128:(hd + 1) * 128],
                                  start=(hd % 4 == 0), stop=False,
                                  skip_group_check=True)
                            c = tr.op("tensor", "matmul",
                                      b2.pO[0:64, hd * 128:(hd + 1) * 128],
                                      b2.vchunk[:, hd * HS:(hd + 1) * HS],
                                      b2.abf[slot][:, :],
                                      start=False, stop=(hd % 4 == 3),
                                      skip_group_check=True, inc=S.s_pe2)
                            h[("opair", n, hd)] = c
                            tr.wait("vector", S.s_pe2, h[("opair", n, hd)])
                            sub = hd % 2
                            kd = hd // 2
                            c = tr.op("vector", "tensor_copy",
                                      b2.ofm[sub * 64:(sub + 1) * 64,
                                             kd * 128:(kd + 1) * 128],
                                      b2.pO[0:64, hd * 128:(hd + 1) * 128],
                                      inc=S.s_dv2)
                            h[("ofm", n, hd)] = c
                            tr.op("vector", "scalar_tensor_tensor",
                                  b2.Scar[:, hd * 128:(hd + 1) * 128],
                                  b2.schk[:, hd * 128:(hd + 1) * 128],
                                  escale[:, hd:hd + 1],
                                  b2.Scar[:, hd * 128:(hd + 1) * 128],
                                  ALU.mult, ALU.add)
                        h[("omm", n)] = S.s_pe2.count
                        cupd = tr.op("vector", "nop", inc=S.s_dv2)
                        tr.wait("vector", S.s_dv2, cupd)
                        for hd in range(NH):
                            c = tr.op("vector", "tensor_copy",
                                      b2.Scarb[:, hd * HS:(hd + 1) * HS],
                                      b2.Scar[:, hd * 128:hd * 128 + 64],
                                      inc=S.s_dv2 if hd == NH - 1 else None)
                        h[("sup", n)] = c
                        if n == 0:
                            tr.wait("tensor", S.d_w, h["wo"])
                        for half in range(2):
                            tr.wait("tensor", S.s_dv2, h[("ofm", n, NH - 1)])
                            if n > 0:
                                tr.wait("tensor", S.s_dv2,
                                        h[("xadd", n - 1, half)])
                            for kd in range(NKC):
                                c = tr.op("tensor", "matmul",
                                          b2.pXO[half][:, :],
                                          b2.ofm[:, kd * 128:(kd + 1) * 128],
                                          b2.woutT[:, kd * D + half * 512:
                                                   kd * D + (half + 1) * 512],
                                          start=(kd == 0),
                                          stop=(kd == NKC - 1),
                                          inc=S.s_pe2
                                          if kd == NKC - 1 else None)
                            h[("xomm", n, half)] = S.s_pe2.count
                            tr.wait("vector", S.s_pe2, h[("xomm", n, half)])
                            c = tr.op("vector", "tensor_tensor",
                                      b2.xres1[:, half * 512:
                                               (half + 1) * 512],
                                      b2.xres1[:, half * 512:
                                               (half + 1) * 512],
                                      b2.pXO[half][:, :], ALU.add,
                                      inc=S.s_dv2)
                            h[("xadd", n, half)] = c
                        tr.wait("gpsimd", S.s_dv2, h[("xadd", n, 1)])
                        c = tr.dma("gpsimd", stream[n * 128:(n + 1) * 128, :],
                                   b2.xres1[:, :], S.d_st)
                        h[("stc", n)] = c
                    tr.wait("gpsimd", S.d_st, S.d_st.count)

                run_phase(nc, allsems, emit_p2b)

        if "p3" in phases:
            with ExitStack() as c4:
                f3 = make_ffn_tensors(c4, "f3_")
                run_phase(nc, allsems,
                          lambda tr: emit_ffn(tr, f3, stream, y_out))
        elif "copy" in phases:
            def emit_copy(tr):
                c = tr.dma("gpsimd", y_out[:, :], stream[:, :], S.d_st)
                tr.wait("gpsimd", S.d_st, c)

            run_phase(nc, allsems, emit_copy)

    return nc


def host_consts():
    identf = np.eye(128, dtype=np.float32)
    umask = np.triu(np.ones((128, 128), dtype=np.float32))
    slotb = np.broadcast_to(
        (np.arange(E, dtype=np.float32) * CAP - 1.0)[None, :],
        (128, E)).copy()
    prefm = []
    seqb = []
    for c in range(NCORES):
        pm = np.zeros((64, 8), dtype=np.float32)
        sb_ = np.full((64, 8), -1e38, dtype=np.float32)
        for r in range(NCORES):
            if r // 4 == c // 4:
                sb_[:, r] = 0.0
                if r % 4 < c % 4:
                    pm[:, r] = 1.0
        prefm.append(pm)
        seqb.append(sb_)
    return identf, umask, slotb, prefm, seqb


def kernel(L0_attn_x, L0_attn_W_qkv, L0_attn_W_out, L0_ffn_gate_w,
           L0_ffn_w1, L0_ffn_w3, L0_ffn_w2):
    x = np.ascontiguousarray(np.asarray(L0_attn_x, dtype=np.float32))
    wqkv = np.ascontiguousarray(np.asarray(L0_attn_W_qkv, dtype=np.float32))
    wout = np.ascontiguousarray(np.asarray(L0_attn_W_out, dtype=np.float32))
    gw = np.ascontiguousarray(np.asarray(L0_ffn_gate_w, dtype=np.float32))
    w1 = np.ascontiguousarray(np.asarray(L0_ffn_w1, dtype=np.float32))
    w3 = np.ascontiguousarray(np.asarray(L0_ffn_w3, dtype=np.float32))
    w2 = np.ascontiguousarray(np.asarray(L0_ffn_w2, dtype=np.float32))

    identf, umask, slotb, prefm, seqb = host_consts()
    nc = build_program()
    in_maps = []
    for c in range(NCORES):
        b = c // 4
        t0 = (c % 4) * TPC
        in_maps.append({
            "x": np.ascontiguousarray(x[b, t0:t0 + TPC, :]),
            "wqkv": wqkv, "wout": wout, "gw": gw,
            "w1": w1, "w3": w3, "w2": w2,
            "identf": identf, "umask": umask, "slotb": slotb,
            "prefm": prefm[c], "seqb": seqb[c],
        })
    res = run_bass_kernel_spmd(nc, in_maps, core_ids=list(range(NCORES)))
    out = np.empty((B, T, D), dtype=np.float32)
    for c in range(NCORES):
        b = c // 4
        t0 = (c % 4) * TPC
        out[b, t0:t0 + TPC, :] = np.asarray(res.results[c]["y"],
                                            dtype=np.float32)
    return out


# version 03
# speedup vs baseline: 1.5577x; 1.0263x over previous
import numpy as np
import ml_dtypes
from contextlib import ExitStack

import concourse.bass as bass
import concourse.mybir as mybir
from concourse.bass_utils import run_bass_kernel_spmd

F32 = mybir.dt.float32
BF16 = mybir.dt.bfloat16
AF = mybir.ActivationFunctionType
ALU = mybir.AluOpType
AX = mybir.AxisListType

NCORES = 8
B, T, D = 2, 16384, 1024
NH, HS = 16, 64
E, HID = 8, 2816
TPC = T * B // NCORES
TT = 1024
NTILE = TPC // TT
NTC = TT // 128
NKC = D // 128
NHC = HID // 128
HGROUPS = [(0, 8), (8, 16), (16, 22)]
HBLOCKS = [(0, 4), (4, 8), (8, 12), (12, 16), (16, 20), (20, 22)]
NCH = TPC // 128
LNEPS = 1e-5
CAP = 1280
ZPAD = 2048
SENT = E * CAP
I32 = mybir.dt.int32

ENGS = ("sync", "gpsimd", "vector", "tensor", "scalar")


class Sem:
    __slots__ = ("h", "count")

    def __init__(self, h):
        self.h = h
        self.count = 0


class Tr:

    def __init__(self, active, eng):
        self.a = active
        self.e = eng
        self.hist = {}

    def op(self, name, fn, *args, inc=None, incn=1, **kw):
        if self.a == name:
            inst = getattr(self.e, fn)(*args, **kw)
            if inc is not None:
                inst.then_inc(inc.h, incn)
        if inc is not None:
            inc.count += incn
            return inc.count
        return None

    def wait(self, name, sem, val):
        if val is None or val <= 0:
            return
        if self.a == name:
            self.e.wait_ge(sem.h, val)

    def dma(self, name, out, in_, sem, **kw):
        return self.op(name, "dma_start", inc=sem, incn=16, out=out, in_=in_, **kw)

    def opw(self, name, fn, *args, inc, incn=1, **kw):
        c = self.op(name, fn, *args, inc=inc, incn=incn, **kw)
        self.wait(name, inc, c)
        return c


def run_phase(nc, sems, emit):
    base = [(s, s.count) for s in sems]
    with nc.Block() as block:
        for name in ENGS:
            for s, c in base:
                s.count = c

            def body(eng, _n=name):
                emit(Tr(_n, eng))

            getattr(block, name)(body)


class NS:
    pass


def build_program(phases=("p0", "p1", "p2", "p3")):
    nc = bass.Bass("TRN2", target_bir_lowering=False)
    t = NS()

    x_in = nc.dram_tensor("x", [TPC, D], F32, kind="ExternalInput")
    wqkv_in = nc.dram_tensor("wqkv", [3 * D, D], F32, kind="ExternalInput")
    wout_in = nc.dram_tensor("wout", [D, D], F32, kind="ExternalInput")
    gw_in = nc.dram_tensor("gw", [E, D], F32, kind="ExternalInput")
    w1_in = nc.dram_tensor("w1", [E, HID, D], F32, kind="ExternalInput")
    w3_in = nc.dram_tensor("w3", [E, HID, D], F32, kind="ExternalInput")
    w2_in = nc.dram_tensor("w2", [E, D, HID], F32, kind="ExternalInput")
    identf_in = nc.dram_tensor("identf", [128, 128], F32, kind="ExternalInput")
    umask_in = nc.dram_tensor("umask", [128, 128], F32, kind="ExternalInput")
    prefm_in = nc.dram_tensor("prefm", [64, 8], F32, kind="ExternalInput")
    seqb_in = nc.dram_tensor("seqb", [64, 8], F32, kind="ExternalInput")
    slotb_in = nc.dram_tensor("slotb", [128, 8], F32, kind="ExternalInput")
    y_out = nc.dram_tensor("y", [TPC, D], F32, kind="ExternalOutput")

    w1_bf = nc.dram_tensor("w1_bf", [E, HID, D], BF16, kind="Internal")
    w3_bf = nc.dram_tensor("w3_bf", [E, HID, D], BF16, kind="Internal")
    w2_bf = nc.dram_tensor("w2_bf", [E, D, HID], BF16, kind="Internal")
    wqkv_bf = nc.dram_tensor("wqkv_bf", [3 * D, D], BF16, kind="Internal")
    wout_bf = nc.dram_tensor("wout_bf", [D, D], BF16, kind="Internal")
    stream = nc.dram_tensor("stream", [TPC, D], F32, kind="Internal")
    kcm_d = nc.dram_tensor("kcm_d", [NCH, 64, NH * 128], F32, kind="Internal")
    qscm_d = nc.dram_tensor("qscm_d", [NCH, 64, NH * 128], F32, kind="Internal")
    vtm_d = nc.dram_tensor("vtm_d", [TPC, D], BF16, kind="Internal")
    schunks_d = nc.dram_tensor("schunks_d", [NCH, 64, NH * 128], F32, kind="Internal")
    CCW = 16 + NH * 128
    ccsrc_d = nc.dram_tensor("ccsrc_d", [64, CCW], F32, kind="Internal")
    ccdst_d = nc.dram_tensor("ccdst_d", [64 * NCORES, CCW], F32, kind="Internal")
    xnbf_d = nc.dram_tensor("xnbf_d", [TPC, D], F32, kind="Internal")
    xg_d = nc.dram_tensor("xg_d", [E * CAP + 1, D], F32, kind="Internal")
    y_d = nc.dram_tensor("y_d", [E * CAP + ZPAD, D], F32, kind="Internal")

    with ExitStack() as top:
        def sb(ctx, name, shape, dt=F32):
            return ctx.enter_context(nc.sbuf_tensor(name, shape, dt))

        def ps(ctx, name, shape, dt=F32):
            return ctx.enter_context(nc.psum_tensor(name, shape, dt))

        identf = sb(top, "identf_s", [128, 128])
        umaskf = sb(top, "umaskf_s", [128, 128])
        prefm = sb(top, "prefm_s", [64, 8])
        seqb = sb(top, "seqb_s", [64, 8])
        gwsb = sb(top, "gwsb", [8, D])
        slotbase = sb(top, "slotbase_s", [128, 8])
        gwT = sb(top, "gwT", [128, 8 * E])
        Stot = sb(top, "Stot", [64, NH * 128])
        kmaxloc = sb(top, "kmaxloc", [64, NH])
        escale = sb(top, "escale", [64, NH])
        negm = sb(top, "negm", [64, NH])

        semnames = ["d_cin", "d_wc", "d_w", "d_x", "d_st", "d_sp", "cc_s",
                    "s_tr", "s_tev", "s_amm", "s_silu", "s_hm", "s_bmm",
                    "s_bep", "s_w13c", "s_w2c", "s_gmm", "s_dv", "s_ac",
                    "s_pe2", "s_dv2", "s_ac2", "s_misc"]
        sems = {n: Sem(top.enter_context(nc.semaphore(n))) for n in semnames}
        S = NS()
        for n, s in sems.items():
            setattr(S, n, s)
        allsems = list(sems.values())

        if "p0" in phases:
            with ExitStack() as c0:
                pg0 = ps(c0, "pg0", [128, 16])
                pg1 = ps(c0, "pg1", [128, 16])
                z0 = sb(c0, "z0", [128, D])

                def emit_p0(tr):
                    for ap_out, ap_in in [(identf[:, :], identf_in[:, :]),
                                          (umaskf[:, :], umask_in[:, :]),
                                          (prefm[:, :], prefm_in[:, :]),
                                          (seqb[:, :], seqb_in[:, :]),
                                          (gwsb[:, :], gw_in[:, :]),
                                          (slotbase[:, :], slotb_in[:, :])]:
                        tr.dma("sync", ap_out, ap_in, S.d_cin)
                    cin_done = S.d_cin.count
                    tr.wait("tensor", S.d_cin, cin_done)
                    for k in range(NKC):
                        pg = pg0 if k % 2 == 0 else pg1
                        if k >= 2:
                            tr.wait("tensor", S.s_tev, tr.hist[("gev", k - 2)])
                        c = tr.op("tensor", "transpose", pg[:, 0:8],
                                  gwsb[0:8, k * 128:(k + 1) * 128],
                                  identf[0:8, 0:8], inc=S.s_tr)
                        tr.hist[("gtr", k)] = c
                        tr.wait("vector", S.s_tr, tr.hist[("gtr", k)])
                        c = tr.op("vector", "tensor_copy",
                                  gwT[:, k * 8:(k + 1) * 8], pg[:, 0:8],
                                  inc=S.s_tev)
                        tr.hist[("gev", k)] = c
                    tr.op("vector", "memset", Stot[:, :], 0.0)
                    tr.op("vector", "memset", kmaxloc[:, :], -1e38)
                    cz = tr.op("vector", "memset", z0[:, :], 0.0, inc=S.s_dv)
                    tr.wait("gpsimd", S.s_dv, cz)
                    for i in range(ZPAD // 128):
                        tr.dma("gpsimd",
                               y_d[E * CAP + i * 128:E * CAP + (i + 1) * 128, :],
                               z0[:, :], S.d_st)

                run_phase(nc, allsems, emit_p0)

        def make_ffn_tensors(ctx, pfx):
            f = NS()
            f.xstg = sb(ctx, pfx + "xstg", [128, D])
            f.cst = sb(ctx, pfx + "cst", [128, D])
            f.sqst = sb(ctx, pfx + "sqst", [128, D])
            f.xntm = sb(ctx, pfx + "xntm", [128, D])
            f.xnf = sb(ctx, pfx + "xnf", [128, D])
            f.coef = sb(ctx, pfx + "coef", [128, NCH * E])
            f.maskcm = sb(ctx, pfx + "maskcm", [8, TPC])
            f.zeros8 = sb(ctx, pfx + "zeros8", [8, 1024])
            f.postm = sb(ctx, pfx + "postm", [128, 8])
            f.mtm = sb(ctx, pfx + "mtm", [128, 8])
            f.ovf = sb(ctx, pfx + "ovf", [128, 8])
            f.m2t = sb(ctx, pfx + "m2t", [128, 8])
            f.slf = sb(ctx, pfx + "slf", [128, 8])
            f.sl2 = sb(ctx, pfx + "sl2", [128, 8])
            f.mm1 = sb(ctx, pfx + "mm1", [128, 8])
            f.cm1 = sb(ctx, pfx + "cm1", [128, 8])
            f.o1 = sb(ctx, pfx + "o1", [128, 1])
            f.o2 = sb(ctx, pfx + "o2", [128, 1])
            f.oc = sb(ctx, pfx + "oc", [128, 1])
            f.c12 = sb(ctx, pfx + "c12", [128, 2 * NCH])
            f.sIdx = sb(ctx, pfx + "sIdx", [128, 2 * NCH], I32)
            f.gIdx = sb(ctx, pfx + "gIdx", [128, 2 * NCH], I32)
            f.xgld = sb(ctx, pfx + "xgld", [128, D])
            f.xgfm = sb(ctx, pfx + "xgfm", [128, NKC * CAP], BF16)
            f.hbuf = sb(ctx, pfx + "hbuf", [128, 8 * CAP], BF16)
            f.outacc = sb(ctx, pfx + "outacc", [128, (CAP // 128) * D])
            f.w1b = [sb(ctx, pfx + "w1b%d" % i, [128, NKC * 512], BF16)
                     for i in range(2)]
            f.w3b = [sb(ctx, pfx + "w3b%d" % i, [128, NKC * 512], BF16)
                     for i in range(2)]
            f.w2b = sb(ctx, pfx + "w2b", [128, 8 * D], BF16)
            f.silu = [sb(ctx, pfx + "silu%d" % i, [128, 512]) for i in range(2)]
            f.g1g = sb(ctx, pfx + "g1g", [128, D])
            f.g2g = sb(ctx, pfx + "g2g", [128, D])
            f.ostg = sb(ctx, pfx + "ostg", [128, D])
            f.r1 = sb(ctx, pfx + "r1", [128, 1])
            f.epsb = sb(ctx, pfx + "epsb", [128, 1])
            f.nmean = sb(ctx, pfx + "nmean", [128, 1])
            f.ssq = sb(ctx, pfx + "ssq", [128, 1])
            f.sd = sb(ctx, pfx + "sd", [128, 1])
            f.rstd = sb(ctx, pfx + "rstd", [128, 1])
            f.m1n = sb(ctx, pfx + "m1n", [128, 1])
            f.m2 = sb(ctx, pfx + "m2", [128, 1])
            f.d1 = sb(ctx, pfx + "d1", [128, 1])
            f.r2 = sb(ctx, pfx + "r2", [128, 1])
            f.gst = sb(ctx, pfx + "gst", [128, 8])
            f.ge1 = sb(ctx, pfx + "ge1", [128, 8])
            f.tmp8 = sb(ctx, pfx + "tmp8", [128, 8])
            f.gm = sb(ctx, pfx + "gm", [128, 8])
            f.pH1 = [ps(ctx, pfx + "pH1%d" % i, [128, 512]) for i in range(2)]
            f.pH3 = [ps(ctx, pfx + "pH3%d" % i, [128, 512]) for i in range(2)]
            f.pB = [ps(ctx, pfx + "pB%d" % i, [128, 512]) for i in range(2)]
            f.pT = [ps(ctx, pfx + "pT%d" % i, [128, 128]) for i in range(2)]
            return f

        xcounts = {}

        def emit_ffn(tr, f, src, dst, do_casts=False):
            h = tr.hist
            w13c0 = S.s_w13c.count
            w2c0 = S.s_w2c.count
            if do_casts:
                for e in range(E):
                    tr.dma("gpsimd", w1_bf[e, :, :], w1_in[e, :, :], S.d_wc)
                    tr.dma("gpsimd", w3_bf[e, :, :], w3_in[e, :, :], S.d_wc)
                    c = tr.dma("gpsimd", w2_bf[e, :, :], w2_in[e, :, :], S.d_wc)
                    xcounts[("wc", e)] = c
                xcounts[("wc_qkv",)] = tr.dma("gpsimd", wqkv_bf[:, :],
                                              wqkv_in[:, :], S.d_wc)
                xcounts[("wc_out",)] = tr.dma("gpsimd", wout_bf[:, :],
                                              wout_in[:, :], S.d_wc)
            xgf0 = S.s_misc.count
            tr.op("vector", "memset", f.epsb[:, :], LNEPS, inc=S.s_dv)
            tr.op("vector", "memset", f.zeros8[:, :], 0.0, inc=S.s_dv)
            trk = 0

            for tc in range(NCH):
                if tc > 0:
                    tr.wait("sync", S.s_dv, h[("lnd", tc - 1)])
                c = tr.dma("sync", f.xstg[:, :],
                           src[tc * 128:(tc + 1) * 128, :], S.d_x)
                h[("xld", tc)] = c
                tr.wait("vector", S.d_x, h[("xld", tc)])
                tr.opw("vector", "reduce_sum", f.r1[:, :], f.xstg[:, :],
                       axis=AX.X, inc=S.s_dv)
                tr.opw("vector", "tensor_scalar", f.nmean[:, :], f.r1[:, :],
                       -1.0 / D, None, ALU.mult, inc=S.s_dv)
                c = tr.op("vector", "tensor_scalar", f.cst[:, :], f.xstg[:, :],
                          f.nmean[:, :], None, ALU.add, inc=S.s_dv)
                h[("lnd", tc)] = c
                tr.wait("scalar", S.s_dv, c)
                tr.opw("scalar", "activation", f.sqst[:, :], f.cst[:, :],
                       AF.Square, accum_out=f.ssq[:, :], inc=S.s_ac)
                c = tr.op("scalar", "activation", f.sd[:, :], f.ssq[:, :],
                          AF.Sqrt, scale=1.0 / D, bias=f.epsb[:, :],
                          inc=S.s_ac)
                tr.wait("vector", S.s_ac, c)
                tr.opw("vector", "reciprocal", f.rstd[:, :], f.sd[:, :],
                       inc=S.s_dv)
                if tc > 0:
                    tr.wait("vector", S.s_tr, h[("trl", tc - 1)])
                    tr.wait("vector", S.d_sp, h[("xnst", tc - 1)])
                cxn = tr.op("vector", "tensor_scalar", f.xntm[:, :],
                            f.cst[:, :], f.rstd[:, :], None, ALU.mult,
                            inc=S.s_dv)
                tr.wait("sync", S.s_dv, cxn)
                c = tr.dma("sync", xnbf_d[tc * 128:(tc + 1) * 128, :],
                           f.xntm[:, :], S.d_sp)
                h[("xnst", tc)] = c
                tr.wait("tensor", S.s_dv, cxn)
                for k in range(NKC):
                    slot = trk % 2
                    if trk >= 2:
                        tr.wait("tensor", S.s_tev, h[("tev", trk - 2)])
                    c = tr.op("tensor", "transpose", f.pT[slot][:, :],
                              f.xntm[:, k * 128:(k + 1) * 128],
                              identf[:, :], inc=S.s_tr)
                    h[("tr", trk)] = c
                    tr.wait("vector", S.s_tr, h[("tr", trk)])
                    c = tr.op("vector", "tensor_copy",
                              f.xnf[:, k * 128:(k + 1) * 128],
                              f.pT[slot][:, :], inc=S.s_tev)
                    h[("tev", trk)] = c
                    trk += 1
                tr.wait("tensor", S.s_tev, h[("tev", trk - 1)])
                if tc > 0:
                    tr.wait("tensor", S.s_ac, h[("gex", tc - 1)])
                for k in range(NKC):
                    tr.op("tensor", "matmul", f.pB[0][:, 0:8],
                          f.xnf[:, k * 128:(k + 1) * 128],
                          gwT[:, k * 8:(k + 1) * 8],
                          start=(k == 0), stop=(k == NKC - 1),
                          skip_group_check=True,
                          inc=S.s_gmm if k == NKC - 1 else None)
                h[("g", tc)] = S.s_gmm.count
                tr.wait("vector", S.s_gmm, h[("g", tc)])
                c = tr.op("vector", "reduce_max", f.m1n[:, :],
                          f.pB[0][:, 0:8], axis=AX.X, negate=True, inc=S.s_dv)
                tr.wait("scalar", S.s_dv, c)
                c = tr.op("scalar", "activation", f.gst[:, :],
                          f.pB[0][:, 0:8], AF.Exp, bias=f.m1n[:, :],
                          inc=S.s_ac)
                h[("gex", tc)] = c
                tr.wait("vector", S.s_ac, c)
                tr.opw("vector", "tensor_single_scalar", f.ge1[:, :],
                       f.gst[:, :], 1.0, ALU.is_ge, inc=S.s_dv)
                tr.opw("vector", "scalar_tensor_tensor", f.tmp8[:, :],
                       f.ge1[:, :], -1e30, f.gst[:, :], ALU.mult, ALU.add,
                       inc=S.s_dv)
                tr.opw("vector", "reduce_max", f.m2[:, :], f.tmp8[:, :],
                       axis=AX.X, inc=S.s_dv)
                tr.opw("vector", "tensor_single_scalar", f.tmp8[:, :],
                       f.gst[:, :], f.m2[:, :], ALU.is_ge, inc=S.s_dv)
                tr.opw("vector", "tensor_tensor", f.gm[:, :], f.gst[:, :],
                       f.tmp8[:, :], ALU.mult, inc=S.s_dv)
                tr.opw("vector", "tensor_scalar", f.d1[:, :], f.m2[:, :],
                       1.0, None, ALU.add, inc=S.s_dv)
                tr.opw("vector", "reciprocal", f.r2[:, :], f.d1[:, :],
                       inc=S.s_dv)
                tr.opw("vector", "tensor_scalar",
                       f.coef[:, tc * E:(tc + 1) * E], f.gm[:, :],
                       f.r2[:, :], 0.5, ALU.mult, ALU.mult, inc=S.s_dv)
                cmk = tr.opw("vector", "tensor_single_scalar", f.gm[:, :],
                             f.coef[:, tc * E:(tc + 1) * E], 0.0, ALU.is_gt,
                             inc=S.s_dv)
                slot = trk % 2
                tr.wait("tensor", S.s_dv, cmk)
                if trk >= 2:
                    tr.wait("tensor", S.s_tev, h[("tev", trk - 2)])
                c = tr.op("tensor", "transpose", f.pT[slot][0:8, 0:128],
                          f.gm[:, :], identf[:, :], inc=S.s_tr)
                h[("tr", trk)] = c
                h[("trl", tc)] = c
                tr.wait("vector", S.s_tr, h[("tr", trk)])
                c = tr.op("vector", "tensor_copy",
                          f.maskcm[:, tc * 128:(tc + 1) * 128],
                          f.pT[slot][0:8, 0:128], inc=S.s_tev)
                h[("tev", trk)] = c
                trk += 1
            h[("g1_done",)] = S.s_dv.count

            tr.wait("vector", S.s_tev, h[("tev", trk - 1)])
            for b in range(4):
                init = (f.zeros8[:, 0:1] if b == 0
                        else f.maskcm[:, b * 1024 - 1:b * 1024])
                tr.opw("vector", "tensor_tensor_scan",
                       f.maskcm[:, b * 1024:(b + 1) * 1024],
                       f.zeros8[:, :],
                       f.maskcm[:, b * 1024:(b + 1) * 1024],
                       init, ALU.add, ALU.add, inc=S.s_dv)
            for tc in range(NCH):
                slot = trk % 2
                if trk >= 2:
                    tr.wait("tensor", S.s_tev, h[("tev", trk - 2)])
                tr.wait("tensor", S.s_dv, S.s_dv.count)
                c = tr.op("tensor", "transpose", f.pT[slot][:, 0:8],
                          f.maskcm[:, tc * 128:(tc + 1) * 128],
                          identf[0:8, 0:8], inc=S.s_tr)
                h[("tr", trk)] = c
                tr.wait("vector", S.s_tr, h[("tr", trk)])
                c = tr.op("vector", "tensor_copy", f.postm[:, :],
                          f.pT[slot][:, 0:8], inc=S.s_tev)
                h[("tev", trk)] = c
                trk += 1
                tr.wait("vector", S.s_tev, h[("tev", trk - 1)])
                tr.opw("vector", "tensor_single_scalar", f.mtm[:, :],
                       f.coef[:, tc * E:(tc + 1) * E], 0.0, ALU.is_gt,
                       inc=S.s_dv)
                tr.opw("vector", "tensor_single_scalar", f.ovf[:, :],
                       f.postm[:, :], float(CAP), ALU.is_le, inc=S.s_dv)
                tr.opw("vector", "tensor_tensor", f.m2t[:, :], f.mtm[:, :],
                       f.ovf[:, :], ALU.mult, inc=S.s_dv)
                tr.opw("vector", "tensor_tensor", f.slf[:, :], f.postm[:, :],
                       slotbase[:, :], ALU.add, inc=S.s_dv)
                tr.opw("vector", "tensor_tensor", f.slf[:, :], f.slf[:, :],
                       f.m2t[:, :], ALU.mult, inc=S.s_dv)
                tr.opw("vector", "tensor_scalar", f.sl2[:, :], f.m2t[:, :],
                       -float(SENT), float(SENT), ALU.mult, ALU.add,
                       inc=S.s_dv)
                tr.opw("vector", "tensor_tensor", f.slf[:, :], f.slf[:, :],
                       f.sl2[:, :], ALU.add, inc=S.s_dv)
                tr.opw("vector", "tensor_reduce", f.o1[:, :], f.slf[:, :],
                       axis=AX.X, op=ALU.min, inc=S.s_dv)
                tr.opw("vector", "tensor_single_scalar", f.mm1[:, :],
                       f.slf[:, :], f.o1[:, :], ALU.is_equal, inc=S.s_dv)
                tr.opw("vector", "scalar_tensor_tensor", f.sl2[:, :],
                       f.mm1[:, :], float(SENT), f.slf[:, :], ALU.mult,
                       ALU.add, inc=S.s_dv)
                tr.opw("vector", "tensor_reduce", f.o2[:, :], f.sl2[:, :],
                       axis=AX.X, op=ALU.min, inc=S.s_dv)
                tr.opw("vector", "tensor_tensor", f.cm1[:, :], f.mm1[:, :],
                       f.coef[:, tc * E:(tc + 1) * E], ALU.mult, inc=S.s_dv)
                tr.opw("vector", "reduce_sum", f.c12[:, 2 * tc:2 * tc + 1],
                       f.cm1[:, :], axis=AX.X, inc=S.s_dv)
                tr.opw("vector", "tensor_single_scalar", f.mm1[:, :],
                       f.sl2[:, :], f.o2[:, :], ALU.is_equal, inc=S.s_dv)
                tr.opw("vector", "tensor_tensor", f.cm1[:, :], f.mm1[:, :],
                       f.coef[:, tc * E:(tc + 1) * E], ALU.mult, inc=S.s_dv)
                tr.opw("vector", "reduce_sum", f.c12[:, 2 * tc + 1:2 * tc + 2],
                       f.cm1[:, :], axis=AX.X, inc=S.s_dv)
                tr.opw("vector", "tensor_copy", f.sIdx[:, 2 * tc:2 * tc + 1],
                       f.o1[:, :], inc=S.s_dv)
                tr.opw("vector", "tensor_copy",
                       f.sIdx[:, 2 * tc + 1:2 * tc + 2], f.o2[:, :],
                       inc=S.s_dv)
                tr.opw("vector", "tensor_single_scalar", f.oc[:, :],
                       f.o1[:, :], float(E * CAP), ALU.min, inc=S.s_dv)
                tr.opw("vector", "tensor_copy", f.gIdx[:, 2 * tc:2 * tc + 1],
                       f.oc[:, :], inc=S.s_dv)
                tr.opw("vector", "tensor_single_scalar", f.oc[:, :],
                       f.o2[:, :], float(E * CAP), ALU.min, inc=S.s_dv)
                c = tr.opw("vector", "tensor_copy",
                           f.gIdx[:, 2 * tc + 1:2 * tc + 2], f.oc[:, :],
                           inc=S.s_dv)
                h[("idx", tc)] = c
            h[("g2_done",)] = S.s_dv.count

            tr.wait("gpsimd", S.s_dv, h[("g2_done",)])
            tr.wait("gpsimd", S.d_sp, h[("xnst", NCH - 1)])
            for tc in range(NCH):
                if tc > 0:
                    tr.wait("gpsimd", S.d_sp, h[("sc", tc - 1)])
                c = tr.dma("gpsimd", f.xgld[:, :],
                           xnbf_d[tc * 128:(tc + 1) * 128, :], S.d_x)
                tr.wait("gpsimd", S.d_x, c)
                for kk in range(2):
                    c = tr.op("gpsimd", "indirect_dma_start",
                              out=xg_d[:, :],
                              out_offset=bass.IndirectOffsetOnAxis(
                                  f.sIdx[:, 2 * tc + kk:2 * tc + kk + 1], 0),
                              in_=f.xgld[:, :], in_offset=None,
                              inc=S.d_sp, incn=16)
                h[("sc", tc)] = c
            h[("scat_done",)] = c
            tr.wait("gpsimd", S.d_sp, h[("scat_done",)])

            astep = 0
            bstep = 0
            blkidx = 0
            grpidx = 0
            NTE = CAP // 128
            ASL = [(0, 512), (512, 1024), (1024, CAP)]
            for e in range(E):
                for te in range(NTE):
                    if (e, te) != (0, 0):
                        tr.wait("gpsimd", S.s_tr, h[("xgtr_prev",)])
                    c = tr.dma("gpsimd", f.xgld[:, :],
                               xg_d[e * CAP + te * 128:
                                    e * CAP + (te + 1) * 128, :], S.d_x)
                    h[("xg", e, te)] = c
                    tr.wait("tensor", S.d_x, h[("xg", e, te)])
                    if e > 0 and te == 0:
                        tr.wait("tensor", S.s_amm, h[("aend", e - 1)])
                    for k in range(NKC):
                        slot = trk % 2
                        if trk >= 2:
                            tr.wait("tensor", S.s_tev, h[("tev", trk - 2)])
                        c = tr.op("tensor", "transpose", f.pT[slot][:, :],
                                  f.xgld[:, k * 128:(k + 1) * 128],
                                  identf[:, :], inc=S.s_tr)
                        h[("tr", trk)] = c
                        tr.wait("vector", S.s_tr, h[("tr", trk)])
                        c = tr.op("vector", "tensor_copy",
                                  f.xgfm[:, k * CAP + te * 128:
                                         k * CAP + (te + 1) * 128],
                                  f.pT[slot][:, :], inc=S.s_tev)
                        h[("tev", trk)] = c
                        trk += 1
                    h[("xgtr_prev",)] = S.s_tr.count
                tr.op("vector", "nop", inc=S.s_misc)
                h[("xgfm", e)] = S.s_misc.count
                for gi, (g0, g1) in enumerate(HGROUPS):
                    for bl in (2 * gi, 2 * gi + 1):
                        hb0, hb1 = HBLOCKS[bl]
                        nh2 = hb1 - hb0
                        if gi == 0 and bl == 0:
                            tr.wait("sync", S.d_wc, xcounts.get(("wc", e)))
                        if blkidx >= 2:
                            tr.wait("sync", S.s_w13c, w13c0 + blkidx - 1)
                        for k in range(NKC):
                            tr.dma("sync",
                                   f.w1b[blkidx % 2][:, k * 512:
                                                     k * 512 + nh2 * 128],
                                   w1_bf[e, hb0 * 128:hb1 * 128,
                                         k * 128:(k + 1) * 128],
                                   S.d_w, transpose=True)
                            c = tr.dma("sync",
                                       f.w3b[blkidx % 2][:, k * 512:
                                                         k * 512 + nh2 * 128],
                                       w3_bf[e, hb0 * 128:hb1 * 128,
                                             k * 128:(k + 1) * 128],
                                       S.d_w, transpose=True)
                        h[("wblk", blkidx)] = c
                        blkidx += 1
                    if grpidx >= 1:
                        tr.wait("sync", S.s_w2c, w2c0 + grpidx)
                    for hl, hh in enumerate(range(g0, g1)):
                        c = tr.dma("sync", f.w2b[:, hl * D:(hl + 1) * D],
                                   w2_bf[e, 0:D, hh * 128:(hh + 1) * 128],
                                   S.d_w, transpose=True)
                    h[("w2g", grpidx)] = c
                    grpidx += 1

                    for hh in range(g0, g1):
                        bl = hh // 4
                        gblk = e * 6 + bl
                        par = gblk % 2
                        hloc = hh - HBLOCKS[bl][0]
                        for n, (n0, n1) in enumerate(ASL):
                            p = astep % 2
                            tr.wait("tensor", S.d_w, h[("wblk", gblk)])
                            if astep == 0:
                                tr.wait("tensor", S.s_misc, h[("xgfm", e)])
                            if astep >= 2:
                                tr.wait("tensor", S.s_silu,
                                        h[("silu", astep - 2)])
                                tr.wait("tensor", S.s_hm, h[("hm", astep - 2)])
                            for k in range(NKC):
                                tr.op("tensor", "matmul",
                                      f.pH1[p][:, 0:n1 - n0],
                                      f.w1b[par][:, k * 512 + hloc * 128:
                                                 k * 512 + (hloc + 1) * 128],
                                      f.xgfm[:, k * CAP + n0:k * CAP + n1],
                                      start=(k == 0), stop=(k == NKC - 1))
                            for k in range(NKC):
                                tr.op("tensor", "matmul",
                                      f.pH3[p][:, 0:n1 - n0],
                                      f.w3b[par][:, k * 512 + hloc * 128:
                                                 k * 512 + (hloc + 1) * 128],
                                      f.xgfm[:, k * CAP + n0:k * CAP + n1],
                                      start=(k == 0), stop=(k == NKC - 1),
                                      inc=S.s_amm if k == NKC - 1 else None)
                            h[("amm", astep)] = S.s_amm.count
                            tr.wait("scalar", S.s_amm, h[("amm", astep)])
                            if astep >= 2:
                                tr.wait("scalar", S.s_hm, h[("hm", astep - 2)])
                            c = tr.op("scalar", "activation",
                                      f.silu[p][:, 0:n1 - n0],
                                      f.pH1[p][:, 0:n1 - n0], AF.Silu,
                                      inc=S.s_silu)
                            h[("silu", astep)] = c
                            tr.wait("vector", S.s_silu, h[("silu", astep)])
                            c = tr.op("vector", "tensor_tensor",
                                      f.hbuf[:, (hh - g0) * CAP + n0:
                                             (hh - g0) * CAP + n1],
                                      f.silu[p][:, 0:n1 - n0],
                                      f.pH3[p][:, 0:n1 - n0],
                                      ALU.mult, inc=S.s_hm)
                            h[("hm", astep)] = c
                            astep += 1
                        if hh == HBLOCKS[bl][1] - 1:
                            tr.op("tensor", "nop", inc=S.s_w13c)
                    nhg = g1 - g0
                    first_b = True
                    for te in range(NTE):
                        for half in range(2):
                            p = bstep % 2
                            if first_b:
                                tr.wait("tensor", S.s_hm, h[("hm", astep - 1)])
                                tr.wait("tensor", S.d_w,
                                        h[("w2g", grpidx - 1)])
                                first_b = False
                            if bstep >= 2:
                                tr.wait("tensor", S.s_bep,
                                        h[("bep", bstep - 2)])
                            for hl in range(nhg):
                                tr.op("tensor", "matmul", f.pB[p][:, :],
                                      f.hbuf[:, hl * CAP + te * 128:
                                             hl * CAP + (te + 1) * 128],
                                      f.w2b[:, hl * D + half * 512:
                                            hl * D + (half + 1) * 512],
                                      start=(hl == 0), stop=(hl == nhg - 1),
                                      skip_group_check=True,
                                      inc=S.s_bmm if hl == nhg - 1 else None)
                            h[("bmm", bstep)] = S.s_bmm.count
                            tr.wait("vector", S.s_bmm, h[("bmm", bstep)])
                            if e > 0 and gi == 0 and te == 0 and half == 0:
                                tr.wait("vector", S.d_st, h[("yst", e - 1)])
                            oslc = f.outacc[:, te * D + half * 512:
                                            te * D + (half + 1) * 512]
                            if gi == 0:
                                c = tr.op("vector", "tensor_copy", oslc,
                                          f.pB[p][:, :], inc=S.s_bep)
                            else:
                                c = tr.op("vector", "tensor_tensor", oslc,
                                          f.pB[p][:, :], oslc, ALU.add,
                                          inc=S.s_bep)
                            h[("bep", bstep)] = c
                            bstep += 1
                    tr.op("tensor", "nop", inc=S.s_w2c)
                h[("aend", e)] = S.s_amm.count
                for te in range(NTE):
                    tr.wait("gpsimd", S.s_bep, h[("bep", bstep - 1)])
                    c = tr.dma("gpsimd",
                               y_d[e * CAP + te * 128:
                                   e * CAP + (te + 1) * 128, :],
                               f.outacc[:, te * D:(te + 1) * D], S.d_st)
                h[("yst", e)] = c

            tr.wait("gpsimd", S.d_st, h[("yst", E - 1)])
            for tc in range(NCH):
                if tc > 0:
                    tr.wait("gpsimd", S.s_dv, h[("res", tc - 1)])
                c = tr.op("gpsimd", "indirect_dma_start",
                          out=f.g1g[:, :], out_offset=None,
                          in_=y_d[:, :],
                          in_offset=bass.IndirectOffsetOnAxis(
                              f.gIdx[:, 2 * tc:2 * tc + 1], 0),
                          inc=S.d_x, incn=16)
                c = tr.op("gpsimd", "indirect_dma_start",
                          out=f.g2g[:, :], out_offset=None,
                          in_=y_d[:, :],
                          in_offset=bass.IndirectOffsetOnAxis(
                              f.gIdx[:, 2 * tc + 1:2 * tc + 2], 0),
                          inc=S.d_x, incn=16)
                c = tr.dma("sync", f.xstg[:, :],
                           src[tc * 128:(tc + 1) * 128, :], S.d_x)
                h[("rld", tc)] = c
                tr.wait("vector", S.d_x, h[("rld", tc)])
                tr.opw("vector", "scalar_tensor_tensor", f.ostg[:, :],
                       f.g1g[:, :], f.c12[:, 2 * tc:2 * tc + 1],
                       f.xstg[:, :], ALU.mult, ALU.add, inc=S.s_dv)
                c = tr.op("vector", "scalar_tensor_tensor", f.ostg[:, :],
                          f.g2g[:, :], f.c12[:, 2 * tc + 1:2 * tc + 2],
                          f.ostg[:, :], ALU.mult, ALU.add, inc=S.s_dv)
                h[("res", tc)] = c
                tr.wait("sync", S.s_dv, h[("res", tc)])
                c = tr.dma("sync", dst[tc * 128:(tc + 1) * 128, :],
                           f.ostg[:, :], S.d_st)
            tr.wait("gpsimd", S.d_st, S.d_st.count)
            tr.wait("sync", S.d_st, S.d_st.count)


        if "p1" in phases:
            with ExitStack() as c1:
                f1 = make_ffn_tensors(c1, "f1_")
                run_phase(nc, allsems,
                          lambda tr: emit_ffn(tr, f1, x_in, stream,
                                              do_casts=True))

        if "p2" in phases:
            with ExitStack() as c2:
                a = NS()
                a.wqkvT = sb(c2, "a_wqkvT", [128, NKC * 3 * D], BF16)
                a.xres1 = sb(c2, "a_xres1", [128, D])
                a.cst = sb(c2, "a_cst", [128, D])
                a.sqst = sb(c2, "a_sqst", [128, D])
                a.xntm = sb(c2, "a_xntm", [128, D])
                a.xnbf1 = sb(c2, "a_xnbf1", [128, D], BF16)
                a.qstage = sb(c2, "a_qstage", [128, D])
                a.qsm = sb(c2, "a_qsm", [128, D])
                a.kstage = sb(c2, "a_kstage", [128, D])
                a.kexp = sb(c2, "a_kexp", [128, D])
                a.kexpb = sb(c2, "a_kexpb", [128, D], BF16)
                a.vbf = sb(c2, "a_vbf", [128, D], BF16)
                a.kcm = sb(c2, "a_kcm", [64, NH * 128])
                a.qscm = sb(c2, "a_qscm", [64, NH * 128])
                a.schst = [sb(c2, "a_schst%d" % i, [64, NH * 128])
                           for i in range(2)]
                a.kmaxc = sb(c2, "a_kmaxc", [64, NH])
                a.onesb = sb(c2, "a_onesb", [128, 1], BF16)
                a.r1 = sb(c2, "a_r1", [128, 1])
                a.nmean = sb(c2, "a_nmean", [128, 1])
                a.ssq = sb(c2, "a_ssq", [128, 1])
                a.sd = sb(c2, "a_sd", [128, 1])
                a.rstd = sb(c2, "a_rstd", [128, 1])
                a.qm = sb(c2, "a_qm", [128, 1])
                a.epsb = sb(c2, "a_epsb", [128, 1])
                a.qsum = sb(c2, "a_qsum", [128, 1])
                a.qr = sb(c2, "a_qr", [128, 1])
                a.pQ = [ps(c2, "a_pQ%d" % i, [128, 512]) for i in range(2)]
                a.pT = [ps(c2, "a_pT%d" % i, [128, 128]) for i in range(2)]
                a.pS = ps(c2, "a_pS", [64, NH * 128])

                def emit_p2a(tr):
                    h = tr.hist
                    tr.wait("sync", S.d_wc, xcounts.get(("wc_qkv",)))
                    for k in range(NKC):
                        for s2 in range(2):
                            c = tr.dma("sync",
                                       a.wqkvT[:, k * 3 * D + s2 * 1536:
                                               k * 3 * D + (s2 + 1) * 1536],
                                       wqkv_bf[s2 * 1536:(s2 + 1) * 1536,
                                               k * 128:(k + 1) * 128],
                                       S.d_w, transpose=True)
                    h["wq"] = c
                    tr.op("vector", "memset", a.onesb[:, :], 1.0)
                    tr.op("vector", "memset", a.epsb[:, :], LNEPS)
                    c0 = tr.op("vector", "memset", a.pS[:, :], 0.0,
                               inc=S.s_dv2)
                    trk = 0
                    for n in range(NCH):
                        if n > 0:
                            tr.wait("gpsimd", S.s_dv2, h[("xdone", n - 1)])
                        c = tr.dma("gpsimd", a.xres1[:, :],
                                   stream[n * 128:(n + 1) * 128, :], S.d_x)
                        h[("xld", n)] = c
                        tr.wait("vector", S.d_x, h[("xld", n)])
                        if n > 0:
                            tr.wait("vector", S.s_pe2, h[("smm", n - 1)])
                            tr.wait("vector", S.d_sp, h[("spst", n - 1)])
                            tr.wait("vector", S.s_tr, h[("trlast", n - 1)])
                        tr.opw("vector", "reduce_sum", a.r1[:, :],
                               a.xres1[:, :], axis=AX.X, inc=S.s_dv2)
                        tr.opw("vector", "tensor_scalar", a.nmean[:, :],
                               a.r1[:, :], -1.0 / D, None, ALU.mult,
                               inc=S.s_dv2)
                        c = tr.op("vector", "tensor_scalar", a.cst[:, :],
                                  a.xres1[:, :], a.nmean[:, :], None, ALU.add,
                                  inc=S.s_dv2)
                        h[("xdone", n)] = c
                        tr.wait("scalar", S.s_dv2, c)
                        tr.opw("scalar", "activation", a.sqst[:, :],
                               a.cst[:, :], AF.Square, accum_out=a.ssq[:, :],
                               inc=S.s_ac2)
                        c = tr.op("scalar", "activation", a.sd[:, :],
                                  a.ssq[:, :], AF.Sqrt, scale=1.0 / D,
                                  bias=a.epsb[:, :], inc=S.s_ac2)
                        tr.wait("vector", S.s_ac2, c)
                        tr.opw("vector", "reciprocal", a.rstd[:, :],
                               a.sd[:, :], inc=S.s_dv2)
                        cxn = tr.op("vector", "tensor_scalar", a.xntm[:, :],
                                    a.cst[:, :], a.rstd[:, :], None, ALU.mult,
                                    inc=S.s_dv2)
                        tr.wait("tensor", S.s_dv2, cxn)
                        for k in range(NKC):
                            slot = trk % 2
                            if trk >= 2:
                                tr.wait("tensor", S.s_tev, h[("tev", trk - 2)])
                            c = tr.op("tensor", "transpose", a.pT[slot][:, :],
                                      a.xntm[:, k * 128:(k + 1) * 128],
                                      identf[:, :], inc=S.s_tr)
                            h[("tr", trk)] = c
                            tr.wait("vector", S.s_tr, h[("tr", trk)])
                            c = tr.op("vector", "tensor_copy",
                                      a.xnbf1[:, k * 128:(k + 1) * 128],
                                      a.pT[slot][:, :], inc=S.s_tev)
                            h[("tev", trk)] = c
                            trk += 1
                        tr.wait("tensor", S.s_tev, h[("tev", trk - 1)])
                        if n == 0:
                            tr.wait("tensor", S.d_w, h["wq"])
                        for qs in range(6):
                            p = qs % 2
                            if n > 0 or qs >= 2:
                                tr.wait("tensor", S.s_dv2,
                                        h[("qev", n if qs >= 2 else n - 1,
                                           qs - 2 if qs >= 2 else qs + 4)])
                            for k in range(NKC):
                                c = tr.op("tensor", "matmul", a.pQ[p][:, :],
                                          a.xnbf1[:, k * 128:(k + 1) * 128],
                                          a.wqkvT[:, k * 3 * D + qs * 512:
                                                  k * 3 * D + (qs + 1) * 512],
                                          start=(k == 0), stop=(k == NKC - 1),
                                          inc=S.s_pe2
                                          if k == NKC - 1 else None)
                            h[("qkv", n, qs)] = S.s_pe2.count
                            tr.wait("vector", S.s_pe2, h[("qkv", n, qs)])
                            if qs < 2:
                                dstap = a.qstage[:, qs * 512:(qs + 1) * 512]
                            elif qs < 4:
                                dstap = a.kstage[:, (qs - 2) * 512:
                                                 (qs - 1) * 512]
                            else:
                                dstap = a.vbf[:, (qs - 4) * 512:(qs - 3) * 512]
                            c = tr.op("vector", "tensor_copy", dstap,
                                      a.pQ[p][:, :], inc=S.s_dv2)
                            h[("qev", n, qs)] = c
                        for hd in range(NH):
                            sl = a.qstage[:, hd * HS:(hd + 1) * HS]
                            c = tr.op("vector", "reduce_max", a.qm[:, :], sl,
                                      axis=AX.X, negate=True, inc=S.s_dv2)
                            tr.wait("scalar", S.s_dv2, c)
                            c = tr.op("scalar", "activation",
                                      a.qsm[:, hd * HS:(hd + 1) * HS], sl,
                                      AF.Exp, bias=a.qm[:, :],
                                      accum_out=a.qsum[:, :], inc=S.s_ac2)
                            tr.wait("vector", S.s_ac2, c)
                            tr.opw("vector", "reciprocal", a.qr[:, :],
                                   a.qsum[:, :], inc=S.s_dv2)
                            tr.opw("vector", "tensor_scalar",
                                   a.qsm[:, hd * HS:(hd + 1) * HS],
                                   a.qsm[:, hd * HS:(hd + 1) * HS],
                                   a.qr[:, :], 0.125, ALU.mult, ALU.mult,
                                   inc=S.s_dv2)
                        c = tr.op("vector", "nop", inc=S.s_dv2)
                        tr.wait("scalar", S.s_dv2, c)
                        c = tr.op("scalar", "activation", a.kexp[:, :],
                                  a.kstage[:, :], AF.Exp, inc=S.s_ac2)
                        tr.wait("vector", S.s_ac2, c)
                        cvb = tr.op("vector", "tensor_copy", a.kexpb[:, :],
                                    a.kexp[:, :], inc=S.s_dv2)
                        tr.wait("tensor", S.s_dv2, cvb)
                        if n > 0:
                            tr.wait("tensor", S.s_dv2, h[("sdone", n - 1)])
                        else:
                            tr.wait("tensor", S.s_dv2, c0)
                        for hd in range(NH):
                            tr.op("tensor", "matmul",
                                  a.pS[0:64, hd * 128:hd * 128 + 64],
                                  a.kexpb[:, hd * HS:(hd + 1) * HS],
                                  a.vbf[:, hd * HS:(hd + 1) * HS],
                                  start=(hd % 4 == 0), stop=False,
                                  skip_group_check=True)
                            c = tr.op("tensor", "matmul",
                                      a.pS[0:64, hd * 128 + 64:hd * 128 + 65],
                                      a.kexpb[:, hd * HS:(hd + 1) * HS],
                                      a.onesb[:, :], start=False,
                                      stop=(hd % 4 == 3),
                                      skip_group_check=True,
                                      inc=S.s_pe2 if hd == NH - 1 else None)
                        h[("smm", n)] = S.s_pe2.count
                        for hd in range(NH):
                            for which, srcap, dstt in (
                                    (0, a.kstage[:, hd * HS:(hd + 1) * HS],
                                     a.kcm),
                                    (1, a.qsm[:, hd * HS:(hd + 1) * HS],
                                     a.qscm)):
                                slot = trk % 2
                                if trk >= 2:
                                    tr.wait("tensor", S.s_tev,
                                            h[("tev", trk - 2)])
                                c = tr.op("tensor", "transpose",
                                          a.pT[slot][0:64, :], srcap,
                                          identf[:, :], inc=S.s_tr)
                                h[("tr", trk)] = c
                                tr.wait("vector", S.s_tr, h[("tr", trk)])
                                c = tr.op("vector", "tensor_copy",
                                          dstt[:, hd * 128:(hd + 1) * 128],
                                          a.pT[slot][0:64, :], inc=S.s_tev)
                                h[("tev", trk)] = c
                                trk += 1
                        cktr = S.s_tev.count
                        h[("trlast", n)] = S.s_tr.count
                        for hd in range(NH):
                            cr = tr.op("vector", "reduce_max",
                                       a.kmaxc[:, hd:hd + 1],
                                       a.kcm[:, hd * 128:(hd + 1) * 128],
                                       axis=AX.X,
                                       inc=S.s_dv2 if hd == NH - 1 else None)
                        tr.wait("vector", S.s_dv2, cr)
                        tr.opw("vector", "tensor_tensor", kmaxloc[:, :],
                               kmaxloc[:, :], a.kmaxc[:, :], ALU.max,
                               inc=S.s_dv2)
                        p2 = n % 2
                        tr.wait("vector", S.s_pe2, h[("smm", n)])
                        if n >= 2:
                            tr.wait("vector", S.d_sp, h[("spst", n - 2)])
                        tr.opw("vector", "tensor_copy", a.schst[p2][:, :],
                               a.pS[:, :], inc=S.s_dv2)
                        c = tr.op("vector", "tensor_tensor", Stot[:, :],
                                  Stot[:, :], a.schst[p2][:, :], ALU.add,
                                  inc=S.s_dv2)
                        h[("sdone", n)] = c
                        tr.wait("gpsimd", S.s_tev, cktr)
                        tr.dma("gpsimd", kcm_d[n, :, :], a.kcm[:, :], S.d_sp)
                        tr.dma("gpsimd", qscm_d[n, :, :], a.qscm[:, :], S.d_sp)
                        tr.wait("gpsimd", S.s_dv2, h[("qev", n, 5)])
                        tr.dma("gpsimd", vtm_d[n * 128:(n + 1) * 128, :],
                               a.vbf[:, :], S.d_sp)
                        tr.wait("gpsimd", S.s_dv2, h[("sdone", n)])
                        c = tr.dma("gpsimd", schunks_d[n, :, :],
                                   a.schst[p2][:, :], S.d_sp)
                        h[("spst", n)] = c
                    tr.wait("gpsimd", S.d_sp, S.d_sp.count)
                    cl = tr.op("vector", "nop", inc=S.s_dv2)
                    tr.wait("gpsimd", S.s_dv2, cl)
                    tr.dma("gpsimd", ccsrc_d[:, 0:16], kmaxloc[:, :], S.d_sp)
                    c = tr.dma("gpsimd", ccsrc_d[:, 16:CCW], Stot[:, :],
                               S.d_sp)
                    tr.wait("gpsimd", S.d_sp, c)
                    tr.op("gpsimd", "collective_compute", "AllGather",
                          ALU.bypass,
                          replica_groups=[list(range(NCORES))],
                          ins=[ccsrc_d.ap().opt()], outs=[ccdst_d.ap().opt()],
                          inc=S.cc_s)
                    tr.wait("gpsimd", S.cc_s, 1)

                run_phase(nc, allsems, emit_p2a)

            with ExitStack() as c3:
                b2 = NS()
                b2.woutT = sb(c3, "b_woutT", [128, NKC * D], BF16)
                b2.gath = sb(c3, "b_gath", [64, CCW])
                b2.Scar = sb(c3, "b_Scar", [64, NH * 128])
                b2.Scarb = sb(c3, "b_Scarb", [64, NH * HS], BF16)
                b2.kcm = sb(c3, "b_kcm", [64, NH * 128])
                b2.kexp = sb(c3, "b_kexp", [64, NH * 128])
                b2.kexpb = sb(c3, "b_kexpb", [64, NH * 128], BF16)
                b2.qscm = sb(c3, "b_qscm", [64, NH * 128])
                b2.qtb = sb(c3, "b_qtb", [64, NH * 128], BF16)
                b2.zfull = sb(c3, "b_zfull", [64, NH * 128])
                b2.ztmp = sb(c3, "b_ztmp", [64, NH * 128])
                b2.schk = sb(c3, "b_schk", [64, NH * 128])
                b2.vchunk = sb(c3, "b_vchunk", [128, D], BF16)
                b2.abf = [sb(c3, "b_abf%d" % i, [128, 128], BF16)
                          for i in range(2)]
                b2.ofm = sb(c3, "b_ofm", [128, D], BF16)
                b2.xres1 = sb(c3, "b_xres1", [128, D])
                b2.ones64 = sb(c3, "b_ones64", [64, 128])
                b2.pA = [ps(c3, "b_pA%d" % i, [128, 128]) for i in range(2)]
                b2.pO = ps(c3, "b_pO", [64, NH * 128])
                b2.pXO = [ps(c3, "b_pXO%d" % i, [128, 512]) for i in range(2)]

                def emit_p2b(tr):
                    h = tr.hist
                    tr.wait("sync", S.d_wc, xcounts.get(("wc_out",)))
                    for k in range(NKC):
                        c = tr.dma("sync", b2.woutT[:, k * D:(k + 1) * D],
                                   wout_bf[0:D, k * 128:(k + 1) * 128],
                                   S.d_w, transpose=True)
                    h["wo"] = c
                    tr.op("vector", "memset", b2.ones64[:, :], 1.0)
                    tr.op("vector", "memset", b2.Scar[:, :], 0.0)
                    tr.op("vector", "memset", kmaxloc[:, :], -1e38)
                    cm = tr.op("vector", "nop", inc=S.s_dv2)
                    for r in range(NCORES):
                        tr.wait("gpsimd", S.s_dv2,
                                cm if r == 0 else h[("gproc", r - 1)])
                        c = tr.dma("gpsimd", b2.gath[:, :],
                                   ccdst_d[r * 64:(r + 1) * 64, :], S.d_x)
                        h[("gld", r)] = c
                        tr.wait("vector", S.d_x, h[("gld", r)])
                        tr.opw("vector", "tensor_scalar", b2.kcm[:, 0:16],
                               b2.gath[:, 0:16], seqb[:, r:r + 1], None,
                               ALU.add, inc=S.s_dv2)
                        tr.opw("vector", "tensor_tensor", kmaxloc[:, :],
                               kmaxloc[:, :], b2.kcm[:, 0:16], ALU.max,
                               inc=S.s_dv2)
                        c = tr.opw("vector", "scalar_tensor_tensor",
                                   b2.Scar[:, :], b2.gath[:, 16:CCW],
                                   prefm[:, r:r + 1], b2.Scar[:, :],
                                   ALU.mult, ALU.add, inc=S.s_dv2)
                        h[("gproc", r)] = c
                    c = tr.op("vector", "nop", inc=S.s_dv2)
                    tr.wait("scalar", S.s_dv2, c)
                    c = tr.op("scalar", "activation", escale[:, :],
                              kmaxloc[:, :], AF.Exp, scale=-1.0, inc=S.s_ac2)
                    tr.wait("vector", S.s_ac2, c)
                    tr.opw("vector", "tensor_scalar", negm[:, :],
                           kmaxloc[:, :], -1.0, None, ALU.mult, inc=S.s_dv2)
                    for hd in range(NH):
                        crs = tr.op("vector", "tensor_scalar",
                                    b2.Scar[:, hd * 128:(hd + 1) * 128],
                                    b2.Scar[:, hd * 128:(hd + 1) * 128],
                                    escale[:, hd:hd + 1], None, ALU.mult,
                                    inc=S.s_dv2 if hd == NH - 1 else None)
                    tr.wait("vector", S.s_dv2, crs)
                    for hd in range(NH):
                        c = tr.op("vector", "tensor_copy",
                                  b2.Scarb[:, hd * HS:(hd + 1) * HS],
                                  b2.Scar[:, hd * 128:hd * 128 + 64],
                                  inc=S.s_dv2 if hd == NH - 1 else None)
                    h[("scb", -1)] = c

                    for n in range(NCH):
                        if n == 0:
                            tr.wait("gpsimd", S.s_dv2,
                                    h[("gproc", NCORES - 1)])
                        if n > 0:
                            tr.wait("gpsimd", S.s_ac2, h[("kexp", n - 1)])
                        c = tr.dma("gpsimd", b2.kcm[:, :], kcm_d[n, :, :],
                                   S.d_x)
                        if n > 0:
                            tr.wait("gpsimd", S.s_dv2, h[("qtb", n - 1)])
                        c = tr.dma("gpsimd", b2.qscm[:, :], qscm_d[n, :, :],
                                   S.d_x)
                        if n > 0:
                            tr.wait("gpsimd", S.s_dv2, h[("sup", n - 1)])
                        c = tr.dma("gpsimd", b2.schk[:, :],
                                   schunks_d[n, :, :], S.d_x)
                        if n > 0:
                            tr.wait("gpsimd", S.s_pe2, h[("omm", n - 1)])
                        c = tr.dma("gpsimd", b2.vchunk[:, :],
                                   vtm_d[n * 128:(n + 1) * 128, :], S.d_x)
                        if n > 0:
                            tr.wait("gpsimd", S.d_st, h[("stc", n - 1)])
                        c = tr.dma("gpsimd", b2.xres1[:, :],
                                   stream[n * 128:(n + 1) * 128, :], S.d_x)
                        h[("ld", n)] = c
                        tr.wait("scalar", S.d_x, h[("ld", n)])
                        if n == 0:
                            tr.wait("scalar", S.s_dv2, h[("scb", -1)])
                        for hd in range(NH):
                            c = tr.op("scalar", "activation",
                                      b2.kexp[:, hd * 128:(hd + 1) * 128],
                                      b2.kcm[:, hd * 128:(hd + 1) * 128],
                                      AF.Exp, bias=negm[:, hd:hd + 1],
                                      inc=S.s_ac2 if hd == NH - 1 else None)
                        h[("kexp", n)] = c
                        tr.wait("vector", S.s_ac2, h[("kexp", n)])
                        ckb = tr.op("vector", "tensor_copy", b2.kexpb[:, :],
                                    b2.kexp[:, :], inc=S.s_dv2)
                        for hd in range(NH):
                            csc = tr.op("vector", "tensor_tensor_scan",
                                        b2.zfull[:, hd * 128:(hd + 1) * 128],
                                        b2.ones64[:, :],
                                        b2.kexp[:, hd * 128:(hd + 1) * 128],
                                        b2.Scar[:, hd * 128 + 64:
                                                hd * 128 + 65],
                                        ALU.mult, ALU.add,
                                        inc=S.s_dv2 if hd == NH - 1 else None)
                        tr.wait("vector", S.s_dv2, csc)
                        tr.opw("vector", "tensor_scalar", b2.ztmp[:, :],
                               b2.zfull[:, :], 1e-9, None, ALU.add,
                               inc=S.s_dv2)
                        tr.opw("vector", "reciprocal", b2.ztmp[:, :],
                               b2.ztmp[:, :], inc=S.s_dv2)
                        tr.wait("vector", S.d_x, h[("ld", n)])
                        cqt = tr.op("vector", "tensor_tensor", b2.qtb[:, :],
                                    b2.qscm[:, :], b2.ztmp[:, :], ALU.mult,
                                    inc=S.s_dv2)
                        h[("qtb", n)] = cqt
                        for hd in range(NH):
                            slot = hd % 2
                            tr.wait("tensor", S.s_dv2, cqt)
                            if hd >= 2:
                                tr.wait("tensor", S.s_dv2,
                                        h[("am", n, hd - 2)])
                            c = tr.op("tensor", "matmul", b2.pA[slot][:, :],
                                      b2.kexpb[:, hd * 128:(hd + 1) * 128],
                                      b2.qtb[:, hd * 128:(hd + 1) * 128],
                                      start=True, stop=True,
                                      skip_group_check=True, inc=S.s_pe2)
                            h[("amm", n, hd)] = c
                            tr.wait("vector", S.s_pe2, h[("amm", n, hd)])
                            c = tr.op("vector", "tensor_tensor",
                                      b2.abf[slot][:, :], b2.pA[slot][:, :],
                                      umaskf[:, :], ALU.mult, inc=S.s_dv2)
                            h[("am", n, hd)] = c
                            tr.wait("tensor", S.s_dv2, h[("am", n, hd)])
                            if n > 0 and hd == 0:
                                tr.wait("tensor", S.s_dv2,
                                        h[("ofm", n - 1, NH - 1)])
                            tr.op("tensor", "matmul",
                                  b2.pO[0:64, hd * 128:(hd + 1) * 128],
                                  b2.Scarb[:, hd * HS:(hd + 1) * HS],
                                  b2.qtb[:, hd * 128:(hd + 1) * 128],
                                  start=(hd % 4 == 0), stop=False,
                                  skip_group_check=True)
                            c = tr.op("tensor", "matmul",
                                      b2.pO[0:64, hd * 128:(hd + 1) * 128],
                                      b2.vchunk[:, hd * HS:(hd + 1) * HS],
                                      b2.abf[slot][:, :],
                                      start=False, stop=(hd % 4 == 3),
                                      skip_group_check=True, inc=S.s_pe2)
                            h[("opair", n, hd)] = c
                            tr.wait("vector", S.s_pe2, h[("opair", n, hd)])
                            sub = hd % 2
                            kd = hd // 2
                            c = tr.op("vector", "tensor_copy",
                                      b2.ofm[sub * 64:(sub + 1) * 64,
                                             kd * 128:(kd + 1) * 128],
                                      b2.pO[0:64, hd * 128:(hd + 1) * 128],
                                      inc=S.s_dv2)
                            h[("ofm", n, hd)] = c
                            tr.op("vector", "scalar_tensor_tensor",
                                  b2.Scar[:, hd * 128:(hd + 1) * 128],
                                  b2.schk[:, hd * 128:(hd + 1) * 128],
                                  escale[:, hd:hd + 1],
                                  b2.Scar[:, hd * 128:(hd + 1) * 128],
                                  ALU.mult, ALU.add)
                        h[("omm", n)] = S.s_pe2.count
                        cupd = tr.op("vector", "nop", inc=S.s_dv2)
                        tr.wait("vector", S.s_dv2, cupd)
                        for hd in range(NH):
                            c = tr.op("vector", "tensor_copy",
                                      b2.Scarb[:, hd * HS:(hd + 1) * HS],
                                      b2.Scar[:, hd * 128:hd * 128 + 64],
                                      inc=S.s_dv2 if hd == NH - 1 else None)
                        h[("sup", n)] = c
                        if n == 0:
                            tr.wait("tensor", S.d_w, h["wo"])
                        for half in range(2):
                            tr.wait("tensor", S.s_dv2, h[("ofm", n, NH - 1)])
                            if n > 0:
                                tr.wait("tensor", S.s_dv2,
                                        h[("xadd", n - 1, half)])
                            for kd in range(NKC):
                                c = tr.op("tensor", "matmul",
                                          b2.pXO[half][:, :],
                                          b2.ofm[:, kd * 128:(kd + 1) * 128],
                                          b2.woutT[:, kd * D + half * 512:
                                                   kd * D + (half + 1) * 512],
                                          start=(kd == 0),
                                          stop=(kd == NKC - 1),
                                          inc=S.s_pe2
                                          if kd == NKC - 1 else None)
                            h[("xomm", n, half)] = S.s_pe2.count
                            tr.wait("vector", S.s_pe2, h[("xomm", n, half)])
                            c = tr.op("vector", "tensor_tensor",
                                      b2.xres1[:, half * 512:
                                               (half + 1) * 512],
                                      b2.xres1[:, half * 512:
                                               (half + 1) * 512],
                                      b2.pXO[half][:, :], ALU.add,
                                      inc=S.s_dv2)
                            h[("xadd", n, half)] = c
                        tr.wait("gpsimd", S.s_dv2, h[("xadd", n, 1)])
                        c = tr.dma("gpsimd", stream[n * 128:(n + 1) * 128, :],
                                   b2.xres1[:, :], S.d_st)
                        h[("stc", n)] = c
                    tr.wait("gpsimd", S.d_st, S.d_st.count)

                run_phase(nc, allsems, emit_p2b)

        if "p3" in phases:
            with ExitStack() as c4:
                f3 = make_ffn_tensors(c4, "f3_")
                run_phase(nc, allsems,
                          lambda tr: emit_ffn(tr, f3, stream, y_out))
        elif "copy" in phases:
            def emit_copy(tr):
                c = tr.dma("gpsimd", y_out[:, :], stream[:, :], S.d_st)
                tr.wait("gpsimd", S.d_st, c)

            run_phase(nc, allsems, emit_copy)

    return nc


def host_consts():
    identf = np.eye(128, dtype=np.float32)
    umask = np.triu(np.ones((128, 128), dtype=np.float32))
    slotb = np.broadcast_to(
        (np.arange(E, dtype=np.float32) * CAP - 1.0)[None, :],
        (128, E)).copy()
    prefm = []
    seqb = []
    for c in range(NCORES):
        pm = np.zeros((64, 8), dtype=np.float32)
        sb_ = np.full((64, 8), -1e38, dtype=np.float32)
        for r in range(NCORES):
            if r // 4 == c // 4:
                sb_[:, r] = 0.0
                if r % 4 < c % 4:
                    pm[:, r] = 1.0
        prefm.append(pm)
        seqb.append(sb_)
    return identf, umask, slotb, prefm, seqb


def kernel(L0_attn_x, L0_attn_W_qkv, L0_attn_W_out, L0_ffn_gate_w,
           L0_ffn_w1, L0_ffn_w3, L0_ffn_w2):
    x = np.ascontiguousarray(np.asarray(L0_attn_x, dtype=np.float32))
    wqkv = np.ascontiguousarray(np.asarray(L0_attn_W_qkv, dtype=np.float32))
    wout = np.ascontiguousarray(np.asarray(L0_attn_W_out, dtype=np.float32))
    gw = np.ascontiguousarray(np.asarray(L0_ffn_gate_w, dtype=np.float32))
    w1 = np.ascontiguousarray(np.asarray(L0_ffn_w1, dtype=np.float32))
    w3 = np.ascontiguousarray(np.asarray(L0_ffn_w3, dtype=np.float32))
    w2 = np.ascontiguousarray(np.asarray(L0_ffn_w2, dtype=np.float32))

    identf, umask, slotb, prefm, seqb = host_consts()
    nc = build_program()
    in_maps = []
    for c in range(NCORES):
        b = c // 4
        t0 = (c % 4) * TPC
        in_maps.append({
            "x": np.ascontiguousarray(x[b, t0:t0 + TPC, :]),
            "wqkv": wqkv, "wout": wout, "gw": gw,
            "w1": w1, "w3": w3, "w2": w2,
            "identf": identf, "umask": umask, "slotb": slotb,
            "prefm": prefm[c], "seqb": seqb[c],
        })
    res = run_bass_kernel_spmd(nc, in_maps, core_ids=list(range(NCORES)))
    out = np.empty((B, T, D), dtype=np.float32)
    for c in range(NCORES):
        b = c // 4
        t0 = (c % 4) * TPC
        out[b, t0:t0 + TPC, :] = np.asarray(res.results[c]["y"],
                                            dtype=np.float32)
    return out


# version 05
# speedup vs baseline: 1.5959x; 1.0246x over previous
import numpy as np
import ml_dtypes
from contextlib import ExitStack

import concourse.bass as bass
import concourse.mybir as mybir
from concourse.bass_utils import run_bass_kernel_spmd

F32 = mybir.dt.float32
BF16 = mybir.dt.bfloat16
AF = mybir.ActivationFunctionType
ALU = mybir.AluOpType
AX = mybir.AxisListType

NCORES = 8
B, T, D = 2, 16384, 1024
NH, HS = 16, 64
E, HID = 8, 2816
TPC = T * B // NCORES
TT = 1024
NTILE = TPC // TT
NTC = TT // 128
NKC = D // 128
NHC = HID // 128
HGROUPS = [(0, 8), (8, 16), (16, 22)]
HBLOCKS = [(0, 4), (4, 8), (8, 12), (12, 16), (16, 20), (20, 22)]
NCH = TPC // 128
LNEPS = 1e-5
CAP = 1280
ZPAD = 2048
SENT = E * CAP
I32 = mybir.dt.int32

ENGS = ("sync", "gpsimd", "vector", "tensor", "scalar")


class Sem:
    __slots__ = ("h", "count")

    def __init__(self, h):
        self.h = h
        self.count = 0


class Tr:

    def __init__(self, active, eng):
        self.a = active
        self.e = eng
        self.hist = {}

    def op(self, name, fn, *args, inc=None, incn=1, **kw):
        if self.a == name:
            inst = getattr(self.e, fn)(*args, **kw)
            if inc is not None:
                inst.then_inc(inc.h, incn)
        if inc is not None:
            inc.count += incn
            return inc.count
        return None

    def wait(self, name, sem, val):
        if val is None or val <= 0:
            return
        if self.a == name:
            self.e.wait_ge(sem.h, val)

    def dma(self, name, out, in_, sem, **kw):
        return self.op(name, "dma_start", inc=sem, incn=16, out=out, in_=in_, **kw)

    def opw(self, name, fn, *args, inc, incn=1, **kw):
        c = self.op(name, fn, *args, inc=inc, incn=incn, **kw)
        self.wait(name, inc, c)
        return c


def run_phase(nc, sems, emit):
    base = [(s, s.count) for s in sems]
    with nc.Block() as block:
        for name in ENGS:
            for s, c in base:
                s.count = c

            def body(eng, _n=name):
                emit(Tr(_n, eng))

            getattr(block, name)(body)


class NS:
    pass


def build_program(phases=("p0", "p1", "p2", "p3")):
    nc = bass.Bass("TRN2", target_bir_lowering=False)
    t = NS()

    x_in = nc.dram_tensor("x", [TPC, D], F32, kind="ExternalInput")
    wqkv_in = nc.dram_tensor("wqkv", [3 * D, D], F32, kind="ExternalInput")
    wout_in = nc.dram_tensor("wout", [D, D], F32, kind="ExternalInput")
    gw_in = nc.dram_tensor("gw", [E, D], F32, kind="ExternalInput")
    w1_in = nc.dram_tensor("w1", [E, HID, D], F32, kind="ExternalInput")
    w3_in = nc.dram_tensor("w3", [E, HID, D], F32, kind="ExternalInput")
    w2_in = nc.dram_tensor("w2", [E, D, HID], F32, kind="ExternalInput")
    identf_in = nc.dram_tensor("identf", [128, 128], F32, kind="ExternalInput")
    umask_in = nc.dram_tensor("umask", [128, 128], F32, kind="ExternalInput")
    prefm_in = nc.dram_tensor("prefm", [64, 8], F32, kind="ExternalInput")
    seqb_in = nc.dram_tensor("seqb", [64, 8], F32, kind="ExternalInput")
    slotb_in = nc.dram_tensor("slotb", [128, 8], F32, kind="ExternalInput")
    y_out = nc.dram_tensor("y", [TPC, D], F32, kind="ExternalOutput")

    w1_bf = nc.dram_tensor("w1_bf", [E, HID, D], BF16, kind="Internal")
    w3_bf = nc.dram_tensor("w3_bf", [E, HID, D], BF16, kind="Internal")
    w2_bf = nc.dram_tensor("w2_bf", [E, D, HID], BF16, kind="Internal")
    wqkv_bf = nc.dram_tensor("wqkv_bf", [3 * D, D], BF16, kind="Internal")
    wout_bf = nc.dram_tensor("wout_bf", [D, D], BF16, kind="Internal")
    stream = nc.dram_tensor("stream", [TPC, D], F32, kind="Internal")
    kcm_d = nc.dram_tensor("kcm_d", [NCH, 64, NH * 128], F32, kind="Internal")
    qscm_d = nc.dram_tensor("qscm_d", [NCH, 64, NH * 128], F32, kind="Internal")
    vtm_d = nc.dram_tensor("vtm_d", [TPC, D], BF16, kind="Internal")
    schunks_d = nc.dram_tensor("schunks_d", [NCH, 64, NH * 128], F32, kind="Internal")
    CCW = 16 + NH * 128
    ccsrc_d = nc.dram_tensor("ccsrc_d", [64, CCW], F32, kind="Internal")
    ccdst_d = nc.dram_tensor("ccdst_d", [64 * NCORES, CCW], F32, kind="Internal")
    xnbf_d = nc.dram_tensor("xnbf_d", [TPC, D], F32, kind="Internal")
    xg_d = nc.dram_tensor("xg_d", [E * CAP + 1, D], F32, kind="Internal")
    y_d = nc.dram_tensor("y_d", [E * CAP + ZPAD, D], F32, kind="Internal")

    with ExitStack() as top:
        def sb(ctx, name, shape, dt=F32):
            return ctx.enter_context(nc.sbuf_tensor(name, shape, dt))

        def ps(ctx, name, shape, dt=F32):
            return ctx.enter_context(nc.psum_tensor(name, shape, dt))

        identf = sb(top, "identf_s", [128, 128])
        umaskf = sb(top, "umaskf_s", [128, 128])
        prefm = sb(top, "prefm_s", [64, 8])
        seqb = sb(top, "seqb_s", [64, 8])
        gwsb = sb(top, "gwsb", [8, D])
        slotbase = sb(top, "slotbase_s", [128, 8])
        gwT = sb(top, "gwT", [128, 8 * E])
        Stot = sb(top, "Stot", [64, NH * 128])
        kmaxloc = sb(top, "kmaxloc", [64, NH])
        escale = sb(top, "escale", [64, NH])
        negm = sb(top, "negm", [64, NH])

        semnames = ["d_cin", "d_wc", "d_w", "d_x", "d_st", "d_sp", "cc_s",
                    "s_tr", "s_tev", "s_amm", "s_silu", "s_hm", "s_bmm",
                    "s_bep", "s_w13c", "s_w2c", "s_gmm", "s_dv", "s_ac",
                    "s_pe2", "s_dv2", "s_ac2", "s_misc"]
        sems = {n: Sem(top.enter_context(nc.semaphore(n))) for n in semnames}
        S = NS()
        for n, s in sems.items():
            setattr(S, n, s)
        allsems = list(sems.values())

        if "p0" in phases:
            with ExitStack() as c0:
                pg0 = ps(c0, "pg0", [128, 16])
                pg1 = ps(c0, "pg1", [128, 16])
                z0 = sb(c0, "z0", [128, D])

                def emit_p0(tr):
                    for ap_out, ap_in in [(identf[:, :], identf_in[:, :]),
                                          (umaskf[:, :], umask_in[:, :]),
                                          (prefm[:, :], prefm_in[:, :]),
                                          (seqb[:, :], seqb_in[:, :]),
                                          (gwsb[:, :], gw_in[:, :]),
                                          (slotbase[:, :], slotb_in[:, :])]:
                        tr.dma("sync", ap_out, ap_in, S.d_cin)
                    cin_done = S.d_cin.count
                    tr.wait("tensor", S.d_cin, cin_done)
                    for k in range(NKC):
                        pg = pg0 if k % 2 == 0 else pg1
                        if k >= 2:
                            tr.wait("tensor", S.s_tev, tr.hist[("gev", k - 2)])
                        c = tr.op("tensor", "transpose", pg[:, 0:8],
                                  gwsb[0:8, k * 128:(k + 1) * 128],
                                  identf[0:8, 0:8], inc=S.s_tr)
                        tr.hist[("gtr", k)] = c
                        tr.wait("vector", S.s_tr, tr.hist[("gtr", k)])
                        c = tr.op("vector", "tensor_copy",
                                  gwT[:, k * 8:(k + 1) * 8], pg[:, 0:8],
                                  inc=S.s_tev)
                        tr.hist[("gev", k)] = c
                    tr.op("vector", "memset", Stot[:, :], 0.0)
                    tr.op("vector", "memset", kmaxloc[:, :], -1e38)
                    cz = tr.op("vector", "memset", z0[:, :], 0.0, inc=S.s_dv)
                    tr.wait("gpsimd", S.s_dv, cz)
                    for i in range(ZPAD // 128):
                        tr.dma("gpsimd",
                               y_d[E * CAP + i * 128:E * CAP + (i + 1) * 128, :],
                               z0[:, :], S.d_st)

                run_phase(nc, allsems, emit_p0)

        def make_ffn_tensors(ctx, pfx):
            f = NS()
            f.xstg = sb(ctx, pfx + "xstg", [128, D])
            f.cst = sb(ctx, pfx + "cst", [128, D])
            f.sqst = sb(ctx, pfx + "sqst", [128, D])
            f.xntm = sb(ctx, pfx + "xntm", [128, D])
            f.xnf = sb(ctx, pfx + "xnf", [128, D])
            f.coef = sb(ctx, pfx + "coef", [128, NCH * E])
            f.maskcm = sb(ctx, pfx + "maskcm", [8, TPC])
            f.zeros8 = sb(ctx, pfx + "zeros8", [8, 1024])
            f.postm = sb(ctx, pfx + "postm", [128, 8])
            f.mtm = sb(ctx, pfx + "mtm", [128, 8])
            f.ovf = sb(ctx, pfx + "ovf", [128, 8])
            f.m2t = sb(ctx, pfx + "m2t", [128, 8])
            f.slf = sb(ctx, pfx + "slf", [128, 8])
            f.sl2 = sb(ctx, pfx + "sl2", [128, 8])
            f.mm1 = sb(ctx, pfx + "mm1", [128, 8])
            f.cm1 = sb(ctx, pfx + "cm1", [128, 8])
            f.o1 = sb(ctx, pfx + "o1", [128, 1])
            f.o2 = sb(ctx, pfx + "o2", [128, 1])
            f.oc = sb(ctx, pfx + "oc", [128, 1])
            f.c12 = sb(ctx, pfx + "c12", [128, 2 * NCH])
            f.sIdx = sb(ctx, pfx + "sIdx", [128, 2 * NCH], I32)
            f.gIdx = sb(ctx, pfx + "gIdx", [128, 2 * NCH], I32)
            f.xgld = sb(ctx, pfx + "xgld", [128, D])
            f.xgfm = sb(ctx, pfx + "xgfm", [128, NKC * CAP], BF16)
            f.hbuf = sb(ctx, pfx + "hbuf", [128, 8 * CAP], BF16)
            f.outacc = sb(ctx, pfx + "outacc", [128, (CAP // 128) * D])
            f.w1b = [sb(ctx, pfx + "w1b%d" % i, [128, NKC * 512], BF16)
                     for i in range(2)]
            f.w3b = [sb(ctx, pfx + "w3b%d" % i, [128, NKC * 512], BF16)
                     for i in range(2)]
            f.w2b = sb(ctx, pfx + "w2b", [128, 8 * D], BF16)
            f.silu = [sb(ctx, pfx + "silu%d" % i, [128, 512]) for i in range(2)]
            f.g1g = sb(ctx, pfx + "g1g", [128, D])
            f.g2g = sb(ctx, pfx + "g2g", [128, D])
            f.ostg = sb(ctx, pfx + "ostg", [128, D])
            f.r1 = sb(ctx, pfx + "r1", [128, 1])
            f.epsb = sb(ctx, pfx + "epsb", [128, 1])
            f.nmean = sb(ctx, pfx + "nmean", [128, 1])
            f.ssq = sb(ctx, pfx + "ssq", [128, 1])
            f.sd = sb(ctx, pfx + "sd", [128, 1])
            f.rstd = sb(ctx, pfx + "rstd", [128, 1])
            f.m1n = sb(ctx, pfx + "m1n", [128, 1])
            f.m2 = sb(ctx, pfx + "m2", [128, 1])
            f.d1 = sb(ctx, pfx + "d1", [128, 1])
            f.r2 = sb(ctx, pfx + "r2", [128, 1])
            f.gst = sb(ctx, pfx + "gst", [128, 8])
            f.ge1 = sb(ctx, pfx + "ge1", [128, 8])
            f.tmp8 = sb(ctx, pfx + "tmp8", [128, 8])
            f.gm = sb(ctx, pfx + "gm", [128, 8])
            f.pH1 = [ps(ctx, pfx + "pH1%d" % i, [128, 512]) for i in range(2)]
            f.pH3 = [ps(ctx, pfx + "pH3%d" % i, [128, 512]) for i in range(2)]
            f.pB = [ps(ctx, pfx + "pB%d" % i, [128, 512]) for i in range(2)]
            f.pT = [ps(ctx, pfx + "pT%d" % i, [128, 128]) for i in range(2)]
            return f

        xcounts = {}

        def emit_ffn(tr, f, src, dst, do_casts=False):
            h = tr.hist
            w13c0 = S.s_w13c.count
            w2c0 = S.s_w2c.count
            if do_casts:
                for e in range(E):
                    tr.dma("gpsimd", w1_bf[e, :, :], w1_in[e, :, :], S.d_wc)
                    tr.dma("gpsimd", w3_bf[e, :, :], w3_in[e, :, :], S.d_wc)
                    c = tr.dma("gpsimd", w2_bf[e, :, :], w2_in[e, :, :], S.d_wc)
                    xcounts[("wc", e)] = c
                xcounts[("wc_qkv",)] = tr.dma("gpsimd", wqkv_bf[:, :],
                                              wqkv_in[:, :], S.d_wc)
                xcounts[("wc_out",)] = tr.dma("gpsimd", wout_bf[:, :],
                                              wout_in[:, :], S.d_wc)
            xgf0 = S.s_misc.count
            tr.op("vector", "memset", f.epsb[:, :], LNEPS, inc=S.s_dv)
            tr.op("vector", "memset", f.zeros8[:, :], 0.0, inc=S.s_dv)
            trk = 0

            for tc in range(NCH):
                if tc > 0:
                    tr.wait("sync", S.s_dv, h[("lnd", tc - 1)])
                c = tr.dma("sync", f.xstg[:, :],
                           src[tc * 128:(tc + 1) * 128, :], S.d_x)
                h[("xld", tc)] = c
                tr.wait("vector", S.d_x, h[("xld", tc)])
                tr.opw("vector", "reduce_sum", f.r1[:, :], f.xstg[:, :],
                       axis=AX.X, inc=S.s_dv)
                tr.opw("vector", "tensor_scalar", f.nmean[:, :], f.r1[:, :],
                       -1.0 / D, None, ALU.mult, inc=S.s_dv)
                c = tr.op("vector", "tensor_scalar", f.cst[:, :], f.xstg[:, :],
                          f.nmean[:, :], None, ALU.add, inc=S.s_dv)
                h[("lnd", tc)] = c
                tr.wait("scalar", S.s_dv, c)
                tr.opw("scalar", "activation", f.sqst[:, :], f.cst[:, :],
                       AF.Square, accum_out=f.ssq[:, :], inc=S.s_ac)
                c = tr.op("scalar", "activation", f.sd[:, :], f.ssq[:, :],
                          AF.Sqrt, scale=1.0 / D, bias=f.epsb[:, :],
                          inc=S.s_ac)
                tr.wait("vector", S.s_ac, c)
                tr.opw("vector", "reciprocal", f.rstd[:, :], f.sd[:, :],
                       inc=S.s_dv)
                if tc > 0:
                    tr.wait("vector", S.s_tr, h[("trl", tc - 1)])
                    tr.wait("vector", S.d_sp, h[("xnst", tc - 1)])
                cxn = tr.op("vector", "tensor_scalar", f.xntm[:, :],
                            f.cst[:, :], f.rstd[:, :], None, ALU.mult,
                            inc=S.s_dv)
                tr.wait("sync", S.s_dv, cxn)
                c = tr.dma("sync", xnbf_d[tc * 128:(tc + 1) * 128, :],
                           f.xntm[:, :], S.d_sp)
                h[("xnst", tc)] = c
                tr.wait("tensor", S.s_dv, cxn)
                for k in range(NKC):
                    slot = trk % 2
                    if trk >= 2:
                        tr.wait("tensor", S.s_tev, h[("tev", trk - 2)])
                    c = tr.op("tensor", "transpose", f.pT[slot][:, :],
                              f.xntm[:, k * 128:(k + 1) * 128],
                              identf[:, :], inc=S.s_tr)
                    h[("tr", trk)] = c
                    tr.wait("vector", S.s_tr, h[("tr", trk)])
                    c = tr.op("vector", "tensor_copy",
                              f.xnf[:, k * 128:(k + 1) * 128],
                              f.pT[slot][:, :], inc=S.s_tev)
                    h[("tev", trk)] = c
                    trk += 1
                tr.wait("tensor", S.s_tev, h[("tev", trk - 1)])
                if tc > 0:
                    tr.wait("tensor", S.s_ac, h[("gex", tc - 1)])
                for k in range(NKC):
                    tr.op("tensor", "matmul", f.pB[0][:, 0:8],
                          f.xnf[:, k * 128:(k + 1) * 128],
                          gwT[:, k * 8:(k + 1) * 8],
                          start=(k == 0), stop=(k == NKC - 1),
                          skip_group_check=True,
                          inc=S.s_gmm if k == NKC - 1 else None)
                h[("g", tc)] = S.s_gmm.count
                tr.wait("vector", S.s_gmm, h[("g", tc)])
                c = tr.op("vector", "reduce_max", f.m1n[:, :],
                          f.pB[0][:, 0:8], axis=AX.X, negate=True, inc=S.s_dv)
                tr.wait("scalar", S.s_dv, c)
                c = tr.op("scalar", "activation", f.gst[:, :],
                          f.pB[0][:, 0:8], AF.Exp, bias=f.m1n[:, :],
                          inc=S.s_ac)
                h[("gex", tc)] = c
                tr.wait("vector", S.s_ac, c)
                tr.opw("vector", "tensor_single_scalar", f.ge1[:, :],
                       f.gst[:, :], 1.0, ALU.is_ge, inc=S.s_dv)
                tr.opw("vector", "scalar_tensor_tensor", f.tmp8[:, :],
                       f.ge1[:, :], -1e30, f.gst[:, :], ALU.mult, ALU.add,
                       inc=S.s_dv)
                tr.opw("vector", "reduce_max", f.m2[:, :], f.tmp8[:, :],
                       axis=AX.X, inc=S.s_dv)
                tr.opw("vector", "tensor_single_scalar", f.tmp8[:, :],
                       f.gst[:, :], f.m2[:, :], ALU.is_ge, inc=S.s_dv)
                tr.opw("vector", "tensor_tensor", f.gm[:, :], f.gst[:, :],
                       f.tmp8[:, :], ALU.mult, inc=S.s_dv)
                tr.opw("vector", "tensor_scalar", f.d1[:, :], f.m2[:, :],
                       1.0, None, ALU.add, inc=S.s_dv)
                tr.opw("vector", "reciprocal", f.r2[:, :], f.d1[:, :],
                       inc=S.s_dv)
                tr.opw("vector", "tensor_scalar",
                       f.coef[:, tc * E:(tc + 1) * E], f.gm[:, :],
                       f.r2[:, :], 0.5, ALU.mult, ALU.mult, inc=S.s_dv)
                cmk = tr.opw("vector", "tensor_single_scalar", f.gm[:, :],
                             f.coef[:, tc * E:(tc + 1) * E], 0.0, ALU.is_gt,
                             inc=S.s_dv)
                slot = trk % 2
                tr.wait("tensor", S.s_dv, cmk)
                if trk >= 2:
                    tr.wait("tensor", S.s_tev, h[("tev", trk - 2)])
                c = tr.op("tensor", "transpose", f.pT[slot][0:8, 0:128],
                          f.gm[:, :], identf[:, :], inc=S.s_tr)
                h[("tr", trk)] = c
                h[("trl", tc)] = c
                tr.wait("vector", S.s_tr, h[("tr", trk)])
                c = tr.op("vector", "tensor_copy",
                          f.maskcm[:, tc * 128:(tc + 1) * 128],
                          f.pT[slot][0:8, 0:128], inc=S.s_tev)
                h[("tev", trk)] = c
                trk += 1
            h[("g1_done",)] = S.s_dv.count

            tr.wait("vector", S.s_tev, h[("tev", trk - 1)])
            for b in range(4):
                init = (f.zeros8[:, 0:1] if b == 0
                        else f.maskcm[:, b * 1024 - 1:b * 1024])
                tr.opw("vector", "tensor_tensor_scan",
                       f.maskcm[:, b * 1024:(b + 1) * 1024],
                       f.zeros8[:, :],
                       f.maskcm[:, b * 1024:(b + 1) * 1024],
                       init, ALU.add, ALU.add, inc=S.s_dv)
            for tc in range(NCH):
                slot = trk % 2
                if trk >= 2:
                    tr.wait("tensor", S.s_tev, h[("tev", trk - 2)])
                tr.wait("tensor", S.s_dv, S.s_dv.count)
                c = tr.op("tensor", "transpose", f.pT[slot][:, 0:8],
                          f.maskcm[:, tc * 128:(tc + 1) * 128],
                          identf[0:8, 0:8], inc=S.s_tr)
                h[("tr", trk)] = c
                tr.wait("vector", S.s_tr, h[("tr", trk)])
                c = tr.op("vector", "tensor_copy", f.postm[:, :],
                          f.pT[slot][:, 0:8], inc=S.s_tev)
                h[("tev", trk)] = c
                trk += 1
                tr.wait("vector", S.s_tev, h[("tev", trk - 1)])
                tr.opw("vector", "tensor_single_scalar", f.mtm[:, :],
                       f.coef[:, tc * E:(tc + 1) * E], 0.0, ALU.is_gt,
                       inc=S.s_dv)
                tr.opw("vector", "tensor_single_scalar", f.ovf[:, :],
                       f.postm[:, :], float(CAP), ALU.is_le, inc=S.s_dv)
                tr.opw("vector", "tensor_tensor", f.m2t[:, :], f.mtm[:, :],
                       f.ovf[:, :], ALU.mult, inc=S.s_dv)
                tr.opw("vector", "tensor_tensor", f.slf[:, :], f.postm[:, :],
                       slotbase[:, :], ALU.add, inc=S.s_dv)
                tr.opw("vector", "tensor_tensor", f.slf[:, :], f.slf[:, :],
                       f.m2t[:, :], ALU.mult, inc=S.s_dv)
                tr.opw("vector", "tensor_scalar", f.sl2[:, :], f.m2t[:, :],
                       -float(SENT), float(SENT), ALU.mult, ALU.add,
                       inc=S.s_dv)
                tr.opw("vector", "tensor_tensor", f.slf[:, :], f.slf[:, :],
                       f.sl2[:, :], ALU.add, inc=S.s_dv)
                tr.opw("vector", "tensor_reduce", f.o1[:, :], f.slf[:, :],
                       axis=AX.X, op=ALU.min, inc=S.s_dv)
                tr.opw("vector", "tensor_single_scalar", f.mm1[:, :],
                       f.slf[:, :], f.o1[:, :], ALU.is_equal, inc=S.s_dv)
                tr.opw("vector", "scalar_tensor_tensor", f.sl2[:, :],
                       f.mm1[:, :], float(SENT), f.slf[:, :], ALU.mult,
                       ALU.add, inc=S.s_dv)
                tr.opw("vector", "tensor_reduce", f.o2[:, :], f.sl2[:, :],
                       axis=AX.X, op=ALU.min, inc=S.s_dv)
                tr.opw("vector", "tensor_tensor", f.cm1[:, :], f.mm1[:, :],
                       f.coef[:, tc * E:(tc + 1) * E], ALU.mult, inc=S.s_dv)
                tr.opw("vector", "reduce_sum", f.c12[:, 2 * tc:2 * tc + 1],
                       f.cm1[:, :], axis=AX.X, inc=S.s_dv)
                tr.opw("vector", "tensor_single_scalar", f.mm1[:, :],
                       f.sl2[:, :], f.o2[:, :], ALU.is_equal, inc=S.s_dv)
                tr.opw("vector", "tensor_tensor", f.cm1[:, :], f.mm1[:, :],
                       f.coef[:, tc * E:(tc + 1) * E], ALU.mult, inc=S.s_dv)
                tr.opw("vector", "reduce_sum", f.c12[:, 2 * tc + 1:2 * tc + 2],
                       f.cm1[:, :], axis=AX.X, inc=S.s_dv)
                tr.opw("vector", "tensor_copy", f.sIdx[:, 2 * tc:2 * tc + 1],
                       f.o1[:, :], inc=S.s_dv)
                tr.opw("vector", "tensor_copy",
                       f.sIdx[:, 2 * tc + 1:2 * tc + 2], f.o2[:, :],
                       inc=S.s_dv)
                tr.opw("vector", "tensor_single_scalar", f.oc[:, :],
                       f.o1[:, :], float(E * CAP), ALU.min, inc=S.s_dv)
                tr.opw("vector", "tensor_copy", f.gIdx[:, 2 * tc:2 * tc + 1],
                       f.oc[:, :], inc=S.s_dv)
                tr.opw("vector", "tensor_single_scalar", f.oc[:, :],
                       f.o2[:, :], float(E * CAP), ALU.min, inc=S.s_dv)
                c = tr.opw("vector", "tensor_copy",
                           f.gIdx[:, 2 * tc + 1:2 * tc + 2], f.oc[:, :],
                           inc=S.s_dv)
                h[("idx", tc)] = c
            h[("g2_done",)] = S.s_dv.count

            tr.wait("gpsimd", S.s_dv, h[("g2_done",)])
            tr.wait("gpsimd", S.d_sp, h[("xnst", NCH - 1)])
            for tc in range(NCH):
                if tc > 0:
                    tr.wait("gpsimd", S.d_sp, h[("sc", tc - 1)])
                c = tr.dma("gpsimd", f.xgld[:, :],
                           xnbf_d[tc * 128:(tc + 1) * 128, :], S.d_x)
                tr.wait("gpsimd", S.d_x, c)
                for kk in range(2):
                    c = tr.op("gpsimd", "indirect_dma_start",
                              out=xg_d[:, :],
                              out_offset=bass.IndirectOffsetOnAxis(
                                  f.sIdx[:, 2 * tc + kk:2 * tc + kk + 1], 0),
                              in_=f.xgld[:, :], in_offset=None,
                              inc=S.d_sp, incn=16)
                h[("sc", tc)] = c
            h[("scat_done",)] = c
            tr.wait("gpsimd", S.d_sp, h[("scat_done",)])

            astep = 0
            bstep = 0
            blkidx = 0
            grpidx = 0
            NTE = CAP // 128
            ASL = [(0, 512), (512, 1024), (1024, CAP)]
            for e in range(E):
                for te in range(NTE):
                    if (e, te) != (0, 0):
                        tr.wait("gpsimd", S.s_tr, h[("xgtr_prev",)])
                    c = tr.dma("gpsimd", f.xgld[:, :],
                               xg_d[e * CAP + te * 128:
                                    e * CAP + (te + 1) * 128, :], S.d_x)
                    h[("xg", e, te)] = c
                    tr.wait("tensor", S.d_x, h[("xg", e, te)])
                    if e > 0 and te == 0:
                        tr.wait("tensor", S.s_amm, h[("aend", e - 1)])
                    for k in range(NKC):
                        slot = trk % 2
                        if trk >= 2:
                            tr.wait("tensor", S.s_tev, h[("tev", trk - 2)])
                        c = tr.op("tensor", "transpose", f.pT[slot][:, :],
                                  f.xgld[:, k * 128:(k + 1) * 128],
                                  identf[:, :], inc=S.s_tr)
                        h[("tr", trk)] = c
                        tr.wait("vector", S.s_tr, h[("tr", trk)])
                        c = tr.op("vector", "tensor_copy",
                                  f.xgfm[:, k * CAP + te * 128:
                                         k * CAP + (te + 1) * 128],
                                  f.pT[slot][:, :], inc=S.s_tev)
                        h[("tev", trk)] = c
                        trk += 1
                    h[("xgtr_prev",)] = S.s_tr.count
                tr.op("vector", "nop", inc=S.s_misc)
                h[("xgfm", e)] = S.s_misc.count
                for gi, (g0, g1) in enumerate(HGROUPS):
                    for bl in (2 * gi, 2 * gi + 1):
                        hb0, hb1 = HBLOCKS[bl]
                        nh2 = hb1 - hb0
                        if gi == 0 and bl == 0:
                            tr.wait("sync", S.d_wc, xcounts.get(("wc", e)))
                        if blkidx >= 2:
                            tr.wait("sync", S.s_w13c, w13c0 + blkidx - 1)
                        for k in range(NKC):
                            tr.dma("sync",
                                   f.w1b[blkidx % 2][:, k * 512:
                                                     k * 512 + nh2 * 128],
                                   w1_bf[e, hb0 * 128:hb1 * 128,
                                         k * 128:(k + 1) * 128],
                                   S.d_w, transpose=True)
                            c = tr.dma("sync",
                                       f.w3b[blkidx % 2][:, k * 512:
                                                         k * 512 + nh2 * 128],
                                       w3_bf[e, hb0 * 128:hb1 * 128,
                                             k * 128:(k + 1) * 128],
                                       S.d_w, transpose=True)
                        h[("wblk", blkidx)] = c
                        blkidx += 1
                    if grpidx >= 1:
                        tr.wait("sync", S.s_w2c, w2c0 + grpidx)
                    for hl, hh in enumerate(range(g0, g1)):
                        c = tr.dma("sync", f.w2b[:, hl * D:(hl + 1) * D],
                                   w2_bf[e, 0:D, hh * 128:(hh + 1) * 128],
                                   S.d_w, transpose=True)
                    h[("w2g", grpidx)] = c
                    grpidx += 1

                    for hh in range(g0, g1):
                        bl = hh // 4
                        gblk = e * 6 + bl
                        par = gblk % 2
                        hloc = hh - HBLOCKS[bl][0]
                        for n, (n0, n1) in enumerate(ASL):
                            p = astep % 2
                            tr.wait("tensor", S.d_w, h[("wblk", gblk)])
                            if astep == 0:
                                tr.wait("tensor", S.s_misc, h[("xgfm", e)])
                            if astep >= 2:
                                tr.wait("tensor", S.s_silu,
                                        h[("silu", astep - 2)])
                                tr.wait("tensor", S.s_hm, h[("hm", astep - 2)])
                            for k in range(NKC):
                                tr.op("tensor", "matmul",
                                      f.pH1[p][:, 0:n1 - n0],
                                      f.w1b[par][:, k * 512 + hloc * 128:
                                                 k * 512 + (hloc + 1) * 128],
                                      f.xgfm[:, k * CAP + n0:k * CAP + n1],
                                      start=(k == 0), stop=(k == NKC - 1))
                            for k in range(NKC):
                                tr.op("tensor", "matmul",
                                      f.pH3[p][:, 0:n1 - n0],
                                      f.w3b[par][:, k * 512 + hloc * 128:
                                                 k * 512 + (hloc + 1) * 128],
                                      f.xgfm[:, k * CAP + n0:k * CAP + n1],
                                      start=(k == 0), stop=(k == NKC - 1),
                                      inc=S.s_amm if k == NKC - 1 else None)
                            h[("amm", astep)] = S.s_amm.count
                            tr.wait("scalar", S.s_amm, h[("amm", astep)])
                            if astep >= 2:
                                tr.wait("scalar", S.s_hm, h[("hm", astep - 2)])
                            c = tr.op("scalar", "activation",
                                      f.silu[p][:, 0:n1 - n0],
                                      f.pH1[p][:, 0:n1 - n0], AF.Silu,
                                      inc=S.s_silu)
                            h[("silu", astep)] = c
                            tr.wait("vector", S.s_silu, h[("silu", astep)])
                            c = tr.op("vector", "tensor_tensor",
                                      f.hbuf[:, (hh - g0) * CAP + n0:
                                             (hh - g0) * CAP + n1],
                                      f.silu[p][:, 0:n1 - n0],
                                      f.pH3[p][:, 0:n1 - n0],
                                      ALU.mult, inc=S.s_hm)
                            h[("hm", astep)] = c
                            astep += 1
                        if hh == HBLOCKS[bl][1] - 1:
                            tr.op("tensor", "nop", inc=S.s_w13c)
                    nhg = g1 - g0
                    first_b = True
                    for te in range(NTE):
                        for half in range(2):
                            p = bstep % 2
                            if first_b:
                                tr.wait("tensor", S.s_hm, h[("hm", astep - 1)])
                                tr.wait("tensor", S.d_w,
                                        h[("w2g", grpidx - 1)])
                                first_b = False
                            if bstep >= 2:
                                tr.wait("tensor", S.s_bep,
                                        h[("bep", bstep - 2)])
                            for hl in range(nhg):
                                tr.op("tensor", "matmul", f.pB[p][:, :],
                                      f.hbuf[:, hl * CAP + te * 128:
                                             hl * CAP + (te + 1) * 128],
                                      f.w2b[:, hl * D + half * 512:
                                            hl * D + (half + 1) * 512],
                                      start=(hl == 0), stop=(hl == nhg - 1),
                                      skip_group_check=True,
                                      inc=S.s_bmm if hl == nhg - 1 else None)
                            h[("bmm", bstep)] = S.s_bmm.count
                            tr.wait("vector", S.s_bmm, h[("bmm", bstep)])
                            if e > 0 and gi == 0 and te == 0 and half == 0:
                                tr.wait("vector", S.d_st, h[("yst", e - 1)])
                            oslc = f.outacc[:, te * D + half * 512:
                                            te * D + (half + 1) * 512]
                            if gi == 0:
                                c = tr.op("vector", "tensor_copy", oslc,
                                          f.pB[p][:, :], inc=S.s_bep)
                            else:
                                c = tr.op("vector", "tensor_tensor", oslc,
                                          f.pB[p][:, :], oslc, ALU.add,
                                          inc=S.s_bep)
                            h[("bep", bstep)] = c
                            bstep += 1
                    tr.op("tensor", "nop", inc=S.s_w2c)
                h[("aend", e)] = S.s_amm.count
                for te in range(NTE):
                    tr.wait("gpsimd", S.s_bep, h[("bep", bstep - 1)])
                    c = tr.dma("gpsimd",
                               y_d[e * CAP + te * 128:
                                   e * CAP + (te + 1) * 128, :],
                               f.outacc[:, te * D:(te + 1) * D], S.d_st)
                h[("yst", e)] = c

            tr.wait("gpsimd", S.d_st, h[("yst", E - 1)])
            for tc in range(NCH):
                if tc > 0:
                    tr.wait("gpsimd", S.s_dv, h[("res", tc - 1)])
                c = tr.op("gpsimd", "indirect_dma_start",
                          out=f.g1g[:, :], out_offset=None,
                          in_=y_d[:, :],
                          in_offset=bass.IndirectOffsetOnAxis(
                              f.gIdx[:, 2 * tc:2 * tc + 1], 0),
                          inc=S.d_x, incn=16)
                c = tr.op("gpsimd", "indirect_dma_start",
                          out=f.g2g[:, :], out_offset=None,
                          in_=y_d[:, :],
                          in_offset=bass.IndirectOffsetOnAxis(
                              f.gIdx[:, 2 * tc + 1:2 * tc + 2], 0),
                          inc=S.d_x, incn=16)
                c = tr.dma("sync", f.xstg[:, :],
                           src[tc * 128:(tc + 1) * 128, :], S.d_x)
                h[("rld", tc)] = c
                tr.wait("vector", S.d_x, h[("rld", tc)])
                tr.opw("vector", "scalar_tensor_tensor", f.ostg[:, :],
                       f.g1g[:, :], f.c12[:, 2 * tc:2 * tc + 1],
                       f.xstg[:, :], ALU.mult, ALU.add, inc=S.s_dv)
                c = tr.op("vector", "scalar_tensor_tensor", f.ostg[:, :],
                          f.g2g[:, :], f.c12[:, 2 * tc + 1:2 * tc + 2],
                          f.ostg[:, :], ALU.mult, ALU.add, inc=S.s_dv)
                h[("res", tc)] = c
                tr.wait("sync", S.s_dv, h[("res", tc)])
                c = tr.dma("sync", dst[tc * 128:(tc + 1) * 128, :],
                           f.ostg[:, :], S.d_st)
            tr.wait("gpsimd", S.d_st, S.d_st.count)
            tr.wait("sync", S.d_st, S.d_st.count)


        if "p1" in phases:
            with ExitStack() as c1:
                f1 = make_ffn_tensors(c1, "f1_")
                run_phase(nc, allsems,
                          lambda tr: emit_ffn(tr, f1, x_in, stream,
                                              do_casts=True))

        if "p2" in phases:
            with ExitStack() as c2:
                a = NS()
                a.wqkvT = sb(c2, "a_wqkvT", [128, NKC * 3 * D], BF16)
                dbl = lambda nm, shape, dt=F32: [
                    sb(c2, "a_" + nm + str(i), shape, dt) for i in range(2)]
                a.xres1 = dbl("xres1", [128, D])
                a.cst = dbl("cst", [128, D])
                a.sqst = dbl("sqst", [128, D])
                a.xntm = dbl("xntm", [128, D])
                a.xnbf1 = dbl("xnbf1", [128, D], BF16)
                a.qstage = dbl("qstage", [128, D])
                a.qsm = dbl("qsm", [128, D])
                a.kstage = dbl("kstage", [128, D])
                a.kexp = dbl("kexp", [128, D])
                a.kexpb = dbl("kexpb", [128, D], BF16)
                a.vbf = dbl("vbf", [128, D], BF16)
                a.kcm = dbl("kcm", [64, NH * 128])
                a.qscm = dbl("qscm", [64, NH * 128])
                a.schst = dbl("schst", [64, NH * 128])
                a.r1 = dbl("r1", [128, 1])
                a.nmean = dbl("nmean", [128, 1])
                a.ssq = dbl("ssq", [128, 1])
                a.sd = dbl("sd", [128, 1])
                a.rstd = dbl("rstd", [128, 1])
                a.qm = dbl("qm", [128, 1])
                a.qsum = dbl("qsum", [128, 1])
                a.qr = dbl("qr", [128, 1])
                a.kmaxc = sb(c2, "a_kmaxc", [64, NH])
                a.onesb = sb(c2, "a_onesb", [128, 1], BF16)
                a.epsb = sb(c2, "a_epsb", [128, 1])
                a.pQ = [ps(c2, "a_pQ%d" % i, [128, 512]) for i in range(2)]
                a.pT = [ps(c2, "a_pT%d" % i, [128, 128]) for i in range(2)]
                a.pS = ps(c2, "a_pS", [64, NH * 128])

                def emit_p2a(tr):
                    h = tr.hist
                    tr.wait("sync", S.d_wc, xcounts.get(("wc_qkv",)))
                    for k in range(NKC):
                        for s2 in range(2):
                            c = tr.dma("sync",
                                       a.wqkvT[:, k * 3 * D + s2 * 1536:
                                               k * 3 * D + (s2 + 1) * 1536],
                                       wqkv_bf[s2 * 1536:(s2 + 1) * 1536,
                                               k * 128:(k + 1) * 128],
                                       S.d_w, transpose=True)
                    h["wq"] = c
                    tr.op("vector", "memset", a.onesb[:, :], 1.0)
                    tr.op("vector", "memset", a.epsb[:, :], LNEPS)
                    c0 = tr.op("vector", "memset", a.pS[:, :], 0.0,
                               inc=S.s_dv2)
                    trk = 0
                    for n in range(NCH):
                        P = n % 2
                        if n >= 2:
                            tr.wait("sync", S.s_dv2, h[("xdone", n - 2)])
                        c = tr.dma("sync", a.xres1[P][:, :],
                                   stream[n * 128:(n + 1) * 128, :], S.d_x)
                        h[("xld", n)] = c
                        tr.wait("vector", S.d_x, h[("xld", n)])
                        if n >= 2:
                            tr.wait("vector", S.s_ac2, h[("kexp", n - 2)])
                            tr.wait("vector", S.s_pe2, h[("smm", n - 2)])
                            tr.wait("vector", S.s_tr, h[("trlast", n - 2)])
                            tr.wait("vector", S.d_sp, h[("spst", n - 2)])
                        tr.opw("vector", "reduce_sum", a.r1[P][:, :],
                               a.xres1[P][:, :], axis=AX.X, inc=S.s_dv2)
                        tr.opw("vector", "tensor_scalar", a.nmean[P][:, :],
                               a.r1[P][:, :], -1.0 / D, None, ALU.mult,
                               inc=S.s_dv2)
                        c = tr.op("vector", "tensor_scalar", a.cst[P][:, :],
                                  a.xres1[P][:, :], a.nmean[P][:, :], None,
                                  ALU.add, inc=S.s_dv2)
                        h[("xdone", n)] = c
                        tr.wait("scalar", S.s_dv2, c)
                        tr.opw("scalar", "activation", a.sqst[P][:, :],
                               a.cst[P][:, :], AF.Square,
                               accum_out=a.ssq[P][:, :], inc=S.s_ac2)
                        c = tr.op("scalar", "activation", a.sd[P][:, :],
                                  a.ssq[P][:, :], AF.Sqrt, scale=1.0 / D,
                                  bias=a.epsb[:, :], inc=S.s_ac2)
                        tr.wait("vector", S.s_ac2, c)
                        tr.opw("vector", "reciprocal", a.rstd[P][:, :],
                               a.sd[P][:, :], inc=S.s_dv2)
                        cxn = tr.op("vector", "tensor_scalar",
                                    a.xntm[P][:, :], a.cst[P][:, :],
                                    a.rstd[P][:, :], None, ALU.mult,
                                    inc=S.s_dv2)
                        tr.wait("tensor", S.s_dv2, cxn)
                        if n >= 2:
                            tr.wait("tensor", S.s_pe2, h[("qkvl", n - 2)])
                        for k in range(NKC):
                            slot = trk % 2
                            if trk >= 2:
                                tr.wait("tensor", S.s_tev, h[("tev", trk - 2)])
                            c = tr.op("tensor", "transpose", a.pT[slot][:, :],
                                      a.xntm[P][:, k * 128:(k + 1) * 128],
                                      identf[:, :], inc=S.s_tr)
                            h[("tr", trk)] = c
                            tr.wait("vector", S.s_tr, h[("tr", trk)])
                            c = tr.op("vector", "tensor_copy",
                                      a.xnbf1[P][:, k * 128:(k + 1) * 128],
                                      a.pT[slot][:, :], inc=S.s_tev)
                            h[("tev", trk)] = c
                            trk += 1
                        tr.wait("tensor", S.s_tev, h[("tev", trk - 1)])
                        if n == 0:
                            tr.wait("tensor", S.d_w, h["wq"])
                        for qs in range(6):
                            p = qs % 2
                            if n > 0 or qs >= 2:
                                tr.wait("tensor", S.s_dv2,
                                        h[("qev", n if qs >= 2 else n - 1,
                                           qs - 2 if qs >= 2 else qs + 4)])
                            for k in range(NKC):
                                c = tr.op("tensor", "matmul", a.pQ[p][:, :],
                                          a.xnbf1[P][:, k * 128:
                                                     (k + 1) * 128],
                                          a.wqkvT[:, k * 3 * D + qs * 512:
                                                  k * 3 * D + (qs + 1) * 512],
                                          start=(k == 0), stop=(k == NKC - 1),
                                          inc=S.s_pe2
                                          if k == NKC - 1 else None)
                            h[("qkv", n, qs)] = S.s_pe2.count
                            if qs == 5:
                                h[("qkvl", n)] = S.s_pe2.count
                            tr.wait("vector", S.s_pe2, h[("qkv", n, qs)])
                            if qs < 2:
                                dstap = a.qstage[P][:, qs * 512:
                                                   (qs + 1) * 512]
                            elif qs < 4:
                                dstap = a.kstage[P][:, (qs - 2) * 512:
                                                   (qs - 1) * 512]
                            else:
                                dstap = a.vbf[P][:, (qs - 4) * 512:
                                                (qs - 3) * 512]
                            c = tr.op("vector", "tensor_copy", dstap,
                                      a.pQ[p][:, :], inc=S.s_dv2)
                            h[("qev", n, qs)] = c
                        if n >= 2:
                            tr.wait("scalar", S.s_tr, h[("trlast", n - 2)])
                        for hd in range(NH):
                            sl = a.qstage[P][:, hd * HS:(hd + 1) * HS]
                            c = tr.op("vector", "reduce_max", a.qm[P][:, :],
                                      sl, axis=AX.X, negate=True,
                                      inc=S.s_dv2)
                            tr.wait("scalar", S.s_dv2, c)
                            c = tr.op("scalar", "activation",
                                      a.qsm[P][:, hd * HS:(hd + 1) * HS], sl,
                                      AF.Exp, bias=a.qm[P][:, :],
                                      accum_out=a.qsum[P][:, :], inc=S.s_ac2)
                            tr.wait("vector", S.s_ac2, c)
                            tr.opw("vector", "reciprocal", a.qr[P][:, :],
                                   a.qsum[P][:, :], inc=S.s_dv2)
                            tr.opw("vector", "tensor_scalar",
                                   a.qsm[P][:, hd * HS:(hd + 1) * HS],
                                   a.qsm[P][:, hd * HS:(hd + 1) * HS],
                                   a.qr[P][:, :], 0.125, ALU.mult, ALU.mult,
                                   inc=S.s_dv2)
                        c = tr.op("vector", "nop", inc=S.s_dv2)
                        tr.wait("scalar", S.s_dv2, c)
                        c = tr.op("scalar", "activation", a.kexp[P][:, :],
                                  a.kstage[P][:, :], AF.Exp, inc=S.s_ac2)
                        h[("kexp", n)] = c
                        tr.wait("vector", S.s_ac2, c)
                        cvb = tr.op("vector", "tensor_copy",
                                    a.kexpb[P][:, :], a.kexp[P][:, :],
                                    inc=S.s_dv2)
                        tr.wait("tensor", S.s_dv2, cvb)
                        if n > 0:
                            tr.wait("tensor", S.s_dv2, h[("sdone", n - 1)])
                        else:
                            tr.wait("tensor", S.s_dv2, c0)
                        for hd in range(NH):
                            tr.op("tensor", "matmul",
                                  a.pS[0:64, hd * 128:hd * 128 + 64],
                                  a.kexpb[P][:, hd * HS:(hd + 1) * HS],
                                  a.vbf[P][:, hd * HS:(hd + 1) * HS],
                                  start=(hd % 4 == 0), stop=False,
                                  skip_group_check=True)
                            c = tr.op("tensor", "matmul",
                                      a.pS[0:64, hd * 128 + 64:
                                           hd * 128 + 65],
                                      a.kexpb[P][:, hd * HS:(hd + 1) * HS],
                                      a.onesb[:, :], start=False,
                                      stop=(hd % 4 == 3),
                                      skip_group_check=True,
                                      inc=S.s_pe2 if hd == NH - 1 else None)
                        h[("smm", n)] = S.s_pe2.count
                        for hd in range(NH):
                            for srcap, dstt in (
                                    (a.kstage[P][:, hd * HS:(hd + 1) * HS],
                                     a.kcm[P]),
                                    (a.qsm[P][:, hd * HS:(hd + 1) * HS],
                                     a.qscm[P])):
                                slot = trk % 2
                                if trk >= 2:
                                    tr.wait("tensor", S.s_tev,
                                            h[("tev", trk - 2)])
                                c = tr.op("tensor", "transpose",
                                          a.pT[slot][0:64, :], srcap,
                                          identf[:, :], inc=S.s_tr)
                                h[("tr", trk)] = c
                                tr.wait("vector", S.s_tr, h[("tr", trk)])
                                c = tr.op("vector", "tensor_copy",
                                          dstt[:, hd * 128:(hd + 1) * 128],
                                          a.pT[slot][0:64, :], inc=S.s_tev)
                                h[("tev", trk)] = c
                                trk += 1
                        cktr = S.s_tev.count
                        h[("trlast", n)] = S.s_tr.count
                        for hd in range(NH):
                            cr = tr.op("vector", "reduce_max",
                                       a.kmaxc[:, hd:hd + 1],
                                       a.kcm[P][:, hd * 128:(hd + 1) * 128],
                                       axis=AX.X,
                                       inc=S.s_dv2 if hd == NH - 1 else None)
                        tr.wait("vector", S.s_dv2, cr)
                        tr.opw("vector", "tensor_tensor", kmaxloc[:, :],
                               kmaxloc[:, :], a.kmaxc[:, :], ALU.max,
                               inc=S.s_dv2)
                        tr.wait("vector", S.s_pe2, h[("smm", n)])
                        if n >= 2:
                            tr.wait("vector", S.d_sp, h[("spst", n - 2)])
                        tr.opw("vector", "tensor_copy", a.schst[P][:, :],
                               a.pS[:, :], inc=S.s_dv2)
                        c = tr.op("vector", "tensor_tensor", Stot[:, :],
                                  Stot[:, :], a.schst[P][:, :], ALU.add,
                                  inc=S.s_dv2)
                        h[("sdone", n)] = c
                        tr.wait("gpsimd", S.s_tev, cktr)
                        tr.dma("gpsimd", kcm_d[n, :, :], a.kcm[P][:, :],
                               S.d_sp)
                        tr.dma("gpsimd", qscm_d[n, :, :], a.qscm[P][:, :],
                               S.d_sp)
                        tr.wait("gpsimd", S.s_dv2, h[("qev", n, 5)])
                        tr.dma("gpsimd", vtm_d[n * 128:(n + 1) * 128, :],
                               a.vbf[P][:, :], S.d_sp)
                        tr.wait("gpsimd", S.s_dv2, h[("sdone", n)])
                        c = tr.dma("gpsimd", schunks_d[n, :, :],
                                   a.schst[P][:, :], S.d_sp)
                        h[("spst", n)] = c
                    tr.wait("gpsimd", S.d_sp, S.d_sp.count)
                    cl = tr.op("vector", "nop", inc=S.s_dv2)
                    tr.wait("gpsimd", S.s_dv2, cl)
                    tr.dma("gpsimd", ccsrc_d[:, 0:16], kmaxloc[:, :], S.d_sp)
                    c = tr.dma("gpsimd", ccsrc_d[:, 16:CCW], Stot[:, :],
                               S.d_sp)
                    tr.wait("gpsimd", S.d_sp, c)
                    tr.op("gpsimd", "collective_compute", "AllGather",
                          ALU.bypass,
                          replica_groups=[list(range(NCORES))],
                          ins=[ccsrc_d.ap().opt()], outs=[ccdst_d.ap().opt()],
                          inc=S.cc_s)
                    tr.wait("gpsimd", S.cc_s, 1)

                run_phase(nc, allsems, emit_p2a)

            with ExitStack() as c3:
                b2 = NS()
                b2.woutT = sb(c3, "b_woutT", [128, NKC * D], BF16)
                dblb = lambda nm, shape, dt=F32: [
                    sb(c3, "b_" + nm + str(i), shape, dt) for i in range(2)]
                b2.gath = sb(c3, "b_gath", [64, CCW])
                b2.Scar = sb(c3, "b_Scar", [64, NH * 128])
                b2.Scarb = sb(c3, "b_Scarb", [64, NH * HS], BF16)
                b2.kcm = dblb("kcm", [64, NH * 128])
                b2.kexp = dblb("kexp", [64, NH * 128])
                b2.kexpb = dblb("kexpb", [64, NH * 128], BF16)
                b2.qscm = dblb("qscm", [64, NH * 128])
                b2.qtb = dblb("qtb", [64, NH * 128], BF16)
                b2.zfull = dblb("zfull", [64, NH * 128])
                b2.ztmp = dblb("ztmp", [64, NH * 128])
                b2.schk = dblb("schk", [64, NH * 128])
                b2.vchunk = dblb("vchunk", [128, D], BF16)
                b2.ofm = dblb("ofm", [128, D], BF16)
                b2.xres1 = dblb("xres1", [128, D])
                b2.abf = [sb(c3, "b_abf%d" % i, [128, 128], BF16)
                          for i in range(2)]
                b2.ones64 = sb(c3, "b_ones64", [64, 128])
                b2.pA = [ps(c3, "b_pA%d" % i, [128, 128]) for i in range(2)]
                b2.pO = ps(c3, "b_pO", [64, NH * 128])
                b2.pXO = [ps(c3, "b_pXO%d" % i, [128, 512]) for i in range(2)]

                def emit_p2b(tr):
                    h = tr.hist
                    tr.wait("sync", S.d_wc, xcounts.get(("wc_out",)))
                    for k in range(NKC):
                        c = tr.dma("sync", b2.woutT[:, k * D:(k + 1) * D],
                                   wout_bf[0:D, k * 128:(k + 1) * 128],
                                   S.d_w, transpose=True)
                    h["wo"] = c
                    tr.op("vector", "memset", b2.ones64[:, :], 1.0)
                    tr.op("vector", "memset", b2.Scar[:, :], 0.0)
                    tr.op("vector", "memset", kmaxloc[:, :], -1e38)
                    cm = tr.op("vector", "nop", inc=S.s_dv2)
                    for r in range(NCORES):
                        tr.wait("gpsimd", S.s_dv2,
                                cm if r == 0 else h[("gproc", r - 1)])
                        c = tr.dma("gpsimd", b2.gath[:, :],
                                   ccdst_d[r * 64:(r + 1) * 64, :], S.d_x)
                        h[("gld", r)] = c
                        tr.wait("vector", S.d_x, h[("gld", r)])
                        tr.opw("vector", "tensor_scalar", b2.kcm[0][:, 0:16],
                               b2.gath[:, 0:16], seqb[:, r:r + 1], None,
                               ALU.add, inc=S.s_dv2)
                        tr.opw("vector", "tensor_tensor", kmaxloc[:, :],
                               kmaxloc[:, :], b2.kcm[0][:, 0:16], ALU.max,
                               inc=S.s_dv2)
                        c = tr.opw("vector", "scalar_tensor_tensor",
                                   b2.Scar[:, :], b2.gath[:, 16:CCW],
                                   prefm[:, r:r + 1], b2.Scar[:, :],
                                   ALU.mult, ALU.add, inc=S.s_dv2)
                        h[("gproc", r)] = c
                    c = tr.op("vector", "nop", inc=S.s_dv2)
                    tr.wait("scalar", S.s_dv2, c)
                    c = tr.op("scalar", "activation", escale[:, :],
                              kmaxloc[:, :], AF.Exp, scale=-1.0, inc=S.s_ac2)
                    tr.wait("vector", S.s_ac2, c)
                    tr.opw("vector", "tensor_scalar", negm[:, :],
                           kmaxloc[:, :], -1.0, None, ALU.mult, inc=S.s_dv2)
                    for hd in range(NH):
                        crs = tr.op("vector", "tensor_scalar",
                                    b2.Scar[:, hd * 128:(hd + 1) * 128],
                                    b2.Scar[:, hd * 128:(hd + 1) * 128],
                                    escale[:, hd:hd + 1], None, ALU.mult,
                                    inc=S.s_dv2 if hd == NH - 1 else None)
                    tr.wait("vector", S.s_dv2, crs)
                    for hd in range(NH):
                        c = tr.op("vector", "tensor_copy",
                                  b2.Scarb[:, hd * HS:(hd + 1) * HS],
                                  b2.Scar[:, hd * 128:hd * 128 + 64],
                                  inc=S.s_dv2 if hd == NH - 1 else None)
                    h[("scb", -1)] = c

                    for n in range(NCH):
                        P = n % 2
                        if n == 0:
                            tr.wait("gpsimd", S.s_dv2,
                                    h[("gproc", NCORES - 1)])
                        if n >= 2:
                            tr.wait("gpsimd", S.s_dv2, h[("csc", n - 2)])
                        c = tr.dma("gpsimd", b2.kcm[P][:, :], kcm_d[n, :, :],
                                   S.d_x)
                        if n >= 2:
                            tr.wait("gpsimd", S.s_dv2, h[("qtb", n - 2)])
                        c = tr.dma("gpsimd", b2.qscm[P][:, :],
                                   qscm_d[n, :, :], S.d_x)
                        if n >= 2:
                            tr.wait("gpsimd", S.s_dv2, h[("sup", n - 2)])
                        c = tr.dma("gpsimd", b2.schk[P][:, :],
                                   schunks_d[n, :, :], S.d_x)
                        if n >= 2:
                            tr.wait("gpsimd", S.s_pe2, h[("omm", n - 2)])
                        c = tr.dma("gpsimd", b2.vchunk[P][:, :],
                                   vtm_d[n * 128:(n + 1) * 128, :], S.d_x)
                        if n >= 2:
                            tr.wait("gpsimd", S.d_st, h[("stc", n - 2)])
                        c = tr.dma("gpsimd", b2.xres1[P][:, :],
                                   stream[n * 128:(n + 1) * 128, :], S.d_x)
                        h[("ld", n)] = c
                        tr.wait("scalar", S.d_x, h[("ld", n)])
                        if n == 0:
                            tr.wait("scalar", S.s_dv2, h[("scb", -1)])
                        if n >= 2:
                            tr.wait("scalar", S.s_dv2, h[("csc", n - 2)])
                        for hd in range(NH):
                            c = tr.op("scalar", "activation",
                                      b2.kexp[P][:, hd * 128:(hd + 1) * 128],
                                      b2.kcm[P][:, hd * 128:(hd + 1) * 128],
                                      AF.Exp, bias=negm[:, hd:hd + 1],
                                      inc=S.s_ac2 if hd == NH - 1 else None)
                        h[("kexp", n)] = c
                        tr.wait("vector", S.s_ac2, h[("kexp", n)])
                        if n >= 2:
                            tr.wait("vector", S.s_pe2, h[("omm", n - 2)])
                        ckb = tr.op("vector", "tensor_copy",
                                    b2.kexpb[P][:, :], b2.kexp[P][:, :],
                                    inc=S.s_dv2)
                        for hd in range(NH):
                            csc = tr.op("vector", "tensor_tensor_scan",
                                        b2.zfull[P][:, hd * 128:
                                                    (hd + 1) * 128],
                                        b2.ones64[:, :],
                                        b2.kexp[P][:, hd * 128:
                                                   (hd + 1) * 128],
                                        b2.Scar[:, hd * 128 + 64:
                                                hd * 128 + 65],
                                        ALU.mult, ALU.add,
                                        inc=S.s_dv2 if hd == NH - 1 else None)
                        h[("csc", n)] = csc
                        tr.wait("vector", S.s_dv2, csc)
                        tr.opw("vector", "tensor_scalar", b2.ztmp[P][:, :],
                               b2.zfull[P][:, :], 1e-9, None, ALU.add,
                               inc=S.s_dv2)
                        tr.opw("vector", "reciprocal", b2.ztmp[P][:, :],
                               b2.ztmp[P][:, :], inc=S.s_dv2)
                        tr.wait("vector", S.d_x, h[("ld", n)])
                        cqt = tr.op("vector", "tensor_tensor",
                                    b2.qtb[P][:, :], b2.qscm[P][:, :],
                                    b2.ztmp[P][:, :], ALU.mult, inc=S.s_dv2)
                        h[("qtb", n)] = cqt
                        for hd in range(NH + 1):
                            if hd < NH:
                                slot = hd % 2
                                tr.wait("tensor", S.s_dv2, cqt)
                                if hd >= 2:
                                    tr.wait("tensor", S.s_dv2,
                                            h[("am", n, hd - 2)])
                                c = tr.op("tensor", "matmul",
                                          b2.pA[slot][:, :],
                                          b2.kexpb[P][:, hd * 128:
                                                      (hd + 1) * 128],
                                          b2.qtb[P][:, hd * 128:
                                                   (hd + 1) * 128],
                                          start=True, stop=True,
                                          skip_group_check=True, inc=S.s_pe2)
                                h[("amm", n, hd)] = c
                                tr.wait("vector", S.s_pe2, h[("amm", n, hd)])
                                c = tr.op("vector", "tensor_tensor",
                                          b2.abf[slot][:, :],
                                          b2.pA[slot][:, :],
                                          umaskf[:, :], ALU.mult, inc=S.s_dv2)
                                h[("am", n, hd)] = c
                            if hd >= 1:
                                hp = hd - 1
                                slotp = hp % 2
                                tr.wait("tensor", S.s_dv2, h[("am", n, hp)])
                                if n > 0 and hp == 0:
                                    tr.wait("tensor", S.s_dv2,
                                            h[("ofm", n - 1, NH - 1)])
                                tr.op("tensor", "matmul",
                                      b2.pO[0:64, hp * 128:(hp + 1) * 128],
                                      b2.Scarb[:, hp * HS:(hp + 1) * HS],
                                      b2.qtb[P][:, hp * 128:(hp + 1) * 128],
                                      start=(hp % 4 == 0), stop=False,
                                      skip_group_check=True)
                                c = tr.op("tensor", "matmul",
                                          b2.pO[0:64, hp * 128:
                                                (hp + 1) * 128],
                                          b2.vchunk[P][:, hp * HS:
                                                       (hp + 1) * HS],
                                          b2.abf[slotp][:, :],
                                          start=False, stop=(hp % 4 == 3),
                                          skip_group_check=True, inc=S.s_pe2)
                                h[("opair", n, hp)] = c
                                if hp % 4 == 3:
                                    tr.wait("vector", S.s_pe2,
                                            h[("opair", n, hp)])
                                    if hp == 3 and n >= 2:
                                        tr.wait("vector", S.s_pe2,
                                                h[("xomm", n - 2, 1)])
                                    for hq in range(hp - 3, hp + 1):
                                        sub = hq % 2
                                        kd = hq // 2
                                        c = tr.op("vector", "tensor_copy",
                                                  b2.ofm[P][sub * 64:
                                                            (sub + 1) * 64,
                                                            kd * 128:
                                                            (kd + 1) * 128],
                                                  b2.pO[0:64, hq * 128:
                                                        (hq + 1) * 128],
                                                  inc=S.s_dv2)
                                        h[("ofm", n, hq)] = c
                                        tr.op("vector",
                                              "scalar_tensor_tensor",
                                              b2.Scar[:, hq * 128:
                                                      (hq + 1) * 128],
                                              b2.schk[P][:, hq * 128:
                                                         (hq + 1) * 128],
                                              escale[:, hq:hq + 1],
                                              b2.Scar[:, hq * 128:
                                                      (hq + 1) * 128],
                                              ALU.mult, ALU.add)
                        h[("omm", n)] = S.s_pe2.count
                        cupd = tr.op("vector", "nop", inc=S.s_dv2)
                        tr.wait("vector", S.s_dv2, cupd)
                        for hd in range(NH):
                            c = tr.op("vector", "tensor_copy",
                                      b2.Scarb[:, hd * HS:(hd + 1) * HS],
                                      b2.Scar[:, hd * 128:hd * 128 + 64],
                                      inc=S.s_dv2 if hd == NH - 1 else None)
                        h[("sup", n)] = c
                        if n == 0:
                            tr.wait("tensor", S.d_w, h["wo"])
                        for half in range(2):
                            tr.wait("tensor", S.s_dv2, h[("ofm", n, NH - 1)])
                            if n > 0:
                                tr.wait("tensor", S.s_dv2,
                                        h[("xadd", n - 1, half)])
                            for kd in range(NKC):
                                c = tr.op("tensor", "matmul",
                                          b2.pXO[half][:, :],
                                          b2.ofm[P][:, kd * 128:
                                                    (kd + 1) * 128],
                                          b2.woutT[:, kd * D + half * 512:
                                                   kd * D + (half + 1) * 512],
                                          start=(kd == 0),
                                          stop=(kd == NKC - 1),
                                          inc=S.s_pe2
                                          if kd == NKC - 1 else None)
                            h[("xomm", n, half)] = S.s_pe2.count
                            tr.wait("vector", S.s_pe2, h[("xomm", n, half)])
                            c = tr.op("vector", "tensor_tensor",
                                      b2.xres1[P][:, half * 512:
                                                  (half + 1) * 512],
                                      b2.xres1[P][:, half * 512:
                                                  (half + 1) * 512],
                                      b2.pXO[half][:, :], ALU.add,
                                      inc=S.s_dv2)
                            h[("xadd", n, half)] = c
                        tr.wait("gpsimd", S.s_dv2, h[("xadd", n, 1)])
                        c = tr.dma("gpsimd", stream[n * 128:(n + 1) * 128, :],
                                   b2.xres1[P][:, :], S.d_st)
                        h[("stc", n)] = c
                    tr.wait("gpsimd", S.d_st, S.d_st.count)

                run_phase(nc, allsems, emit_p2b)


        if "p3" in phases:
            with ExitStack() as c4:
                f3 = make_ffn_tensors(c4, "f3_")
                run_phase(nc, allsems,
                          lambda tr: emit_ffn(tr, f3, stream, y_out))
        elif "copy" in phases:
            def emit_copy(tr):
                c = tr.dma("gpsimd", y_out[:, :], stream[:, :], S.d_st)
                tr.wait("gpsimd", S.d_st, c)

            run_phase(nc, allsems, emit_copy)

    return nc


def host_consts():
    identf = np.eye(128, dtype=np.float32)
    umask = np.triu(np.ones((128, 128), dtype=np.float32))
    slotb = np.broadcast_to(
        (np.arange(E, dtype=np.float32) * CAP - 1.0)[None, :],
        (128, E)).copy()
    prefm = []
    seqb = []
    for c in range(NCORES):
        pm = np.zeros((64, 8), dtype=np.float32)
        sb_ = np.full((64, 8), -1e38, dtype=np.float32)
        for r in range(NCORES):
            if r // 4 == c // 4:
                sb_[:, r] = 0.0
                if r % 4 < c % 4:
                    pm[:, r] = 1.0
        prefm.append(pm)
        seqb.append(sb_)
    return identf, umask, slotb, prefm, seqb


def kernel(L0_attn_x, L0_attn_W_qkv, L0_attn_W_out, L0_ffn_gate_w,
           L0_ffn_w1, L0_ffn_w3, L0_ffn_w2):
    x = np.ascontiguousarray(np.asarray(L0_attn_x, dtype=np.float32))
    wqkv = np.ascontiguousarray(np.asarray(L0_attn_W_qkv, dtype=np.float32))
    wout = np.ascontiguousarray(np.asarray(L0_attn_W_out, dtype=np.float32))
    gw = np.ascontiguousarray(np.asarray(L0_ffn_gate_w, dtype=np.float32))
    w1 = np.ascontiguousarray(np.asarray(L0_ffn_w1, dtype=np.float32))
    w3 = np.ascontiguousarray(np.asarray(L0_ffn_w3, dtype=np.float32))
    w2 = np.ascontiguousarray(np.asarray(L0_ffn_w2, dtype=np.float32))

    identf, umask, slotb, prefm, seqb = host_consts()
    nc = build_program()
    in_maps = []
    for c in range(NCORES):
        b = c // 4
        t0 = (c % 4) * TPC
        in_maps.append({
            "x": np.ascontiguousarray(x[b, t0:t0 + TPC, :]),
            "wqkv": wqkv, "wout": wout, "gw": gw,
            "w1": w1, "w3": w3, "w2": w2,
            "identf": identf, "umask": umask, "slotb": slotb,
            "prefm": prefm[c], "seqb": seqb[c],
        })
    res = run_bass_kernel_spmd(nc, in_maps, core_ids=list(range(NCORES)))
    out = np.empty((B, T, D), dtype=np.float32)
    for c in range(NCORES):
        b = c // 4
        t0 = (c % 4) * TPC
        out[b, t0:t0 + TPC, :] = np.asarray(res.results[c]["y"],
                                            dtype=np.float32)
    return out


# version 06
# speedup vs baseline: 1.6684x; 1.0454x over previous
import numpy as np
import ml_dtypes
from contextlib import ExitStack

import concourse.bass as bass
import concourse.mybir as mybir
from concourse.bass_utils import run_bass_kernel_spmd

F32 = mybir.dt.float32
BF16 = mybir.dt.bfloat16
AF = mybir.ActivationFunctionType
ALU = mybir.AluOpType
AX = mybir.AxisListType

NCORES = 8
B, T, D = 2, 16384, 1024
NH, HS = 16, 64
E, HID = 8, 2816
TPC = T * B // NCORES
TT = 1024
NTILE = TPC // TT
NTC = TT // 128
NKC = D // 128
NHC = HID // 128
HGROUPS = [(0, 8), (8, 16), (16, 22)]
HBLOCKS = [(0, 4), (4, 8), (8, 12), (12, 16), (16, 20), (20, 22)]
NCH = TPC // 128
LNEPS = 1e-5
CAP = 1280
ZPAD = 2048
SENT = E * CAP
I32 = mybir.dt.int32

ENGS = ("sync", "gpsimd", "vector", "tensor", "scalar")


class Sem:
    __slots__ = ("h", "count")

    def __init__(self, h):
        self.h = h
        self.count = 0


class Tr:

    def __init__(self, active, eng):
        self.a = active
        self.e = eng
        self.hist = {}

    def op(self, name, fn, *args, inc=None, incn=1, **kw):
        if self.a == name:
            inst = getattr(self.e, fn)(*args, **kw)
            if inc is not None:
                inst.then_inc(inc.h, incn)
        if inc is not None:
            inc.count += incn
            return inc.count
        return None

    def wait(self, name, sem, val):
        if val is None or val <= 0:
            return
        if self.a == name:
            self.e.wait_ge(sem.h, val)

    def dma(self, name, out, in_, sem, **kw):
        return self.op(name, "dma_start", inc=sem, incn=16, out=out, in_=in_, **kw)

    def opw(self, name, fn, *args, inc, incn=1, **kw):
        c = self.op(name, fn, *args, inc=inc, incn=incn, **kw)
        self.wait(name, inc, c)
        return c


def run_phase(nc, sems, emit):
    base = [(s, s.count) for s in sems]
    with nc.Block() as block:
        for name in ENGS:
            for s, c in base:
                s.count = c

            def body(eng, _n=name):
                emit(Tr(_n, eng))

            getattr(block, name)(body)


class NS:
    pass


def build_program(phases=("p0", "p1", "p2", "p3")):
    nc = bass.Bass("TRN2", target_bir_lowering=False)
    t = NS()

    x_in = nc.dram_tensor("x", [TPC, D], F32, kind="ExternalInput")
    wqkv_in = nc.dram_tensor("wqkv", [3 * D, D], F32, kind="ExternalInput")
    wout_in = nc.dram_tensor("wout", [D, D], F32, kind="ExternalInput")
    gw_in = nc.dram_tensor("gw", [E, D], F32, kind="ExternalInput")
    w1_in = nc.dram_tensor("w1", [E, HID, D], F32, kind="ExternalInput")
    w3_in = nc.dram_tensor("w3", [E, HID, D], F32, kind="ExternalInput")
    w2_in = nc.dram_tensor("w2", [E, D, HID], F32, kind="ExternalInput")
    identf_in = nc.dram_tensor("identf", [128, 128], F32, kind="ExternalInput")
    umask_in = nc.dram_tensor("umask", [128, 128], F32, kind="ExternalInput")
    prefm_in = nc.dram_tensor("prefm", [64, 8], F32, kind="ExternalInput")
    seqb_in = nc.dram_tensor("seqb", [64, 8], F32, kind="ExternalInput")
    slotb_in = nc.dram_tensor("slotb", [128, 8], F32, kind="ExternalInput")
    y_out = nc.dram_tensor("y", [TPC, D], F32, kind="ExternalOutput")

    w1_bf = nc.dram_tensor("w1_bf", [E, HID, D], BF16, kind="Internal")
    w3_bf = nc.dram_tensor("w3_bf", [E, HID, D], BF16, kind="Internal")
    w2_bf = nc.dram_tensor("w2_bf", [E, D, HID], BF16, kind="Internal")
    wqkv_bf = nc.dram_tensor("wqkv_bf", [3 * D, D], BF16, kind="Internal")
    wout_bf = nc.dram_tensor("wout_bf", [D, D], BF16, kind="Internal")
    stream = nc.dram_tensor("stream", [TPC, D], F32, kind="Internal")
    kcm_d = nc.dram_tensor("kcm_d", [NCH, 64, NH * 128], F32, kind="Internal")
    qscm_d = nc.dram_tensor("qscm_d", [NCH, 64, NH * 128], F32, kind="Internal")
    vtm_d = nc.dram_tensor("vtm_d", [TPC, D], BF16, kind="Internal")
    schunks_d = nc.dram_tensor("schunks_d", [NCH, 64, NH * 128], F32, kind="Internal")
    CCW = 16 + NH * 128
    ccsrc_d = nc.dram_tensor("ccsrc_d", [64, CCW], F32, kind="Internal")
    ccdst_d = nc.dram_tensor("ccdst_d", [64 * NCORES, CCW], F32, kind="Internal")
    xnbf_d = nc.dram_tensor("xnbf_d", [TPC, D], F32, kind="Internal")
    xg_d = nc.dram_tensor("xg_d", [E * CAP + 1, D], F32, kind="Internal")
    y_d = nc.dram_tensor("y_d", [E * CAP + ZPAD, D], F32, kind="Internal")

    with ExitStack() as top:
        def sb(ctx, name, shape, dt=F32):
            return ctx.enter_context(nc.sbuf_tensor(name, shape, dt))

        def ps(ctx, name, shape, dt=F32):
            return ctx.enter_context(nc.psum_tensor(name, shape, dt))

        identf = sb(top, "identf_s", [128, 128])
        umaskf = sb(top, "umaskf_s", [128, 128])
        prefm = sb(top, "prefm_s", [64, 8])
        seqb = sb(top, "seqb_s", [64, 8])
        gwsb = sb(top, "gwsb", [8, D])
        slotbase = sb(top, "slotbase_s", [128, 8])
        gwT = sb(top, "gwT", [128, 8 * E])
        Stot = sb(top, "Stot", [64, NH * 128])
        kmaxloc = sb(top, "kmaxloc", [64, NH])
        escale = sb(top, "escale", [64, NH])
        negm = sb(top, "negm", [64, NH])

        semnames = ["d_cin", "d_wc", "d_w", "d_x", "d_st", "d_sp", "cc_s",
                    "s_tr", "s_tev", "s_amm", "s_silu", "s_hm", "s_bmm",
                    "s_bep", "s_w13c", "s_w2c", "s_gmm", "s_dv", "s_ac",
                    "s_pe2", "s_dv2", "s_ac2", "s_misc"]
        sems = {n: Sem(top.enter_context(nc.semaphore(n))) for n in semnames}
        S = NS()
        for n, s in sems.items():
            setattr(S, n, s)
        allsems = list(sems.values())

        if "p0" in phases:
            with ExitStack() as c0:
                pg0 = ps(c0, "pg0", [128, 16])
                pg1 = ps(c0, "pg1", [128, 16])
                z0 = sb(c0, "z0", [128, D])

                def emit_p0(tr):
                    for ap_out, ap_in in [(identf[:, :], identf_in[:, :]),
                                          (umaskf[:, :], umask_in[:, :]),
                                          (prefm[:, :], prefm_in[:, :]),
                                          (seqb[:, :], seqb_in[:, :]),
                                          (gwsb[:, :], gw_in[:, :]),
                                          (slotbase[:, :], slotb_in[:, :])]:
                        tr.dma("sync", ap_out, ap_in, S.d_cin)
                    cin_done = S.d_cin.count
                    tr.wait("tensor", S.d_cin, cin_done)
                    for k in range(NKC):
                        pg = pg0 if k % 2 == 0 else pg1
                        if k >= 2:
                            tr.wait("tensor", S.s_tev, tr.hist[("gev", k - 2)])
                        c = tr.op("tensor", "transpose", pg[:, 0:8],
                                  gwsb[0:8, k * 128:(k + 1) * 128],
                                  identf[0:8, 0:8], inc=S.s_tr)
                        tr.hist[("gtr", k)] = c
                        tr.wait("vector", S.s_tr, tr.hist[("gtr", k)])
                        c = tr.op("vector", "tensor_copy",
                                  gwT[:, k * 8:(k + 1) * 8], pg[:, 0:8],
                                  inc=S.s_tev)
                        tr.hist[("gev", k)] = c
                    tr.op("vector", "memset", Stot[:, :], 0.0)
                    tr.op("vector", "memset", kmaxloc[:, :], -1e38)
                    cz = tr.op("vector", "memset", z0[:, :], 0.0, inc=S.s_dv)
                    tr.wait("gpsimd", S.s_dv, cz)
                    for i in range(ZPAD // 128):
                        tr.dma("gpsimd",
                               y_d[E * CAP + i * 128:E * CAP + (i + 1) * 128, :],
                               z0[:, :], S.d_st)

                run_phase(nc, allsems, emit_p0)

        def make_ffn_tensors(ctx, pfx):
            f = NS()
            f.xstg = sb(ctx, pfx + "xstg", [128, D])
            f.cst = sb(ctx, pfx + "cst", [128, D])
            f.sqst = sb(ctx, pfx + "sqst", [128, D])
            f.xntm = sb(ctx, pfx + "xntm", [128, D])
            f.xnf = sb(ctx, pfx + "xnf", [128, D])
            f.coef = sb(ctx, pfx + "coef", [128, NCH * E])
            f.maskcm = sb(ctx, pfx + "maskcm", [8, TPC])
            f.zeros8 = sb(ctx, pfx + "zeros8", [8, 1024])
            f.postm = sb(ctx, pfx + "postm", [128, 8])
            f.mtm = sb(ctx, pfx + "mtm", [128, 8])
            f.ovf = sb(ctx, pfx + "ovf", [128, 8])
            f.m2t = sb(ctx, pfx + "m2t", [128, 8])
            f.slf = sb(ctx, pfx + "slf", [128, 8])
            f.sl2 = sb(ctx, pfx + "sl2", [128, 8])
            f.mm1 = sb(ctx, pfx + "mm1", [128, 8])
            f.cm1 = sb(ctx, pfx + "cm1", [128, 8])
            f.o1 = sb(ctx, pfx + "o1", [128, 1])
            f.o2 = sb(ctx, pfx + "o2", [128, 1])
            f.oc = sb(ctx, pfx + "oc", [128, 1])
            f.c12 = sb(ctx, pfx + "c12", [128, 2 * NCH])
            f.sIdx = sb(ctx, pfx + "sIdx", [128, 2 * NCH], I32)
            f.gIdx = sb(ctx, pfx + "gIdx", [128, 2 * NCH], I32)
            f.xgld = sb(ctx, pfx + "xgld", [128, D])
            f.xgfm = sb(ctx, pfx + "xgfm", [128, NKC * CAP], BF16)
            f.hbuf = sb(ctx, pfx + "hbuf", [128, 8 * CAP], BF16)
            f.outacc = sb(ctx, pfx + "outacc", [128, (CAP // 128) * D])
            f.w1b = [sb(ctx, pfx + "w1b%d" % i, [128, NKC * 512], BF16)
                     for i in range(2)]
            f.w3b = [sb(ctx, pfx + "w3b%d" % i, [128, NKC * 512], BF16)
                     for i in range(2)]
            f.w2b = sb(ctx, pfx + "w2b", [128, 8 * D], BF16)
            f.silu = [sb(ctx, pfx + "silu%d" % i, [128, 512]) for i in range(2)]
            f.g1g = sb(ctx, pfx + "g1g", [128, D])
            f.g2g = sb(ctx, pfx + "g2g", [128, D])
            f.ostg = sb(ctx, pfx + "ostg", [128, D])
            f.r1 = sb(ctx, pfx + "r1", [128, 1])
            f.epsb = sb(ctx, pfx + "epsb", [128, 1])
            f.nmean = sb(ctx, pfx + "nmean", [128, 1])
            f.ssq = sb(ctx, pfx + "ssq", [128, 1])
            f.sd = sb(ctx, pfx + "sd", [128, 1])
            f.rstd = sb(ctx, pfx + "rstd", [128, 1])
            f.m1n = sb(ctx, pfx + "m1n", [128, 1])
            f.m2 = sb(ctx, pfx + "m2", [128, 1])
            f.d1 = sb(ctx, pfx + "d1", [128, 1])
            f.r2 = sb(ctx, pfx + "r2", [128, 1])
            f.gst = sb(ctx, pfx + "gst", [128, 8])
            f.ge1 = sb(ctx, pfx + "ge1", [128, 8])
            f.tmp8 = sb(ctx, pfx + "tmp8", [128, 8])
            f.gm = sb(ctx, pfx + "gm", [128, 8])
            f.pH1 = [ps(ctx, pfx + "pH1%d" % i, [128, 512]) for i in range(2)]
            f.pH3 = [ps(ctx, pfx + "pH3%d" % i, [128, 512]) for i in range(2)]
            f.pB = [ps(ctx, pfx + "pB%d" % i, [128, 512]) for i in range(2)]
            f.pT = [ps(ctx, pfx + "pT%d" % i, [128, 128]) for i in range(2)]
            return f

        xcounts = {}

        def emit_ffn(tr, f, src, dst, do_casts=False):
            h = tr.hist
            w13c0 = S.s_w13c.count
            w2c0 = S.s_w2c.count
            if do_casts:
                for e in range(E):
                    tr.dma("gpsimd", w1_bf[e, :, :], w1_in[e, :, :], S.d_wc)
                    tr.dma("gpsimd", w3_bf[e, :, :], w3_in[e, :, :], S.d_wc)
                    c = tr.dma("gpsimd", w2_bf[e, :, :], w2_in[e, :, :], S.d_wc)
                    xcounts[("wc", e)] = c
                xcounts[("wc_qkv",)] = tr.dma("gpsimd", wqkv_bf[:, :],
                                              wqkv_in[:, :], S.d_wc)
                xcounts[("wc_out",)] = tr.dma("gpsimd", wout_bf[:, :],
                                              wout_in[:, :], S.d_wc)
            xgf0 = S.s_misc.count
            tr.op("vector", "memset", f.epsb[:, :], LNEPS, inc=S.s_dv)
            tr.op("vector", "memset", f.zeros8[:, :], 0.0, inc=S.s_dv)
            trk = 0

            for tc in range(NCH):
                if tc > 0:
                    tr.wait("sync", S.s_dv, h[("lnd", tc - 1)])
                c = tr.dma("sync", f.xstg[:, :],
                           src[tc * 128:(tc + 1) * 128, :], S.d_x)
                h[("xld", tc)] = c
                tr.wait("vector", S.d_x, h[("xld", tc)])
                tr.opw("vector", "reduce_sum", f.r1[:, :], f.xstg[:, :],
                       axis=AX.X, inc=S.s_dv)
                tr.opw("vector", "tensor_scalar", f.nmean[:, :], f.r1[:, :],
                       -1.0 / D, None, ALU.mult, inc=S.s_dv)
                c = tr.op("vector", "tensor_scalar", f.cst[:, :], f.xstg[:, :],
                          f.nmean[:, :], None, ALU.add, inc=S.s_dv)
                h[("lnd", tc)] = c
                tr.opw("vector", "scalar_tensor_tensor", f.sqst[:, :],
                       f.cst[:, :], 1.0, f.cst[:, :], ALU.mult, ALU.mult,
                       accum_out=f.ssq[:, :], inc=S.s_dv)
                tr.wait("scalar", S.s_dv, max(c, S.s_dv.count))
                tr.opw("scalar", "activation", f.sd[:, :], f.ssq[:, :],
                       AF.Ln, scale=1.0 / D, bias=f.epsb[:, :], inc=S.s_ac)
                c = tr.op("scalar", "activation", f.rstd[:, :], f.sd[:, :],
                          AF.Exp, scale=-0.5, inc=S.s_ac)
                tr.wait("vector", S.s_ac, c)
                if tc > 0:
                    tr.wait("vector", S.s_tr, h[("trl", tc - 1)])
                    tr.wait("vector", S.d_sp, h[("xnst", tc - 1)])
                cxn = tr.op("vector", "tensor_scalar", f.xntm[:, :],
                            f.cst[:, :], f.rstd[:, :], None, ALU.mult,
                            inc=S.s_dv)
                tr.wait("sync", S.s_dv, cxn)
                c = tr.dma("sync", xnbf_d[tc * 128:(tc + 1) * 128, :],
                           f.xntm[:, :], S.d_sp)
                h[("xnst", tc)] = c
                tr.wait("tensor", S.s_dv, cxn)
                for k in range(NKC):
                    slot = trk % 2
                    if trk >= 2:
                        tr.wait("tensor", S.s_tev, h[("tev", trk - 2)])
                    c = tr.op("tensor", "transpose", f.pT[slot][:, :],
                              f.xntm[:, k * 128:(k + 1) * 128],
                              identf[:, :], inc=S.s_tr)
                    h[("tr", trk)] = c
                    tr.wait("vector", S.s_tr, h[("tr", trk)])
                    c = tr.op("vector", "tensor_copy",
                              f.xnf[:, k * 128:(k + 1) * 128],
                              f.pT[slot][:, :], inc=S.s_tev)
                    h[("tev", trk)] = c
                    trk += 1
                tr.wait("tensor", S.s_tev, h[("tev", trk - 1)])
                if tc > 0:
                    tr.wait("tensor", S.s_ac, h[("gex", tc - 1)])
                for k in range(NKC):
                    tr.op("tensor", "matmul", f.pB[0][:, 0:8],
                          f.xnf[:, k * 128:(k + 1) * 128],
                          gwT[:, k * 8:(k + 1) * 8],
                          start=(k == 0), stop=(k == NKC - 1),
                          skip_group_check=True,
                          inc=S.s_gmm if k == NKC - 1 else None)
                h[("g", tc)] = S.s_gmm.count
                tr.wait("vector", S.s_gmm, h[("g", tc)])
                c = tr.op("vector", "reduce_max", f.m1n[:, :],
                          f.pB[0][:, 0:8], axis=AX.X, negate=True, inc=S.s_dv)
                tr.wait("scalar", S.s_dv, c)
                c = tr.op("scalar", "activation", f.gst[:, :],
                          f.pB[0][:, 0:8], AF.Exp, bias=f.m1n[:, :],
                          inc=S.s_ac)
                h[("gex", tc)] = c
                tr.wait("vector", S.s_ac, c)
                tr.opw("vector", "tensor_single_scalar", f.ge1[:, :],
                       f.gst[:, :], 1.0, ALU.is_ge, inc=S.s_dv)
                tr.opw("vector", "scalar_tensor_tensor", f.tmp8[:, :],
                       f.ge1[:, :], -1e30, f.gst[:, :], ALU.mult, ALU.add,
                       inc=S.s_dv)
                tr.opw("vector", "reduce_max", f.m2[:, :], f.tmp8[:, :],
                       axis=AX.X, inc=S.s_dv)
                tr.opw("vector", "tensor_single_scalar", f.tmp8[:, :],
                       f.gst[:, :], f.m2[:, :], ALU.is_ge, inc=S.s_dv)
                tr.opw("vector", "tensor_tensor", f.gm[:, :], f.gst[:, :],
                       f.tmp8[:, :], ALU.mult, inc=S.s_dv)
                tr.opw("vector", "tensor_scalar", f.d1[:, :], f.m2[:, :],
                       1.0, None, ALU.add, inc=S.s_dv)
                tr.opw("vector", "reciprocal", f.r2[:, :], f.d1[:, :],
                       inc=S.s_dv)
                tr.opw("vector", "tensor_scalar",
                       f.coef[:, tc * E:(tc + 1) * E], f.gm[:, :],
                       f.r2[:, :], 0.5, ALU.mult, ALU.mult, inc=S.s_dv)
                cmk = tr.opw("vector", "tensor_single_scalar", f.gm[:, :],
                             f.coef[:, tc * E:(tc + 1) * E], 0.0, ALU.is_gt,
                             inc=S.s_dv)
                slot = trk % 2
                tr.wait("tensor", S.s_dv, cmk)
                if trk >= 2:
                    tr.wait("tensor", S.s_tev, h[("tev", trk - 2)])
                c = tr.op("tensor", "transpose", f.pT[slot][0:8, 0:128],
                          f.gm[:, :], identf[:, :], inc=S.s_tr)
                h[("tr", trk)] = c
                h[("trl", tc)] = c
                tr.wait("vector", S.s_tr, h[("tr", trk)])
                c = tr.op("vector", "tensor_copy",
                          f.maskcm[:, tc * 128:(tc + 1) * 128],
                          f.pT[slot][0:8, 0:128], inc=S.s_tev)
                h[("tev", trk)] = c
                trk += 1
            h[("g1_done",)] = S.s_dv.count

            tr.wait("vector", S.s_tev, h[("tev", trk - 1)])
            for b in range(4):
                init = (f.zeros8[:, 0:1] if b == 0
                        else f.maskcm[:, b * 1024 - 1:b * 1024])
                tr.opw("vector", "tensor_tensor_scan",
                       f.maskcm[:, b * 1024:(b + 1) * 1024],
                       f.zeros8[:, :],
                       f.maskcm[:, b * 1024:(b + 1) * 1024],
                       init, ALU.add, ALU.add, inc=S.s_dv)
            for tc in range(NCH):
                slot = trk % 2
                if trk >= 2:
                    tr.wait("tensor", S.s_tev, h[("tev", trk - 2)])
                tr.wait("tensor", S.s_dv, S.s_dv.count)
                c = tr.op("tensor", "transpose", f.pT[slot][:, 0:8],
                          f.maskcm[:, tc * 128:(tc + 1) * 128],
                          identf[0:8, 0:8], inc=S.s_tr)
                h[("tr", trk)] = c
                tr.wait("vector", S.s_tr, h[("tr", trk)])
                c = tr.op("vector", "tensor_copy", f.postm[:, :],
                          f.pT[slot][:, 0:8], inc=S.s_tev)
                h[("tev", trk)] = c
                trk += 1
                tr.wait("vector", S.s_tev, h[("tev", trk - 1)])
                tr.opw("vector", "tensor_single_scalar", f.mtm[:, :],
                       f.coef[:, tc * E:(tc + 1) * E], 0.0, ALU.is_gt,
                       inc=S.s_dv)
                tr.opw("vector", "tensor_single_scalar", f.ovf[:, :],
                       f.postm[:, :], float(CAP), ALU.is_le, inc=S.s_dv)
                tr.opw("vector", "tensor_tensor", f.m2t[:, :], f.mtm[:, :],
                       f.ovf[:, :], ALU.mult, inc=S.s_dv)
                tr.opw("vector", "tensor_tensor", f.slf[:, :], f.postm[:, :],
                       slotbase[:, :], ALU.add, inc=S.s_dv)
                tr.opw("vector", "tensor_tensor", f.slf[:, :], f.slf[:, :],
                       f.m2t[:, :], ALU.mult, inc=S.s_dv)
                tr.opw("vector", "tensor_scalar", f.sl2[:, :], f.m2t[:, :],
                       -float(SENT), float(SENT), ALU.mult, ALU.add,
                       inc=S.s_dv)
                tr.opw("vector", "tensor_tensor", f.slf[:, :], f.slf[:, :],
                       f.sl2[:, :], ALU.add, inc=S.s_dv)
                tr.opw("vector", "tensor_reduce", f.o1[:, :], f.slf[:, :],
                       axis=AX.X, op=ALU.min, inc=S.s_dv)
                tr.opw("vector", "tensor_single_scalar", f.mm1[:, :],
                       f.slf[:, :], f.o1[:, :], ALU.is_equal, inc=S.s_dv)
                tr.opw("vector", "scalar_tensor_tensor", f.sl2[:, :],
                       f.mm1[:, :], float(SENT), f.slf[:, :], ALU.mult,
                       ALU.add, inc=S.s_dv)
                tr.opw("vector", "tensor_reduce", f.o2[:, :], f.sl2[:, :],
                       axis=AX.X, op=ALU.min, inc=S.s_dv)
                tr.opw("vector", "tensor_tensor", f.cm1[:, :], f.mm1[:, :],
                       f.coef[:, tc * E:(tc + 1) * E], ALU.mult, inc=S.s_dv)
                tr.opw("vector", "reduce_sum", f.c12[:, 2 * tc:2 * tc + 1],
                       f.cm1[:, :], axis=AX.X, inc=S.s_dv)
                tr.opw("vector", "tensor_single_scalar", f.mm1[:, :],
                       f.sl2[:, :], f.o2[:, :], ALU.is_equal, inc=S.s_dv)
                tr.opw("vector", "tensor_tensor", f.cm1[:, :], f.mm1[:, :],
                       f.coef[:, tc * E:(tc + 1) * E], ALU.mult, inc=S.s_dv)
                tr.opw("vector", "reduce_sum", f.c12[:, 2 * tc + 1:2 * tc + 2],
                       f.cm1[:, :], axis=AX.X, inc=S.s_dv)
                tr.opw("vector", "tensor_copy", f.sIdx[:, 2 * tc:2 * tc + 1],
                       f.o1[:, :], inc=S.s_dv)
                tr.opw("vector", "tensor_copy",
                       f.sIdx[:, 2 * tc + 1:2 * tc + 2], f.o2[:, :],
                       inc=S.s_dv)
                tr.opw("vector", "tensor_single_scalar", f.oc[:, :],
                       f.o1[:, :], float(E * CAP), ALU.min, inc=S.s_dv)
                tr.opw("vector", "tensor_copy", f.gIdx[:, 2 * tc:2 * tc + 1],
                       f.oc[:, :], inc=S.s_dv)
                tr.opw("vector", "tensor_single_scalar", f.oc[:, :],
                       f.o2[:, :], float(E * CAP), ALU.min, inc=S.s_dv)
                c = tr.opw("vector", "tensor_copy",
                           f.gIdx[:, 2 * tc + 1:2 * tc + 2], f.oc[:, :],
                           inc=S.s_dv)
                h[("idx", tc)] = c
            h[("g2_done",)] = S.s_dv.count

            tr.wait("gpsimd", S.s_dv, h[("g2_done",)])
            tr.wait("gpsimd", S.d_sp, h[("xnst", NCH - 1)])
            for tc in range(NCH):
                if tc > 0:
                    tr.wait("gpsimd", S.d_sp, h[("sc", tc - 1)])
                c = tr.dma("gpsimd", f.xgld[:, :],
                           xnbf_d[tc * 128:(tc + 1) * 128, :], S.d_x)
                tr.wait("gpsimd", S.d_x, c)
                for kk in range(2):
                    c = tr.op("gpsimd", "indirect_dma_start",
                              out=xg_d[:, :],
                              out_offset=bass.IndirectOffsetOnAxis(
                                  f.sIdx[:, 2 * tc + kk:2 * tc + kk + 1], 0),
                              in_=f.xgld[:, :], in_offset=None,
                              inc=S.d_sp, incn=16)
                h[("sc", tc)] = c
            h[("scat_done",)] = c
            tr.wait("gpsimd", S.d_sp, h[("scat_done",)])

            astep = 0
            bstep = 0
            blkidx = 0
            grpidx = 0
            NTE = CAP // 128
            ASL = [(0, 512), (512, 1024), (1024, CAP)]
            for e in range(E):
                for te in range(NTE):
                    if (e, te) != (0, 0):
                        tr.wait("gpsimd", S.s_tr, h[("xgtr_prev",)])
                    c = tr.dma("gpsimd", f.xgld[:, :],
                               xg_d[e * CAP + te * 128:
                                    e * CAP + (te + 1) * 128, :], S.d_x)
                    h[("xg", e, te)] = c
                    tr.wait("tensor", S.d_x, h[("xg", e, te)])
                    if e > 0 and te == 0:
                        tr.wait("tensor", S.s_amm, h[("aend", e - 1)])
                    for k in range(NKC):
                        slot = trk % 2
                        if trk >= 2:
                            tr.wait("tensor", S.s_tev, h[("tev", trk - 2)])
                        c = tr.op("tensor", "transpose", f.pT[slot][:, :],
                                  f.xgld[:, k * 128:(k + 1) * 128],
                                  identf[:, :], inc=S.s_tr)
                        h[("tr", trk)] = c
                        tr.wait("vector", S.s_tr, h[("tr", trk)])
                        c = tr.op("vector", "tensor_copy",
                                  f.xgfm[:, k * CAP + te * 128:
                                         k * CAP + (te + 1) * 128],
                                  f.pT[slot][:, :], inc=S.s_tev)
                        h[("tev", trk)] = c
                        trk += 1
                    h[("xgtr_prev",)] = S.s_tr.count
                tr.op("vector", "nop", inc=S.s_misc)
                h[("xgfm", e)] = S.s_misc.count
                for gi, (g0, g1) in enumerate(HGROUPS):
                    for bl in (2 * gi, 2 * gi + 1):
                        hb0, hb1 = HBLOCKS[bl]
                        nh2 = hb1 - hb0
                        if gi == 0 and bl == 0:
                            tr.wait("sync", S.d_wc, xcounts.get(("wc", e)))
                        if blkidx >= 2:
                            tr.wait("sync", S.s_w13c, w13c0 + blkidx - 1)
                        for k in range(NKC):
                            tr.dma("sync",
                                   f.w1b[blkidx % 2][:, k * 512:
                                                     k * 512 + nh2 * 128],
                                   w1_bf[e, hb0 * 128:hb1 * 128,
                                         k * 128:(k + 1) * 128],
                                   S.d_w, transpose=True)
                            c = tr.dma("sync",
                                       f.w3b[blkidx % 2][:, k * 512:
                                                         k * 512 + nh2 * 128],
                                       w3_bf[e, hb0 * 128:hb1 * 128,
                                             k * 128:(k + 1) * 128],
                                       S.d_w, transpose=True)
                        h[("wblk", blkidx)] = c
                        blkidx += 1
                    if grpidx >= 1:
                        tr.wait("sync", S.s_w2c, w2c0 + grpidx)
                    for hl, hh in enumerate(range(g0, g1)):
                        c = tr.dma("sync", f.w2b[:, hl * D:(hl + 1) * D],
                                   w2_bf[e, 0:D, hh * 128:(hh + 1) * 128],
                                   S.d_w, transpose=True)
                    h[("w2g", grpidx)] = c
                    grpidx += 1

                    for hh in range(g0, g1):
                        bl = hh // 4
                        gblk = e * 6 + bl
                        par = gblk % 2
                        hloc = hh - HBLOCKS[bl][0]
                        for n, (n0, n1) in enumerate(ASL):
                            p = astep % 2
                            tr.wait("tensor", S.d_w, h[("wblk", gblk)])
                            if astep == 0:
                                tr.wait("tensor", S.s_misc, h[("xgfm", e)])
                            if astep >= 2:
                                tr.wait("tensor", S.s_silu,
                                        h[("silu", astep - 2)])
                                tr.wait("tensor", S.s_hm, h[("hm", astep - 2)])
                            for k in range(NKC):
                                tr.op("tensor", "matmul",
                                      f.pH1[p][:, 0:n1 - n0],
                                      f.w1b[par][:, k * 512 + hloc * 128:
                                                 k * 512 + (hloc + 1) * 128],
                                      f.xgfm[:, k * CAP + n0:k * CAP + n1],
                                      start=(k == 0), stop=(k == NKC - 1))
                            for k in range(NKC):
                                tr.op("tensor", "matmul",
                                      f.pH3[p][:, 0:n1 - n0],
                                      f.w3b[par][:, k * 512 + hloc * 128:
                                                 k * 512 + (hloc + 1) * 128],
                                      f.xgfm[:, k * CAP + n0:k * CAP + n1],
                                      start=(k == 0), stop=(k == NKC - 1),
                                      inc=S.s_amm if k == NKC - 1 else None)
                            h[("amm", astep)] = S.s_amm.count
                            tr.wait("scalar", S.s_amm, h[("amm", astep)])
                            if astep >= 2:
                                tr.wait("scalar", S.s_hm, h[("hm", astep - 2)])
                            c = tr.op("scalar", "activation",
                                      f.silu[p][:, 0:n1 - n0],
                                      f.pH1[p][:, 0:n1 - n0], AF.Silu,
                                      inc=S.s_silu)
                            h[("silu", astep)] = c
                            tr.wait("vector", S.s_silu, h[("silu", astep)])
                            c = tr.op("vector", "tensor_tensor",
                                      f.hbuf[:, (hh - g0) * CAP + n0:
                                             (hh - g0) * CAP + n1],
                                      f.silu[p][:, 0:n1 - n0],
                                      f.pH3[p][:, 0:n1 - n0],
                                      ALU.mult, inc=S.s_hm)
                            h[("hm", astep)] = c
                            astep += 1
                        if hh == HBLOCKS[bl][1] - 1:
                            tr.op("tensor", "nop", inc=S.s_w13c)
                    nhg = g1 - g0
                    first_b = True
                    for te in range(NTE):
                        for half in range(2):
                            p = bstep % 2
                            if first_b:
                                tr.wait("tensor", S.s_hm, h[("hm", astep - 1)])
                                tr.wait("tensor", S.d_w,
                                        h[("w2g", grpidx - 1)])
                                first_b = False
                            if bstep >= 2:
                                tr.wait("tensor", S.s_bep,
                                        h[("bep", bstep - 2)])
                            for hl in range(nhg):
                                tr.op("tensor", "matmul", f.pB[p][:, :],
                                      f.hbuf[:, hl * CAP + te * 128:
                                             hl * CAP + (te + 1) * 128],
                                      f.w2b[:, hl * D + half * 512:
                                            hl * D + (half + 1) * 512],
                                      start=(hl == 0), stop=(hl == nhg - 1),
                                      skip_group_check=True,
                                      inc=S.s_bmm if hl == nhg - 1 else None)
                            h[("bmm", bstep)] = S.s_bmm.count
                            tr.wait("vector", S.s_bmm, h[("bmm", bstep)])
                            if e > 0 and gi == 0 and te == 0 and half == 0:
                                tr.wait("vector", S.d_st, h[("yst", e - 1)])
                            oslc = f.outacc[:, te * D + half * 512:
                                            te * D + (half + 1) * 512]
                            if gi == 0:
                                c = tr.op("vector", "tensor_copy", oslc,
                                          f.pB[p][:, :], inc=S.s_bep)
                            else:
                                c = tr.op("vector", "tensor_tensor", oslc,
                                          f.pB[p][:, :], oslc, ALU.add,
                                          inc=S.s_bep)
                            h[("bep", bstep)] = c
                            bstep += 1
                    tr.op("tensor", "nop", inc=S.s_w2c)
                h[("aend", e)] = S.s_amm.count
                for te in range(NTE):
                    tr.wait("gpsimd", S.s_bep, h[("bep", bstep - 1)])
                    c = tr.dma("gpsimd",
                               y_d[e * CAP + te * 128:
                                   e * CAP + (te + 1) * 128, :],
                               f.outacc[:, te * D:(te + 1) * D], S.d_st)
                h[("yst", e)] = c

            tr.wait("gpsimd", S.d_st, h[("yst", E - 1)])
            for tc in range(NCH):
                if tc > 0:
                    tr.wait("gpsimd", S.s_dv, h[("res", tc - 1)])
                c = tr.op("gpsimd", "indirect_dma_start",
                          out=f.g1g[:, :], out_offset=None,
                          in_=y_d[:, :],
                          in_offset=bass.IndirectOffsetOnAxis(
                              f.gIdx[:, 2 * tc:2 * tc + 1], 0),
                          inc=S.d_x, incn=16)
                c = tr.op("gpsimd", "indirect_dma_start",
                          out=f.g2g[:, :], out_offset=None,
                          in_=y_d[:, :],
                          in_offset=bass.IndirectOffsetOnAxis(
                              f.gIdx[:, 2 * tc + 1:2 * tc + 2], 0),
                          inc=S.d_x, incn=16)
                c = tr.dma("sync", f.xstg[:, :],
                           src[tc * 128:(tc + 1) * 128, :], S.d_x)
                h[("rld", tc)] = c
                tr.wait("vector", S.d_x, h[("rld", tc)])
                tr.opw("vector", "scalar_tensor_tensor", f.ostg[:, :],
                       f.g1g[:, :], f.c12[:, 2 * tc:2 * tc + 1],
                       f.xstg[:, :], ALU.mult, ALU.add, inc=S.s_dv)
                c = tr.op("vector", "scalar_tensor_tensor", f.ostg[:, :],
                          f.g2g[:, :], f.c12[:, 2 * tc + 1:2 * tc + 2],
                          f.ostg[:, :], ALU.mult, ALU.add, inc=S.s_dv)
                h[("res", tc)] = c
                tr.wait("sync", S.s_dv, h[("res", tc)])
                c = tr.dma("sync", dst[tc * 128:(tc + 1) * 128, :],
                           f.ostg[:, :], S.d_st)
            tr.wait("gpsimd", S.d_st, S.d_st.count)
            tr.wait("sync", S.d_st, S.d_st.count)


        if "p1" in phases:
            with ExitStack() as c1:
                f1 = make_ffn_tensors(c1, "f1_")
                run_phase(nc, allsems,
                          lambda tr: emit_ffn(tr, f1, x_in, stream,
                                              do_casts=True))

        if "p2" in phases:
            with ExitStack() as c2:
                a = NS()
                a.wqkvT = sb(c2, "a_wqkvT", [128, NKC * 3 * D], BF16)
                dbl = lambda nm, shape, dt=F32: [
                    sb(c2, "a_" + nm + str(i), shape, dt) for i in range(2)]
                a.xres1 = dbl("xres1", [128, D])
                a.cst = dbl("cst", [128, D])
                a.sqst = dbl("sqst", [128, D])
                a.xntm = dbl("xntm", [128, D])
                a.xnbf1 = dbl("xnbf1", [128, D], BF16)
                a.qstage = dbl("qstage", [128, D])
                a.qsm = dbl("qsm", [128, D])
                a.kstage = dbl("kstage", [128, D])
                a.kexp = dbl("kexp", [128, D])
                a.kexpb = dbl("kexpb", [128, D], BF16)
                a.vbf = dbl("vbf", [128, D], BF16)
                a.kcm = dbl("kcm", [64, NH * 128])
                a.qscm = dbl("qscm", [64, NH * 128])
                a.schst = dbl("schst", [64, NH * 128])
                a.r1 = dbl("r1", [128, 1])
                a.nmean = dbl("nmean", [128, 1])
                a.ssq = dbl("ssq", [128, 1])
                a.sd = dbl("sd", [128, 1])
                a.rstd = dbl("rstd", [128, 1])
                a.qm16 = dbl("qm16", [128, NH])
                a.qsum16 = dbl("qsum16", [128, NH])
                a.qr16 = dbl("qr16", [128, NH])
                a.kmaxc = sb(c2, "a_kmaxc", [64, NH])
                a.onesb = sb(c2, "a_onesb", [128, 1], BF16)
                a.epsb = sb(c2, "a_epsb", [128, 1])
                a.pQ = [ps(c2, "a_pQ%d" % i, [128, 512]) for i in range(2)]
                a.pT = [ps(c2, "a_pT%d" % i, [128, 128]) for i in range(2)]
                a.pS = ps(c2, "a_pS", [64, NH * 128])

                def emit_p2a(tr):
                    h = tr.hist
                    tr.wait("sync", S.d_wc, xcounts.get(("wc_qkv",)))
                    for k in range(NKC):
                        for s2 in range(2):
                            c = tr.dma("sync",
                                       a.wqkvT[:, k * 3 * D + s2 * 1536:
                                               k * 3 * D + (s2 + 1) * 1536],
                                       wqkv_bf[s2 * 1536:(s2 + 1) * 1536,
                                               k * 128:(k + 1) * 128],
                                       S.d_w, transpose=True)
                    h["wq"] = c
                    tr.op("vector", "memset", a.onesb[:, :], 1.0)
                    tr.op("vector", "memset", a.epsb[:, :], LNEPS)
                    c0 = tr.op("vector", "memset", a.pS[:, :], 0.0,
                               inc=S.s_dv2)
                    trk = 0
                    for n in range(NCH):
                        P = n % 2
                        if n >= 2:
                            tr.wait("sync", S.s_dv2, h[("xdone", n - 2)])
                        c = tr.dma("sync", a.xres1[P][:, :],
                                   stream[n * 128:(n + 1) * 128, :], S.d_x)
                        h[("xld", n)] = c
                        tr.wait("vector", S.d_x, h[("xld", n)])
                        if n >= 2:
                            tr.wait("vector", S.s_ac2, h[("kexp", n - 2)])
                            tr.wait("vector", S.s_pe2, h[("smm", n - 2)])
                            tr.wait("vector", S.s_tr, h[("trlast", n - 2)])
                            tr.wait("vector", S.d_sp, h[("spst", n - 2)])
                        tr.opw("vector", "reduce_sum", a.r1[P][:, :],
                               a.xres1[P][:, :], axis=AX.X, inc=S.s_dv2)
                        tr.opw("vector", "tensor_scalar", a.nmean[P][:, :],
                               a.r1[P][:, :], -1.0 / D, None, ALU.mult,
                               inc=S.s_dv2)
                        c = tr.op("vector", "tensor_scalar", a.cst[P][:, :],
                                  a.xres1[P][:, :], a.nmean[P][:, :], None,
                                  ALU.add, inc=S.s_dv2)
                        h[("xdone", n)] = c
                        tr.opw("vector", "scalar_tensor_tensor",
                               a.sqst[P][:, :], a.cst[P][:, :], 1.0,
                               a.cst[P][:, :], ALU.mult, ALU.mult,
                               accum_out=a.ssq[P][:, :], inc=S.s_dv2)
                        tr.wait("scalar", S.s_dv2, S.s_dv2.count)
                        tr.opw("scalar", "activation", a.sd[P][:, :],
                               a.ssq[P][:, :], AF.Ln, scale=1.0 / D,
                               bias=a.epsb[:, :], inc=S.s_ac2)
                        c = tr.op("scalar", "activation", a.rstd[P][:, :],
                                  a.sd[P][:, :], AF.Exp, scale=-0.5,
                                  inc=S.s_ac2)
                        tr.wait("vector", S.s_ac2, c)
                        cxn = tr.op("vector", "tensor_scalar",
                                    a.xntm[P][:, :], a.cst[P][:, :],
                                    a.rstd[P][:, :], None, ALU.mult,
                                    inc=S.s_dv2)
                        tr.wait("tensor", S.s_dv2, cxn)
                        if n >= 2:
                            tr.wait("tensor", S.s_pe2, h[("qkvl", n - 2)])
                        for k in range(NKC):
                            slot = trk % 2
                            if trk >= 2:
                                tr.wait("tensor", S.s_tev, h[("tev", trk - 2)])
                            c = tr.op("tensor", "transpose", a.pT[slot][:, :],
                                      a.xntm[P][:, k * 128:(k + 1) * 128],
                                      identf[:, :], inc=S.s_tr)
                            h[("tr", trk)] = c
                            tr.wait("vector", S.s_tr, h[("tr", trk)])
                            c = tr.op("vector", "tensor_copy",
                                      a.xnbf1[P][:, k * 128:(k + 1) * 128],
                                      a.pT[slot][:, :], inc=S.s_tev)
                            h[("tev", trk)] = c
                            trk += 1
                        tr.wait("tensor", S.s_tev, h[("tev", trk - 1)])
                        if n == 0:
                            tr.wait("tensor", S.d_w, h["wq"])
                        for qs in range(6):
                            p = qs % 2
                            if n > 0 or qs >= 2:
                                tr.wait("tensor", S.s_dv2,
                                        h[("qev", n if qs >= 2 else n - 1,
                                           qs - 2 if qs >= 2 else qs + 4)])
                            for k in range(NKC):
                                c = tr.op("tensor", "matmul", a.pQ[p][:, :],
                                          a.xnbf1[P][:, k * 128:
                                                     (k + 1) * 128],
                                          a.wqkvT[:, k * 3 * D + qs * 512:
                                                  k * 3 * D + (qs + 1) * 512],
                                          start=(k == 0), stop=(k == NKC - 1),
                                          inc=S.s_pe2
                                          if k == NKC - 1 else None)
                            h[("qkv", n, qs)] = S.s_pe2.count
                            if qs == 5:
                                h[("qkvl", n)] = S.s_pe2.count
                            tr.wait("vector", S.s_pe2, h[("qkv", n, qs)])
                            if qs < 2:
                                dstap = a.qstage[P][:, qs * 512:
                                                   (qs + 1) * 512]
                            elif qs < 4:
                                dstap = a.kstage[P][:, (qs - 2) * 512:
                                                   (qs - 1) * 512]
                            else:
                                dstap = a.vbf[P][:, (qs - 4) * 512:
                                                (qs - 3) * 512]
                            c = tr.op("vector", "tensor_copy", dstap,
                                      a.pQ[p][:, :], inc=S.s_dv2)
                            h[("qev", n, qs)] = c
                        if n >= 2:
                            tr.wait("scalar", S.s_tr, h[("trlast", n - 2)])
                        tr.opw("vector", "reduce_max", a.qm16[P][:, :],
                               a.qstage[P][:, :].rearrange(
                                   "p (h c) -> p h c", c=HS),
                               axis=AX.X, negate=True, inc=S.s_dv2)
                        for hd in range(NH):
                            c = tr.op("vector", "tensor_scalar",
                                      a.qsm[P][:, hd * HS:(hd + 1) * HS],
                                      a.qstage[P][:, hd * HS:(hd + 1) * HS],
                                      a.qm16[P][:, hd:hd + 1], None, ALU.add,
                                      inc=S.s_dv2)
                        tr.wait("vector", S.s_dv2, c)
                        tr.wait("scalar", S.s_dv2, c)
                        c = tr.op("scalar", "activation", a.qsm[P][:, :],
                                  a.qsm[P][:, :], AF.Exp, inc=S.s_ac2)
                        tr.wait("vector", S.s_ac2, c)
                        tr.opw("vector", "reduce_sum", a.qsum16[P][:, :],
                               a.qsm[P][:, :].rearrange(
                                   "p (h c) -> p h c", c=HS),
                               axis=AX.X, inc=S.s_dv2)
                        tr.opw("vector", "reciprocal", a.qr16[P][:, :],
                               a.qsum16[P][:, :], inc=S.s_dv2)
                        for hd in range(NH):
                            c = tr.op("vector", "tensor_scalar",
                                      a.qsm[P][:, hd * HS:(hd + 1) * HS],
                                      a.qsm[P][:, hd * HS:(hd + 1) * HS],
                                      a.qr16[P][:, hd:hd + 1], 0.125,
                                      ALU.mult, ALU.mult, inc=S.s_dv2)
                        tr.wait("vector", S.s_dv2, c)
                        c = tr.op("vector", "nop", inc=S.s_dv2)
                        tr.wait("scalar", S.s_dv2, c)
                        c = tr.op("scalar", "activation", a.kexp[P][:, :],
                                  a.kstage[P][:, :], AF.Exp, inc=S.s_ac2)
                        h[("kexp", n)] = c
                        tr.wait("vector", S.s_ac2, c)
                        cvb = tr.op("vector", "tensor_copy",
                                    a.kexpb[P][:, :], a.kexp[P][:, :],
                                    inc=S.s_dv2)
                        tr.wait("tensor", S.s_dv2, cvb)
                        if n > 0:
                            tr.wait("tensor", S.s_dv2, h[("sdone", n - 1)])
                        else:
                            tr.wait("tensor", S.s_dv2, c0)
                        for hd in range(NH):
                            tr.op("tensor", "matmul",
                                  a.pS[0:64, hd * 128:hd * 128 + 64],
                                  a.kexpb[P][:, hd * HS:(hd + 1) * HS],
                                  a.vbf[P][:, hd * HS:(hd + 1) * HS],
                                  start=(hd % 4 == 0), stop=False,
                                  skip_group_check=True)
                            c = tr.op("tensor", "matmul",
                                      a.pS[0:64, hd * 128 + 64:
                                           hd * 128 + 65],
                                      a.kexpb[P][:, hd * HS:(hd + 1) * HS],
                                      a.onesb[:, :], start=False,
                                      stop=(hd % 4 == 3),
                                      skip_group_check=True,
                                      inc=S.s_pe2 if hd == NH - 1 else None)
                        h[("smm", n)] = S.s_pe2.count
                        for hd in range(NH):
                            for srcap, dstt in (
                                    (a.kstage[P][:, hd * HS:(hd + 1) * HS],
                                     a.kcm[P]),
                                    (a.qsm[P][:, hd * HS:(hd + 1) * HS],
                                     a.qscm[P])):
                                slot = trk % 2
                                if trk >= 2:
                                    tr.wait("tensor", S.s_tev,
                                            h[("tev", trk - 2)])
                                c = tr.op("tensor", "transpose",
                                          a.pT[slot][0:64, :], srcap,
                                          identf[:, :], inc=S.s_tr)
                                h[("tr", trk)] = c
                                tr.wait("vector", S.s_tr, h[("tr", trk)])
                                c = tr.op("vector", "tensor_copy",
                                          dstt[:, hd * 128:(hd + 1) * 128],
                                          a.pT[slot][0:64, :], inc=S.s_tev)
                                h[("tev", trk)] = c
                                trk += 1
                        cktr = S.s_tev.count
                        h[("trlast", n)] = S.s_tr.count
                        for hd in range(NH):
                            cr = tr.op("vector", "reduce_max",
                                       a.kmaxc[:, hd:hd + 1],
                                       a.kcm[P][:, hd * 128:(hd + 1) * 128],
                                       axis=AX.X,
                                       inc=S.s_dv2 if hd == NH - 1 else None)
                        tr.wait("vector", S.s_dv2, cr)
                        tr.opw("vector", "tensor_tensor", kmaxloc[:, :],
                               kmaxloc[:, :], a.kmaxc[:, :], ALU.max,
                               inc=S.s_dv2)
                        tr.wait("vector", S.s_pe2, h[("smm", n)])
                        if n >= 2:
                            tr.wait("vector", S.d_sp, h[("spst", n - 2)])
                        tr.opw("vector", "tensor_copy", a.schst[P][:, :],
                               a.pS[:, :], inc=S.s_dv2)
                        c = tr.op("vector", "tensor_tensor", Stot[:, :],
                                  Stot[:, :], a.schst[P][:, :], ALU.add,
                                  inc=S.s_dv2)
                        h[("sdone", n)] = c
                        tr.wait("gpsimd", S.s_tev, cktr)
                        tr.dma("gpsimd", kcm_d[n, :, :], a.kcm[P][:, :],
                               S.d_sp)
                        tr.dma("gpsimd", qscm_d[n, :, :], a.qscm[P][:, :],
                               S.d_sp)
                        tr.wait("gpsimd", S.s_dv2, h[("qev", n, 5)])
                        tr.dma("gpsimd", vtm_d[n * 128:(n + 1) * 128, :],
                               a.vbf[P][:, :], S.d_sp)
                        tr.wait("gpsimd", S.s_dv2, h[("sdone", n)])
                        c = tr.dma("gpsimd", schunks_d[n, :, :],
                                   a.schst[P][:, :], S.d_sp)
                        h[("spst", n)] = c
                    tr.wait("gpsimd", S.d_sp, S.d_sp.count)
                    cl = tr.op("vector", "nop", inc=S.s_dv2)
                    tr.wait("gpsimd", S.s_dv2, cl)
                    tr.dma("gpsimd", ccsrc_d[:, 0:16], kmaxloc[:, :], S.d_sp)
                    c = tr.dma("gpsimd", ccsrc_d[:, 16:CCW], Stot[:, :],
                               S.d_sp)
                    tr.wait("gpsimd", S.d_sp, c)
                    tr.op("gpsimd", "collective_compute", "AllGather",
                          ALU.bypass,
                          replica_groups=[list(range(NCORES))],
                          ins=[ccsrc_d.ap().opt()], outs=[ccdst_d.ap().opt()],
                          inc=S.cc_s)
                    tr.wait("gpsimd", S.cc_s, 1)

                run_phase(nc, allsems, emit_p2a)

            with ExitStack() as c3:
                b2 = NS()
                b2.woutT = sb(c3, "b_woutT", [128, NKC * D], BF16)
                dblb = lambda nm, shape, dt=F32: [
                    sb(c3, "b_" + nm + str(i), shape, dt) for i in range(2)]
                b2.gath = sb(c3, "b_gath", [64, CCW])
                b2.Scar = sb(c3, "b_Scar", [64, NH * 128])
                b2.Scarb = sb(c3, "b_Scarb", [64, NH * HS], BF16)
                b2.kcm = dblb("kcm", [64, NH * 128])
                b2.kexp = dblb("kexp", [64, NH * 128])
                b2.kexpb = dblb("kexpb", [64, NH * 128], BF16)
                b2.qscm = dblb("qscm", [64, NH * 128])
                b2.qtb = dblb("qtb", [64, NH * 128], BF16)
                b2.zfull = dblb("zfull", [64, NH * 128])
                b2.ztmp = dblb("ztmp", [64, NH * 128])
                b2.schk = dblb("schk", [64, NH * 128])
                b2.vchunk = dblb("vchunk", [128, D], BF16)
                b2.ofm = dblb("ofm", [128, D], BF16)
                b2.xres1 = dblb("xres1", [128, D])
                b2.abf = [sb(c3, "b_abf%d" % i, [128, 128], BF16)
                          for i in range(2)]
                b2.ones64 = sb(c3, "b_ones64", [64, 128])
                b2.pA = [ps(c3, "b_pA%d" % i, [128, 128]) for i in range(2)]
                b2.pO = ps(c3, "b_pO", [64, NH * 128])
                b2.pXO = [ps(c3, "b_pXO%d" % i, [128, 512]) for i in range(2)]

                def emit_p2b(tr):
                    h = tr.hist
                    tr.wait("sync", S.d_wc, xcounts.get(("wc_out",)))
                    for k in range(NKC):
                        c = tr.dma("sync", b2.woutT[:, k * D:(k + 1) * D],
                                   wout_bf[0:D, k * 128:(k + 1) * 128],
                                   S.d_w, transpose=True)
                    h["wo"] = c
                    tr.op("vector", "memset", b2.ones64[:, :], 1.0)
                    tr.op("vector", "memset", b2.Scar[:, :], 0.0)
                    tr.op("vector", "memset", kmaxloc[:, :], -1e38)
                    cm = tr.op("vector", "nop", inc=S.s_dv2)
                    for r in range(NCORES):
                        tr.wait("gpsimd", S.s_dv2,
                                cm if r == 0 else h[("gproc", r - 1)])
                        c = tr.dma("gpsimd", b2.gath[:, :],
                                   ccdst_d[r * 64:(r + 1) * 64, :], S.d_x)
                        h[("gld", r)] = c
                        tr.wait("vector", S.d_x, h[("gld", r)])
                        tr.opw("vector", "tensor_scalar", b2.kcm[0][:, 0:16],
                               b2.gath[:, 0:16], seqb[:, r:r + 1], None,
                               ALU.add, inc=S.s_dv2)
                        tr.opw("vector", "tensor_tensor", kmaxloc[:, :],
                               kmaxloc[:, :], b2.kcm[0][:, 0:16], ALU.max,
                               inc=S.s_dv2)
                        c = tr.opw("vector", "scalar_tensor_tensor",
                                   b2.Scar[:, :], b2.gath[:, 16:CCW],
                                   prefm[:, r:r + 1], b2.Scar[:, :],
                                   ALU.mult, ALU.add, inc=S.s_dv2)
                        h[("gproc", r)] = c
                    c = tr.op("vector", "nop", inc=S.s_dv2)
                    tr.wait("scalar", S.s_dv2, c)
                    c = tr.op("scalar", "activation", escale[:, :],
                              kmaxloc[:, :], AF.Exp, scale=-1.0, inc=S.s_ac2)
                    tr.wait("vector", S.s_ac2, c)
                    tr.opw("vector", "tensor_scalar", negm[:, :],
                           kmaxloc[:, :], -1.0, None, ALU.mult, inc=S.s_dv2)
                    for hd in range(NH):
                        crs = tr.op("vector", "tensor_scalar",
                                    b2.Scar[:, hd * 128:(hd + 1) * 128],
                                    b2.Scar[:, hd * 128:(hd + 1) * 128],
                                    escale[:, hd:hd + 1], None, ALU.mult,
                                    inc=S.s_dv2 if hd == NH - 1 else None)
                    tr.wait("vector", S.s_dv2, crs)
                    for hd in range(NH):
                        c = tr.op("vector", "tensor_copy",
                                  b2.Scarb[:, hd * HS:(hd + 1) * HS],
                                  b2.Scar[:, hd * 128:hd * 128 + 64],
                                  inc=S.s_dv2 if hd == NH - 1 else None)
                    h[("scb", -1)] = c

                    for n in range(NCH):
                        P = n % 2
                        if n == 0:
                            tr.wait("gpsimd", S.s_dv2,
                                    h[("gproc", NCORES - 1)])
                        if n >= 2:
                            tr.wait("gpsimd", S.s_dv2, h[("csc", n - 2)])
                        c = tr.dma("gpsimd", b2.kcm[P][:, :], kcm_d[n, :, :],
                                   S.d_x)
                        if n >= 2:
                            tr.wait("gpsimd", S.s_dv2, h[("qtb", n - 2)])
                        c = tr.dma("gpsimd", b2.qscm[P][:, :],
                                   qscm_d[n, :, :], S.d_x)
                        if n >= 2:
                            tr.wait("gpsimd", S.s_dv2, h[("sup", n - 2)])
                        c = tr.dma("gpsimd", b2.schk[P][:, :],
                                   schunks_d[n, :, :], S.d_x)
                        if n >= 2:
                            tr.wait("gpsimd", S.s_pe2, h[("omm", n - 2)])
                        c = tr.dma("gpsimd", b2.vchunk[P][:, :],
                                   vtm_d[n * 128:(n + 1) * 128, :], S.d_x)
                        if n >= 2:
                            tr.wait("gpsimd", S.d_st, h[("stc", n - 2)])
                        c = tr.dma("gpsimd", b2.xres1[P][:, :],
                                   stream[n * 128:(n + 1) * 128, :], S.d_x)
                        h[("ld", n)] = c
                        tr.wait("scalar", S.d_x, h[("ld", n)])
                        if n == 0:
                            tr.wait("scalar", S.s_dv2, h[("scb", -1)])
                        if n >= 2:
                            tr.wait("scalar", S.s_dv2, h[("csc", n - 2)])
                        c = tr.op("scalar", "activation", b2.kexp[P][:, :],
                                  b2.kcm[P][:, :], AF.Exp, inc=S.s_ac2)
                        h[("kexp", n)] = c
                        tr.wait("vector", S.s_ac2, h[("kexp", n)])
                        for hd in range(NH):
                            c = tr.op("vector", "tensor_scalar",
                                      b2.kexp[P][:, hd * 128:(hd + 1) * 128],
                                      b2.kexp[P][:, hd * 128:(hd + 1) * 128],
                                      escale[:, hd:hd + 1], None, ALU.mult,
                                      inc=S.s_dv2)
                        tr.wait("vector", S.s_dv2, c)
                        if n >= 2:
                            tr.wait("vector", S.s_pe2, h[("omm", n - 2)])
                        ckb = tr.op("vector", "tensor_copy",
                                    b2.kexpb[P][:, :], b2.kexp[P][:, :],
                                    inc=S.s_dv2)
                        for hd in range(NH):
                            csc = tr.op("vector", "tensor_tensor_scan",
                                        b2.zfull[P][:, hd * 128:
                                                    (hd + 1) * 128],
                                        b2.ones64[:, :],
                                        b2.kexp[P][:, hd * 128:
                                                   (hd + 1) * 128],
                                        b2.Scar[:, hd * 128 + 64:
                                                hd * 128 + 65],
                                        ALU.mult, ALU.add,
                                        inc=S.s_dv2 if hd == NH - 1 else None)
                        h[("csc", n)] = csc
                        tr.wait("vector", S.s_dv2, csc)
                        tr.opw("vector", "tensor_scalar", b2.ztmp[P][:, :],
                               b2.zfull[P][:, :], 1e-9, None, ALU.add,
                               inc=S.s_dv2)
                        tr.opw("vector", "reciprocal", b2.ztmp[P][:, :],
                               b2.ztmp[P][:, :], inc=S.s_dv2)
                        tr.wait("vector", S.d_x, h[("ld", n)])
                        cqt = tr.op("vector", "tensor_tensor",
                                    b2.qtb[P][:, :], b2.qscm[P][:, :],
                                    b2.ztmp[P][:, :], ALU.mult, inc=S.s_dv2)
                        h[("qtb", n)] = cqt
                        for hd in range(NH + 1):
                            if hd < NH:
                                slot = hd % 2
                                tr.wait("tensor", S.s_dv2, cqt)
                                if hd >= 2:
                                    tr.wait("tensor", S.s_dv2,
                                            h[("am", n, hd - 2)])
                                c = tr.op("tensor", "matmul",
                                          b2.pA[slot][:, :],
                                          b2.kexpb[P][:, hd * 128:
                                                      (hd + 1) * 128],
                                          b2.qtb[P][:, hd * 128:
                                                   (hd + 1) * 128],
                                          start=True, stop=True,
                                          skip_group_check=True, inc=S.s_pe2)
                                h[("amm", n, hd)] = c
                                tr.wait("vector", S.s_pe2, h[("amm", n, hd)])
                                c = tr.op("vector", "tensor_tensor",
                                          b2.abf[slot][:, :],
                                          b2.pA[slot][:, :],
                                          umaskf[:, :], ALU.mult, inc=S.s_dv2)
                                h[("am", n, hd)] = c
                            if hd >= 1:
                                hp = hd - 1
                                slotp = hp % 2
                                tr.wait("tensor", S.s_dv2, h[("am", n, hp)])
                                if n > 0 and hp == 0:
                                    tr.wait("tensor", S.s_dv2,
                                            h[("ofm", n - 1, NH - 1)])
                                tr.op("tensor", "matmul",
                                      b2.pO[0:64, hp * 128:(hp + 1) * 128],
                                      b2.Scarb[:, hp * HS:(hp + 1) * HS],
                                      b2.qtb[P][:, hp * 128:(hp + 1) * 128],
                                      start=(hp % 4 == 0), stop=False,
                                      skip_group_check=True)
                                c = tr.op("tensor", "matmul",
                                          b2.pO[0:64, hp * 128:
                                                (hp + 1) * 128],
                                          b2.vchunk[P][:, hp * HS:
                                                       (hp + 1) * HS],
                                          b2.abf[slotp][:, :],
                                          start=False, stop=(hp % 4 == 3),
                                          skip_group_check=True, inc=S.s_pe2)
                                h[("opair", n, hp)] = c
                                if hp % 4 == 3:
                                    tr.wait("vector", S.s_pe2,
                                            h[("opair", n, hp)])
                                    if hp == 3 and n >= 2:
                                        tr.wait("vector", S.s_pe2,
                                                h[("xomm", n - 2, 1)])
                                    for hq in range(hp - 3, hp + 1):
                                        sub = hq % 2
                                        kd = hq // 2
                                        c = tr.op("vector", "tensor_copy",
                                                  b2.ofm[P][sub * 64:
                                                            (sub + 1) * 64,
                                                            kd * 128:
                                                            (kd + 1) * 128],
                                                  b2.pO[0:64, hq * 128:
                                                        (hq + 1) * 128],
                                                  inc=S.s_dv2)
                                        h[("ofm", n, hq)] = c
                                        tr.op("vector",
                                              "scalar_tensor_tensor",
                                              b2.Scar[:, hq * 128:
                                                      (hq + 1) * 128],
                                              b2.schk[P][:, hq * 128:
                                                         (hq + 1) * 128],
                                              escale[:, hq:hq + 1],
                                              b2.Scar[:, hq * 128:
                                                      (hq + 1) * 128],
                                              ALU.mult, ALU.add)
                        h[("omm", n)] = S.s_pe2.count
                        cupd = tr.op("vector", "nop", inc=S.s_dv2)
                        tr.wait("vector", S.s_dv2, cupd)
                        c = tr.opw("vector", "tensor_copy",
                                   b2.Scarb[:, :].rearrange(
                                       "p (h c) -> p h c", c=HS),
                                   b2.Scar[:, :].rearrange(
                                       "p (h c) -> p h c",
                                       c=128)[:, :, 0:64],
                                   inc=S.s_dv2)
                        h[("sup", n)] = c
                        if n == 0:
                            tr.wait("tensor", S.d_w, h["wo"])
                        for half in range(2):
                            tr.wait("tensor", S.s_dv2, h[("ofm", n, NH - 1)])
                            if n > 0:
                                tr.wait("tensor", S.s_dv2,
                                        h[("xadd", n - 1, half)])
                            for kd in range(NKC):
                                c = tr.op("tensor", "matmul",
                                          b2.pXO[half][:, :],
                                          b2.ofm[P][:, kd * 128:
                                                    (kd + 1) * 128],
                                          b2.woutT[:, kd * D + half * 512:
                                                   kd * D + (half + 1) * 512],
                                          start=(kd == 0),
                                          stop=(kd == NKC - 1),
                                          inc=S.s_pe2
                                          if kd == NKC - 1 else None)
                            h[("xomm", n, half)] = S.s_pe2.count
                            tr.wait("vector", S.s_pe2, h[("xomm", n, half)])
                            c = tr.op("vector", "tensor_tensor",
                                      b2.xres1[P][:, half * 512:
                                                  (half + 1) * 512],
                                      b2.xres1[P][:, half * 512:
                                                  (half + 1) * 512],
                                      b2.pXO[half][:, :], ALU.add,
                                      inc=S.s_dv2)
                            h[("xadd", n, half)] = c
                        tr.wait("gpsimd", S.s_dv2, h[("xadd", n, 1)])
                        c = tr.dma("gpsimd", stream[n * 128:(n + 1) * 128, :],
                                   b2.xres1[P][:, :], S.d_st)
                        h[("stc", n)] = c
                    tr.wait("gpsimd", S.d_st, S.d_st.count)

                run_phase(nc, allsems, emit_p2b)


        if "p3" in phases:
            with ExitStack() as c4:
                f3 = make_ffn_tensors(c4, "f3_")
                run_phase(nc, allsems,
                          lambda tr: emit_ffn(tr, f3, stream, y_out))
        elif "copy" in phases:
            def emit_copy(tr):
                c = tr.dma("gpsimd", y_out[:, :], stream[:, :], S.d_st)
                tr.wait("gpsimd", S.d_st, c)

            run_phase(nc, allsems, emit_copy)

    return nc


def host_consts():
    identf = np.eye(128, dtype=np.float32)
    umask = np.triu(np.ones((128, 128), dtype=np.float32))
    slotb = np.broadcast_to(
        (np.arange(E, dtype=np.float32) * CAP - 1.0)[None, :],
        (128, E)).copy()
    prefm = []
    seqb = []
    for c in range(NCORES):
        pm = np.zeros((64, 8), dtype=np.float32)
        sb_ = np.full((64, 8), -1e38, dtype=np.float32)
        for r in range(NCORES):
            if r // 4 == c // 4:
                sb_[:, r] = 0.0
                if r % 4 < c % 4:
                    pm[:, r] = 1.0
        prefm.append(pm)
        seqb.append(sb_)
    return identf, umask, slotb, prefm, seqb


def kernel(L0_attn_x, L0_attn_W_qkv, L0_attn_W_out, L0_ffn_gate_w,
           L0_ffn_w1, L0_ffn_w3, L0_ffn_w2):
    x = np.ascontiguousarray(np.asarray(L0_attn_x, dtype=np.float32))
    wqkv = np.ascontiguousarray(np.asarray(L0_attn_W_qkv, dtype=np.float32))
    wout = np.ascontiguousarray(np.asarray(L0_attn_W_out, dtype=np.float32))
    gw = np.ascontiguousarray(np.asarray(L0_ffn_gate_w, dtype=np.float32))
    w1 = np.ascontiguousarray(np.asarray(L0_ffn_w1, dtype=np.float32))
    w3 = np.ascontiguousarray(np.asarray(L0_ffn_w3, dtype=np.float32))
    w2 = np.ascontiguousarray(np.asarray(L0_ffn_w2, dtype=np.float32))

    identf, umask, slotb, prefm, seqb = host_consts()
    nc = build_program()
    in_maps = []
    for c in range(NCORES):
        b = c // 4
        t0 = (c % 4) * TPC
        in_maps.append({
            "x": np.ascontiguousarray(x[b, t0:t0 + TPC, :]),
            "wqkv": wqkv, "wout": wout, "gw": gw,
            "w1": w1, "w3": w3, "w2": w2,
            "identf": identf, "umask": umask, "slotb": slotb,
            "prefm": prefm[c], "seqb": seqb[c],
        })
    res = run_bass_kernel_spmd(nc, in_maps, core_ids=list(range(NCORES)))
    out = np.empty((B, T, D), dtype=np.float32)
    for c in range(NCORES):
        b = c // 4
        t0 = (c % 4) * TPC
        out[b, t0:t0 + TPC, :] = np.asarray(res.results[c]["y"],
                                            dtype=np.float32)
    return out


# version 08
# speedup vs baseline: 1.6897x; 1.0128x over previous
import numpy as np
import ml_dtypes
from contextlib import ExitStack

import concourse.bass as bass
import concourse.mybir as mybir
from concourse.bass_utils import run_bass_kernel_spmd

F32 = mybir.dt.float32
BF16 = mybir.dt.bfloat16
AF = mybir.ActivationFunctionType
ALU = mybir.AluOpType
AX = mybir.AxisListType

NCORES = 8
B, T, D = 2, 16384, 1024
NH, HS = 16, 64
E, HID = 8, 2816
TPC = T * B // NCORES
TT = 1024
NTILE = TPC // TT
NTC = TT // 128
NKC = D // 128
NHC = HID // 128
HGROUPS = [(0, 8), (8, 16), (16, 22)]
HBLOCKS = [(0, 4), (4, 8), (8, 12), (12, 16), (16, 20), (20, 22)]
NCH = TPC // 128
LNEPS = 1e-5
CAP = 1280
ZPAD = 2048
SENT = E * CAP
I32 = mybir.dt.int32

ENGS = ("sync", "gpsimd", "vector", "tensor", "scalar")


class Sem:
    __slots__ = ("h", "count")

    def __init__(self, h):
        self.h = h
        self.count = 0


class Tr:

    def __init__(self, active, eng):
        self.a = active
        self.e = eng
        self.hist = {}

    def op(self, name, fn, *args, inc=None, incn=1, **kw):
        if self.a == name:
            inst = getattr(self.e, fn)(*args, **kw)
            if inc is not None:
                inst.then_inc(inc.h, incn)
        if inc is not None:
            inc.count += incn
            return inc.count
        return None

    def wait(self, name, sem, val):
        if val is None or val <= 0:
            return
        if self.a == name:
            self.e.wait_ge(sem.h, val)

    def dma(self, name, out, in_, sem, **kw):
        return self.op(name, "dma_start", inc=sem, incn=16, out=out, in_=in_, **kw)

    def opw(self, name, fn, *args, inc, incn=1, **kw):
        c = self.op(name, fn, *args, inc=inc, incn=incn, **kw)
        self.wait(name, inc, c)
        return c


def run_phase(nc, sems, emit):
    base = [(s, s.count) for s in sems]
    with nc.Block() as block:
        for name in ENGS:
            for s, c in base:
                s.count = c

            def body(eng, _n=name):
                emit(Tr(_n, eng))

            getattr(block, name)(body)


class NS:
    pass


def build_program(phases=("p0", "p1", "p2", "p3"), dump=False):
    nc = bass.Bass("TRN2", target_bir_lowering=False)
    t = NS()

    x_in = nc.dram_tensor("x", [TPC, D], F32, kind="ExternalInput")
    wqkv_in = nc.dram_tensor("wqkv", [3 * D, D], F32, kind="ExternalInput")
    wout_in = nc.dram_tensor("wout", [D, D], F32, kind="ExternalInput")
    gw_in = nc.dram_tensor("gw", [E, D], F32, kind="ExternalInput")
    w1_in = nc.dram_tensor("w1", [E, HID, D], F32, kind="ExternalInput")
    w3_in = nc.dram_tensor("w3", [E, HID, D], F32, kind="ExternalInput")
    w2_in = nc.dram_tensor("w2", [E, D, HID], F32, kind="ExternalInput")
    identf_in = nc.dram_tensor("identf", [128, 128], F32, kind="ExternalInput")
    umask_in = nc.dram_tensor("umask", [128, 128], F32, kind="ExternalInput")
    prefm_in = nc.dram_tensor("prefm", [64, 8], F32, kind="ExternalInput")
    seqb_in = nc.dram_tensor("seqb", [64, 8], F32, kind="ExternalInput")
    slotb_in = nc.dram_tensor("slotb", [128, 8], F32, kind="ExternalInput")
    y_out = nc.dram_tensor("y", [TPC, D], F32, kind="ExternalOutput")

    w1_bf = nc.dram_tensor("w1_bf", [E, HID, D], BF16, kind="Internal")
    w3_bf = nc.dram_tensor("w3_bf", [E, HID, D], BF16, kind="Internal")
    w2_bf = nc.dram_tensor("w2_bf", [E, D, HID], BF16, kind="Internal")
    wqkv_bf = nc.dram_tensor("wqkv_bf", [3 * D, D], BF16, kind="Internal")
    wout_bf = nc.dram_tensor("wout_bf", [D, D], BF16, kind="Internal")
    stream = nc.dram_tensor("stream", [TPC, D], F32, kind="Internal")
    kcm_d = nc.dram_tensor("kcm_d", [NCH, 64, NH * 128], F32, kind="Internal")
    qscm_d = nc.dram_tensor("qscm_d", [NCH, 64, NH * 128], F32, kind="Internal")
    vtm_d = nc.dram_tensor("vtm_d", [TPC, D], BF16, kind="Internal")
    schunks_d = nc.dram_tensor("schunks_d", [NCH, 64, NH * 128], F32, kind="Internal")
    CCW = 16 + NH * 128
    ccsrc_d = nc.dram_tensor("ccsrc_d", [64, CCW], F32, kind="Internal")
    ccdst_d = nc.dram_tensor("ccdst_d", [64 * NCORES, CCW], F32, kind="Internal")
    xnbf_d = nc.dram_tensor("xnbf_d", [TPC, D], F32, kind="Internal")
    xg_d = nc.dram_tensor("xg_d", [E * CAP + 1, D], F32, kind="Internal")
    y_d = nc.dram_tensor("y_d", [E * CAP + ZPAD, D], F32, kind="Internal")
    if dump:
        xg_dump = nc.dram_tensor("xg_dump", [E * CAP + 1, D], F32,
                                 kind="ExternalOutput")
        yd_dump = nc.dram_tensor("yd_dump", [E * CAP, D], F32,
                                 kind="ExternalOutput")
        xn_dump = nc.dram_tensor("xn_dump", [TPC, D], F32,
                                 kind="ExternalOutput")

    with ExitStack() as top:
        def sb(ctx, name, shape, dt=F32):
            return ctx.enter_context(nc.sbuf_tensor(name, shape, dt))

        def ps(ctx, name, shape, dt=F32):
            return ctx.enter_context(nc.psum_tensor(name, shape, dt))

        identf = sb(top, "identf_s", [128, 128])
        umaskf = sb(top, "umaskf_s", [128, 128])
        prefm = sb(top, "prefm_s", [64, 8])
        seqb = sb(top, "seqb_s", [64, 8])
        gwsb = sb(top, "gwsb", [8, D])
        slotbase = sb(top, "slotbase_s", [128, 8])
        gwT = sb(top, "gwT", [128, 8 * E])
        kmaxloc = sb(top, "kmaxloc", [64, NH])
        escale = sb(top, "escale", [64, NH])
        negm = sb(top, "negm", [64, NH])

        semnames = ["d_cin", "d_wc", "d_w", "d_x", "d_st", "d_sp",
                    "d_gb0", "d_gb1", "cc_s",
                    "s_tr", "s_tev", "s_amm", "s_silu", "s_hm", "s_bmm",
                    "s_bep", "s_w13c", "s_w2c", "s_gmm", "s_dv", "s_ac",
                    "s_pe2", "s_dv2", "s_ac2", "s_misc"]
        sems = {n: Sem(top.enter_context(nc.semaphore(n))) for n in semnames}
        S = NS()
        for n, s in sems.items():
            setattr(S, n, s)
        allsems = list(sems.values())

        if "p0" in phases:
            with ExitStack() as c0:
                pg0 = ps(c0, "pg0", [128, 16])
                pg1 = ps(c0, "pg1", [128, 16])
                z0 = sb(c0, "z0", [128, D])

                def emit_p0(tr):
                    for ap_out, ap_in in [(identf[:, :], identf_in[:, :]),
                                          (umaskf[:, :], umask_in[:, :]),
                                          (prefm[:, :], prefm_in[:, :]),
                                          (seqb[:, :], seqb_in[:, :]),
                                          (gwsb[:, :], gw_in[:, :]),
                                          (slotbase[:, :], slotb_in[:, :])]:
                        tr.dma("sync", ap_out, ap_in, S.d_cin)
                    cin_done = S.d_cin.count
                    tr.wait("tensor", S.d_cin, cin_done)
                    for k in range(NKC):
                        pg = pg0 if k % 2 == 0 else pg1
                        if k >= 2:
                            tr.wait("tensor", S.s_tev, tr.hist[("gev", k - 2)])
                        c = tr.op("tensor", "transpose", pg[:, 0:8],
                                  gwsb[0:8, k * 128:(k + 1) * 128],
                                  identf[0:8, 0:8], inc=S.s_tr)
                        tr.hist[("gtr", k)] = c
                        tr.wait("vector", S.s_tr, tr.hist[("gtr", k)])
                        c = tr.op("vector", "tensor_copy",
                                  gwT[:, k * 8:(k + 1) * 8], pg[:, 0:8],
                                  inc=S.s_tev)
                        tr.hist[("gev", k)] = c
                    tr.op("vector", "memset", kmaxloc[:, :], -1e38)
                    cz = tr.op("vector", "memset", z0[:, :], 0.0, inc=S.s_dv)
                    tr.wait("gpsimd", S.s_dv, cz)
                    for i in range(ZPAD // 128):
                        tr.dma("gpsimd",
                               y_d[E * CAP + i * 128:E * CAP + (i + 1) * 128, :],
                               z0[:, :], S.d_st)

                run_phase(nc, allsems, emit_p0)

        def make_ffn_tensors(ctx, pfx):
            f = NS()
            dblf = lambda nm, shape, dt=F32: [
                sb(ctx, pfx + nm + str(i), shape, dt) for i in range(2)]
            f.xstg = dblf("xstg", [128, D])
            f.cst = dblf("cst", [128, D])
            f.sqst = sb(ctx, pfx + "sqst", [128, D])
            f.xntm = dblf("xntm", [128, D])
            f.xnf = dblf("xnf", [128, D])
            f.coef = sb(ctx, pfx + "coef", [128, NCH * E])
            f.maskcm = sb(ctx, pfx + "maskcm", [8, TPC])
            f.zeros8 = sb(ctx, pfx + "zeros8", [8, 1024])
            f.postm = sb(ctx, pfx + "postm", [128, 8])
            f.mtm = sb(ctx, pfx + "mtm", [128, 8])
            f.ovf = sb(ctx, pfx + "ovf", [128, 8])
            f.m2t = sb(ctx, pfx + "m2t", [128, 8])
            f.slf = sb(ctx, pfx + "slf", [128, 8])
            f.sl2 = sb(ctx, pfx + "sl2", [128, 8])
            f.mm1 = sb(ctx, pfx + "mm1", [128, 8])
            f.cm1 = sb(ctx, pfx + "cm1", [128, 8])
            f.o1 = sb(ctx, pfx + "o1", [128, 1])
            f.o2 = sb(ctx, pfx + "o2", [128, 1])
            f.oc = sb(ctx, pfx + "oc", [128, 1])
            f.c12 = sb(ctx, pfx + "c12", [128, 2 * NCH])
            f.sIdx = sb(ctx, pfx + "sIdx", [128, 2 * NCH], I32)
            f.gIdx = sb(ctx, pfx + "gIdx", [128, 2 * NCH], I32)
            f.xgfm = sb(ctx, pfx + "xgfm", [128, NKC * CAP], BF16)
            f.hbuf = sb(ctx, pfx + "hbuf", [128, 8 * CAP], BF16)
            f.outacc = sb(ctx, pfx + "outacc", [128, (CAP // 128) * D])
            f.w1b = [sb(ctx, pfx + "w1b%d" % i, [128, NKC * 512], BF16)
                     for i in range(2)]
            f.w3b = [sb(ctx, pfx + "w3b%d" % i, [128, NKC * 512], BF16)
                     for i in range(2)]
            f.w2b = sb(ctx, pfx + "w2b", [128, 8 * D], BF16)
            f.silu = [sb(ctx, pfx + "silu%d" % i, [128, 512]) for i in range(2)]
            f.g1g = sb(ctx, pfx + "g1g", [128, D])
            f.g2g = sb(ctx, pfx + "g2g", [128, D])
            f.r1 = dblf("r1", [128, 1])
            f.epsb = sb(ctx, pfx + "epsb", [128, 1])
            f.nmean = dblf("nmean", [128, 1])
            f.ssq = dblf("ssq", [128, 1])
            f.sd = dblf("sd", [128, 1])
            f.rstd = dblf("rstd", [128, 1])
            f.m1n = dblf("m1n", [128, 1])
            f.m2 = dblf("m2", [128, 1])
            f.d1 = dblf("d1", [128, 1])
            f.r2 = dblf("r2", [128, 1])
            f.gst = dblf("gst", [128, 8])
            f.ge1 = dblf("ge1", [128, 8])
            f.tmp8 = dblf("tmp8", [128, 8])
            f.gm = dblf("gm", [128, 8])
            f.pH1 = [ps(ctx, pfx + "pH1%d" % i, [128, 512]) for i in range(2)]
            f.pH3 = [ps(ctx, pfx + "pH3%d" % i, [128, 512]) for i in range(2)]
            f.pB = [ps(ctx, pfx + "pB%d" % i, [128, 512]) for i in range(2)]
            f.pT = [ps(ctx, pfx + "pT%d" % i, [128, 128]) for i in range(2)]
            return f

        xcounts = {}

        def emit_ffn(tr, f, src, dst, do_casts=False):
            h = tr.hist
            w13c0 = S.s_w13c.count
            w2c0 = S.s_w2c.count
            if do_casts:
                for e in range(E):
                    tr.dma("gpsimd", w1_bf[e, :, :], w1_in[e, :, :], S.d_wc)
                    tr.dma("gpsimd", w3_bf[e, :, :], w3_in[e, :, :], S.d_wc)
                    c = tr.dma("gpsimd", w2_bf[e, :, :], w2_in[e, :, :], S.d_wc)
                    xcounts[("wc", e)] = c
                xcounts[("wc_qkv",)] = tr.dma("gpsimd", wqkv_bf[:, :],
                                              wqkv_in[:, :], S.d_wc)
                xcounts[("wc_out",)] = tr.dma("gpsimd", wout_bf[:, :],
                                              wout_in[:, :], S.d_wc)
            xgf0 = S.s_misc.count
            tr.op("vector", "memset", f.epsb[:, :], LNEPS, inc=S.s_dv)
            tr.op("vector", "memset", f.zeros8[:, :], 0.0, inc=S.s_dv)
            trk = 0

            for tc in range(NCH):
                P = tc % 2
                if tc >= 2:
                    tr.wait("sync", S.s_dv, h[("lnd", tc - 2)])
                c = tr.dma("sync", f.xstg[P][:, :],
                           src[tc * 128:(tc + 1) * 128, :], S.d_x)
                h[("xld", tc)] = c
                tr.wait("vector", S.d_x, h[("xld", tc)])
                if tc >= 2:
                    tr.wait("vector", S.s_ac, h[("gex", tc - 2)])
                tr.opw("vector", "reduce_sum", f.r1[P][:, :],
                       f.xstg[P][:, :], axis=AX.X, inc=S.s_dv)
                tr.opw("vector", "tensor_scalar", f.nmean[P][:, :],
                       f.r1[P][:, :], -1.0 / D, None, ALU.mult, inc=S.s_dv)
                c = tr.op("vector", "tensor_scalar", f.cst[P][:, :],
                          f.xstg[P][:, :], f.nmean[P][:, :], None, ALU.add,
                          inc=S.s_dv)
                h[("lnd", tc)] = c
                tr.opw("vector", "scalar_tensor_tensor", f.sqst[:, :],
                       f.cst[P][:, :], 1.0, f.cst[P][:, :], ALU.mult,
                       ALU.mult, accum_out=f.ssq[P][:, :], inc=S.s_dv)
                tr.wait("scalar", S.s_dv, S.s_dv.count)
                tr.opw("scalar", "activation", f.sd[P][:, :], f.ssq[P][:, :],
                       AF.Ln, scale=1.0 / D, bias=f.epsb[:, :], inc=S.s_ac)
                c = tr.op("scalar", "activation", f.rstd[P][:, :],
                          f.sd[P][:, :], AF.Exp, scale=-0.5, inc=S.s_ac)
                tr.wait("vector", S.s_ac, c)
                if tc >= 2:
                    tr.wait("vector", S.s_tr, h[("trl", tc - 2)])
                    tr.wait("vector", S.d_sp, h[("xnst", tc - 2)])
                cxn = tr.op("vector", "tensor_scalar", f.xntm[P][:, :],
                            f.cst[P][:, :], f.rstd[P][:, :], None, ALU.mult,
                            inc=S.s_dv)
                tr.wait("sync", S.s_dv, cxn)
                c = tr.dma("sync", xnbf_d[tc * 128:(tc + 1) * 128, :],
                           f.xntm[P][:, :], S.d_sp)
                h[("xnst", tc)] = c
                tr.wait("tensor", S.s_dv, cxn)
                if tc >= 2:
                    tr.wait("tensor", S.s_gmm, h[("g", tc - 2)])
                for k in range(NKC):
                    slot = trk % 2
                    if trk >= 2:
                        tr.wait("tensor", S.s_tev, h[("tev", trk - 2)])
                    c = tr.op("tensor", "transpose", f.pT[slot][:, :],
                              f.xntm[P][:, k * 128:(k + 1) * 128],
                              identf[:, :], inc=S.s_tr)
                    h[("tr", trk)] = c
                    tr.wait("vector", S.s_tr, h[("tr", trk)])
                    c = tr.op("vector", "tensor_copy",
                              f.xnf[P][:, k * 128:(k + 1) * 128],
                              f.pT[slot][:, :], inc=S.s_tev)
                    h[("tev", trk)] = c
                    trk += 1
                tr.wait("tensor", S.s_tev, h[("tev", trk - 1)])
                if tc >= 2:
                    tr.wait("tensor", S.s_ac, h[("gex", tc - 2)])
                for k in range(NKC):
                    tr.op("tensor", "matmul", f.pB[P][:, 0:8],
                          f.xnf[P][:, k * 128:(k + 1) * 128],
                          gwT[:, k * 8:(k + 1) * 8],
                          start=(k == 0), stop=(k == NKC - 1),
                          skip_group_check=True,
                          inc=S.s_gmm if k == NKC - 1 else None)
                h[("g", tc)] = S.s_gmm.count
                tr.wait("vector", S.s_gmm, h[("g", tc)])
                c = tr.op("vector", "reduce_max", f.m1n[P][:, :],
                          f.pB[P][:, 0:8], axis=AX.X, negate=True,
                          inc=S.s_dv)
                tr.wait("scalar", S.s_dv, c)
                c = tr.op("scalar", "activation", f.gst[P][:, :],
                          f.pB[P][:, 0:8], AF.Exp, bias=f.m1n[P][:, :],
                          inc=S.s_ac)
                h[("gex", tc)] = c
                tr.wait("vector", S.s_ac, c)
                tr.opw("vector", "tensor_single_scalar", f.ge1[P][:, :],
                       f.gst[P][:, :], 1.0, ALU.is_ge, inc=S.s_dv)
                tr.opw("vector", "scalar_tensor_tensor", f.tmp8[P][:, :],
                       f.ge1[P][:, :], -1e30, f.gst[P][:, :], ALU.mult,
                       ALU.add, inc=S.s_dv)
                tr.opw("vector", "reduce_max", f.m2[P][:, :],
                       f.tmp8[P][:, :], axis=AX.X, inc=S.s_dv)
                tr.opw("vector", "tensor_single_scalar", f.tmp8[P][:, :],
                       f.gst[P][:, :], f.m2[P][:, :], ALU.is_ge, inc=S.s_dv)
                tr.opw("vector", "tensor_tensor", f.gm[P][:, :],
                       f.gst[P][:, :], f.tmp8[P][:, :], ALU.mult, inc=S.s_dv)
                tr.opw("vector", "tensor_scalar", f.d1[P][:, :],
                       f.m2[P][:, :], 1.0, None, ALU.add, inc=S.s_dv)
                tr.opw("vector", "reciprocal", f.r2[P][:, :], f.d1[P][:, :],
                       inc=S.s_dv)
                tr.opw("vector", "tensor_scalar",
                       f.coef[:, tc * E:(tc + 1) * E], f.gm[P][:, :],
                       f.r2[P][:, :], 0.5, ALU.mult, ALU.mult, inc=S.s_dv)
                cmk = tr.opw("vector", "tensor_single_scalar", f.gm[P][:, :],
                             f.coef[:, tc * E:(tc + 1) * E], 0.0, ALU.is_gt,
                             inc=S.s_dv)
                slot = trk % 2
                tr.wait("tensor", S.s_dv, cmk)
                if trk >= 2:
                    tr.wait("tensor", S.s_tev, h[("tev", trk - 2)])
                c = tr.op("tensor", "transpose", f.pT[slot][0:8, 0:128],
                          f.gm[P][:, :], identf[:, :], inc=S.s_tr)
                h[("tr", trk)] = c
                h[("trl", tc)] = c
                tr.wait("vector", S.s_tr, h[("tr", trk)])
                c = tr.op("vector", "tensor_copy",
                          f.maskcm[:, tc * 128:(tc + 1) * 128],
                          f.pT[slot][0:8, 0:128], inc=S.s_tev)
                h[("tev", trk)] = c
                trk += 1
            h[("g1_done",)] = S.s_dv.count

            tr.wait("vector", S.s_tev, h[("tev", trk - 1)])
            for b in range(4):
                init = (f.zeros8[:, 0:1] if b == 0
                        else f.maskcm[:, b * 1024 - 1:b * 1024])
                tr.opw("vector", "tensor_tensor_scan",
                       f.maskcm[:, b * 1024:(b + 1) * 1024],
                       f.zeros8[:, :],
                       f.maskcm[:, b * 1024:(b + 1) * 1024],
                       init, ALU.add, ALU.add, inc=S.s_dv)
            for tc in range(NCH):
                slot = trk % 2
                if trk >= 2:
                    tr.wait("tensor", S.s_tev, h[("tev", trk - 2)])
                tr.wait("tensor", S.s_dv, S.s_dv.count)
                c = tr.op("tensor", "transpose", f.pT[slot][:, 0:8],
                          f.maskcm[:, tc * 128:(tc + 1) * 128],
                          identf[0:8, 0:8], inc=S.s_tr)
                h[("tr", trk)] = c
                tr.wait("vector", S.s_tr, h[("tr", trk)])
                c = tr.op("vector", "tensor_copy", f.postm[:, :],
                          f.pT[slot][:, 0:8], inc=S.s_tev)
                h[("tev", trk)] = c
                trk += 1
                tr.wait("vector", S.s_tev, h[("tev", trk - 1)])
                tr.opw("vector", "tensor_single_scalar", f.mtm[:, :],
                       f.coef[:, tc * E:(tc + 1) * E], 0.0, ALU.is_gt,
                       inc=S.s_dv)
                tr.opw("vector", "tensor_single_scalar", f.ovf[:, :],
                       f.postm[:, :], float(CAP), ALU.is_le, inc=S.s_dv)
                tr.opw("vector", "tensor_tensor", f.m2t[:, :], f.mtm[:, :],
                       f.ovf[:, :], ALU.mult, inc=S.s_dv)
                tr.opw("vector", "tensor_tensor", f.slf[:, :], f.postm[:, :],
                       slotbase[:, :], ALU.add, inc=S.s_dv)
                tr.opw("vector", "tensor_tensor", f.slf[:, :], f.slf[:, :],
                       f.m2t[:, :], ALU.mult, inc=S.s_dv)
                tr.opw("vector", "tensor_scalar", f.sl2[:, :], f.m2t[:, :],
                       -float(SENT), float(SENT), ALU.mult, ALU.add,
                       inc=S.s_dv)
                tr.opw("vector", "tensor_tensor", f.slf[:, :], f.slf[:, :],
                       f.sl2[:, :], ALU.add, inc=S.s_dv)
                tr.opw("vector", "tensor_reduce", f.o1[:, :], f.slf[:, :],
                       axis=AX.X, op=ALU.min, inc=S.s_dv)
                tr.opw("vector", "tensor_single_scalar", f.mm1[:, :],
                       f.slf[:, :], f.o1[:, :], ALU.is_equal, inc=S.s_dv)
                tr.opw("vector", "scalar_tensor_tensor", f.sl2[:, :],
                       f.mm1[:, :], float(SENT), f.slf[:, :], ALU.mult,
                       ALU.add, inc=S.s_dv)
                tr.opw("vector", "tensor_reduce", f.o2[:, :], f.sl2[:, :],
                       axis=AX.X, op=ALU.min, inc=S.s_dv)
                tr.opw("vector", "tensor_tensor", f.cm1[:, :], f.mm1[:, :],
                       f.coef[:, tc * E:(tc + 1) * E], ALU.mult, inc=S.s_dv)
                tr.opw("vector", "reduce_sum", f.c12[:, 2 * tc:2 * tc + 1],
                       f.cm1[:, :], axis=AX.X, inc=S.s_dv)
                tr.opw("vector", "tensor_single_scalar", f.mm1[:, :],
                       f.sl2[:, :], f.o2[:, :], ALU.is_equal, inc=S.s_dv)
                tr.opw("vector", "tensor_tensor", f.cm1[:, :], f.mm1[:, :],
                       f.coef[:, tc * E:(tc + 1) * E], ALU.mult, inc=S.s_dv)
                tr.opw("vector", "reduce_sum", f.c12[:, 2 * tc + 1:2 * tc + 2],
                       f.cm1[:, :], axis=AX.X, inc=S.s_dv)
                tr.opw("vector", "tensor_copy", f.sIdx[:, 2 * tc:2 * tc + 1],
                       f.o1[:, :], inc=S.s_dv)
                tr.opw("vector", "tensor_copy",
                       f.sIdx[:, 2 * tc + 1:2 * tc + 2], f.o2[:, :],
                       inc=S.s_dv)
                tr.opw("vector", "tensor_single_scalar", f.oc[:, :],
                       f.o1[:, :], float(E * CAP), ALU.min, inc=S.s_dv)
                tr.opw("vector", "tensor_copy", f.gIdx[:, 2 * tc:2 * tc + 1],
                       f.oc[:, :], inc=S.s_dv)
                tr.opw("vector", "tensor_single_scalar", f.oc[:, :],
                       f.o2[:, :], float(E * CAP), ALU.min, inc=S.s_dv)
                c = tr.opw("vector", "tensor_copy",
                           f.gIdx[:, 2 * tc + 1:2 * tc + 2], f.oc[:, :],
                           inc=S.s_dv)
                h[("idx", tc)] = c
            h[("g2_done",)] = S.s_dv.count

            tr.wait("gpsimd", S.s_dv, h[("g2_done",)])
            tr.wait("gpsimd", S.d_sp, h[("xnst", NCH - 1)])
            for tc in range(NCH):
                buf = f.g1g if tc % 2 == 0 else f.g2g
                if tc >= 1:
                    tr.wait("gpsimd", S.d_sp, h[("sc", tc - 1)])
                c = tr.dma("gpsimd", buf[:, :],
                           xnbf_d[tc * 128:(tc + 1) * 128, :], S.d_x)
                tr.wait("gpsimd", S.d_x, c)
                for kk in range(2):
                    c = tr.op("gpsimd", "indirect_dma_start",
                              out=xg_d[:, :],
                              out_offset=bass.IndirectOffsetOnAxis(
                                  f.sIdx[:, 2 * tc + kk:2 * tc + kk + 1], 0),
                              in_=buf[:, :], in_offset=None,
                              inc=S.d_sp, incn=16)
                h[("sc", tc)] = c
            h[("scat_done",)] = c
            tr.wait("gpsimd", S.d_sp, h[("scat_done",)])

            astep = 0
            bstep = 0
            blkidx = 0
            grpidx = 0
            NTE = CAP // 128
            ASL = [(0, 512), (512, 1024), (1024, CAP)]
            for e in range(E):
                for te in range(NTE):
                    gi2 = e * NTE + te
                    buf = f.g1g if gi2 % 2 == 0 else f.g2g
                    if gi2 >= 1:
                        tr.wait("gpsimd", S.s_tr, h[("xgtr", gi2 - 1)])
                    c = tr.dma("gpsimd", buf[:, :],
                               xg_d[e * CAP + te * 128:
                                    e * CAP + (te + 1) * 128, :], S.d_x)
                    h[("xg", e, te)] = c
                    tr.wait("tensor", S.d_x, h[("xg", e, te)])
                    if e > 0 and te == 0:
                        tr.wait("tensor", S.s_amm, h[("aend", e - 1)])
                    for k in range(NKC):
                        slot = trk % 2
                        if trk >= 2:
                            tr.wait("tensor", S.s_tev, h[("tev", trk - 2)])
                        c = tr.op("tensor", "transpose", f.pT[slot][:, :],
                                  buf[:, k * 128:(k + 1) * 128],
                                  identf[:, :], inc=S.s_tr)
                        h[("tr", trk)] = c
                        tr.wait("vector", S.s_tr, h[("tr", trk)])
                        c = tr.op("vector", "tensor_copy",
                                  f.xgfm[:, k * CAP + te * 128:
                                         k * CAP + (te + 1) * 128],
                                  f.pT[slot][:, :], inc=S.s_tev)
                        h[("tev", trk)] = c
                        trk += 1
                    h[("xgtr", gi2)] = S.s_tr.count
                tr.op("vector", "nop", inc=S.s_misc)
                h[("xgfm", e)] = S.s_misc.count
                for gi, (g0, g1) in enumerate(HGROUPS):
                    for bl in (2 * gi, 2 * gi + 1):
                        hb0, hb1 = HBLOCKS[bl]
                        nh2 = hb1 - hb0
                        if gi == 0 and bl == 0:
                            tr.wait("sync", S.d_wc, xcounts.get(("wc", e)))
                        if blkidx >= 2:
                            tr.wait("sync", S.s_w13c, w13c0 + blkidx - 1)
                        for k in range(NKC):
                            tr.dma("sync",
                                   f.w1b[blkidx % 2][:, k * 512:
                                                     k * 512 + nh2 * 128],
                                   w1_bf[e, hb0 * 128:hb1 * 128,
                                         k * 128:(k + 1) * 128],
                                   S.d_w, transpose=True)
                            c = tr.dma("sync",
                                       f.w3b[blkidx % 2][:, k * 512:
                                                         k * 512 + nh2 * 128],
                                       w3_bf[e, hb0 * 128:hb1 * 128,
                                             k * 128:(k + 1) * 128],
                                       S.d_w, transpose=True)
                        h[("wblk", blkidx)] = c
                        blkidx += 1
                    if grpidx >= 1:
                        tr.wait("sync", S.s_w2c, w2c0 + grpidx)
                    for hl, hh in enumerate(range(g0, g1)):
                        c = tr.dma("sync", f.w2b[:, hl * D:(hl + 1) * D],
                                   w2_bf[e, 0:D, hh * 128:(hh + 1) * 128],
                                   S.d_w, transpose=True)
                    h[("w2g", grpidx)] = c
                    grpidx += 1

                    for hh in range(g0, g1):
                        bl = hh // 4
                        gblk = e * 6 + bl
                        par = gblk % 2
                        hloc = hh - HBLOCKS[bl][0]
                        for n, (n0, n1) in enumerate(ASL):
                            p = astep % 2
                            tr.wait("tensor", S.d_w, h[("wblk", gblk)])
                            if astep == 0:
                                tr.wait("tensor", S.s_misc, h[("xgfm", e)])
                            if astep >= 2:
                                tr.wait("tensor", S.s_silu,
                                        h[("silu", astep - 2)])
                                tr.wait("tensor", S.s_hm, h[("hm", astep - 2)])
                            for k in range(NKC):
                                tr.op("tensor", "matmul",
                                      f.pH1[p][:, 0:n1 - n0],
                                      f.w1b[par][:, k * 512 + hloc * 128:
                                                 k * 512 + (hloc + 1) * 128],
                                      f.xgfm[:, k * CAP + n0:k * CAP + n1],
                                      start=(k == 0), stop=(k == NKC - 1))
                            for k in range(NKC):
                                tr.op("tensor", "matmul",
                                      f.pH3[p][:, 0:n1 - n0],
                                      f.w3b[par][:, k * 512 + hloc * 128:
                                                 k * 512 + (hloc + 1) * 128],
                                      f.xgfm[:, k * CAP + n0:k * CAP + n1],
                                      start=(k == 0), stop=(k == NKC - 1),
                                      inc=S.s_amm if k == NKC - 1 else None)
                            h[("amm", astep)] = S.s_amm.count
                            tr.wait("scalar", S.s_amm, h[("amm", astep)])
                            if astep >= 2:
                                tr.wait("scalar", S.s_hm, h[("hm", astep - 2)])
                            c = tr.op("scalar", "activation",
                                      f.silu[p][:, 0:n1 - n0],
                                      f.pH1[p][:, 0:n1 - n0], AF.Silu,
                                      inc=S.s_silu)
                            h[("silu", astep)] = c
                            tr.wait("vector", S.s_silu, h[("silu", astep)])
                            c = tr.op("vector", "tensor_tensor",
                                      f.hbuf[:, (hh - g0) * CAP + n0:
                                             (hh - g0) * CAP + n1],
                                      f.silu[p][:, 0:n1 - n0],
                                      f.pH3[p][:, 0:n1 - n0],
                                      ALU.mult, inc=S.s_hm)
                            h[("hm", astep)] = c
                            astep += 1
                        if hh == HBLOCKS[bl][1] - 1:
                            tr.op("tensor", "nop", inc=S.s_w13c)
                    nhg = g1 - g0
                    first_b = True
                    for te in range(NTE):
                        for half in range(2):
                            p = bstep % 2
                            if first_b:
                                tr.wait("tensor", S.s_hm, h[("hm", astep - 1)])
                                tr.wait("tensor", S.d_w,
                                        h[("w2g", grpidx - 1)])
                                first_b = False
                            if bstep >= 2:
                                tr.wait("tensor", S.s_bep,
                                        h[("bep", bstep - 2)])
                            for hl in range(nhg):
                                tr.op("tensor", "matmul", f.pB[p][:, :],
                                      f.hbuf[:, hl * CAP + te * 128:
                                             hl * CAP + (te + 1) * 128],
                                      f.w2b[:, hl * D + half * 512:
                                            hl * D + (half + 1) * 512],
                                      start=(hl == 0), stop=(hl == nhg - 1),
                                      skip_group_check=True,
                                      inc=S.s_bmm if hl == nhg - 1 else None)
                            h[("bmm", bstep)] = S.s_bmm.count
                            tr.wait("vector", S.s_bmm, h[("bmm", bstep)])
                            if e > 0 and gi == 0 and te == 0 and half == 0:
                                tr.wait("vector", S.d_st, h[("yst", e - 1)])
                            oslc = f.outacc[:, te * D + half * 512:
                                            te * D + (half + 1) * 512]
                            if gi == 0:
                                c = tr.op("vector", "tensor_copy", oslc,
                                          f.pB[p][:, :], inc=S.s_bep)
                            else:
                                c = tr.op("vector", "tensor_tensor", oslc,
                                          f.pB[p][:, :], oslc, ALU.add,
                                          inc=S.s_bep)
                            h[("bep", bstep)] = c
                            bstep += 1
                    tr.op("tensor", "nop", inc=S.s_w2c)
                h[("aend", e)] = S.s_amm.count
                for te in range(NTE):
                    tr.wait("gpsimd", S.s_bep, h[("bep", bstep - 1)])
                    c = tr.dma("gpsimd",
                               y_d[e * CAP + te * 128:
                                   e * CAP + (te + 1) * 128, :],
                               f.outacc[:, te * D:(te + 1) * D], S.d_st)
                h[("yst", e)] = c

            tr.wait("gpsimd", S.d_st, h[("yst", E - 1)])
            for tc in range(NCH):
                P = tc % 2
                gg1 = f.g1g if P == 0 else f.g2g
                gg2 = f.cst[0] if P == 0 else f.cst[1]
                xb = f.xstg[P]
                ob = f.xnf[P]
                dgb = S.d_gb0 if P == 0 else S.d_gb1
                if tc >= 2:
                    tr.wait("gpsimd", S.s_dv, h[("res", tc - 2)])
                c = tr.op("gpsimd", "indirect_dma_start",
                          out=gg1[:, :], out_offset=None,
                          in_=y_d[:, :],
                          in_offset=bass.IndirectOffsetOnAxis(
                              f.gIdx[:, 2 * tc:2 * tc + 1], 0),
                          inc=dgb, incn=16)
                c = tr.op("gpsimd", "indirect_dma_start",
                          out=gg2[:, :], out_offset=None,
                          in_=y_d[:, :],
                          in_offset=bass.IndirectOffsetOnAxis(
                              f.gIdx[:, 2 * tc + 1:2 * tc + 2], 0),
                          inc=dgb, incn=16)
                h[("gbs", tc)] = dgb.count
                c = tr.dma("sync", xb[:, :],
                           src[tc * 128:(tc + 1) * 128, :], S.d_x)
                h[("rld", tc)] = c
                tr.wait("vector", S.d_x, h[("rld", tc)])
                tr.wait("vector", dgb, h[("gbs", tc)])
                tr.opw("vector", "scalar_tensor_tensor", ob[:, :],
                       gg1[:, :], f.c12[:, 2 * tc:2 * tc + 1],
                       xb[:, :], ALU.mult, ALU.add, inc=S.s_dv)
                c = tr.op("vector", "scalar_tensor_tensor", ob[:, :],
                          gg2[:, :], f.c12[:, 2 * tc + 1:2 * tc + 2],
                          ob[:, :], ALU.mult, ALU.add, inc=S.s_dv)
                h[("res", tc)] = c
                tr.wait("sync", S.s_dv, h[("res", tc)])
                c = tr.dma("sync", dst[tc * 128:(tc + 1) * 128, :],
                           ob[:, :], S.d_st)
            tr.wait("gpsimd", S.d_st, S.d_st.count)
            tr.wait("sync", S.d_st, S.d_st.count)


        if "p1" in phases:
            with ExitStack() as c1:
                f1 = make_ffn_tensors(c1, "f1_")
                run_phase(nc, allsems,
                          lambda tr: emit_ffn(tr, f1, x_in, stream,
                                              do_casts=True))

        if "p2" in phases:
          with ExitStack() as cp2:
            Stot = sb(cp2, "Stot", [64, NH * 128])
            with ExitStack() as c2:
                a = NS()
                a.wqkvT = sb(c2, "a_wqkvT", [128, NKC * 3 * D], BF16)
                dbl = lambda nm, shape, dt=F32: [
                    sb(c2, "a_" + nm + str(i), shape, dt) for i in range(2)]
                a.xres1 = dbl("xres1", [128, D])
                a.cst = dbl("cst", [128, D])
                a.sqst = dbl("sqst", [128, D])
                a.xntm = dbl("xntm", [128, D])
                a.xnbf1 = dbl("xnbf1", [128, D], BF16)
                a.qstage = dbl("qstage", [128, D])
                a.qsm = dbl("qsm", [128, D])
                a.kstage = dbl("kstage", [128, D])
                a.kexp = dbl("kexp", [128, D])
                a.kexpb = dbl("kexpb", [128, D], BF16)
                a.vbf = dbl("vbf", [128, D], BF16)
                a.kcm = dbl("kcm", [64, NH * 128])
                a.qscm = dbl("qscm", [64, NH * 128])
                a.schst = dbl("schst", [64, NH * 128])
                a.r1 = dbl("r1", [128, 1])
                a.nmean = dbl("nmean", [128, 1])
                a.ssq = dbl("ssq", [128, 1])
                a.sd = dbl("sd", [128, 1])
                a.rstd = dbl("rstd", [128, 1])
                a.qm16 = dbl("qm16", [128, NH])
                a.qsum16 = dbl("qsum16", [128, NH])
                a.qr16 = dbl("qr16", [128, NH])
                a.kmaxc = sb(c2, "a_kmaxc", [64, NH])
                a.onesb = sb(c2, "a_onesb", [128, 1], BF16)
                a.epsb = sb(c2, "a_epsb", [128, 1])
                a.pQ = [ps(c2, "a_pQ%d" % i, [128, 512]) for i in range(2)]
                a.pT = [ps(c2, "a_pT%d" % i, [128, 128]) for i in range(2)]
                a.pS = ps(c2, "a_pS", [64, NH * 128])

                def emit_p2a(tr):
                    h = tr.hist
                    tr.wait("sync", S.d_wc, xcounts.get(("wc_qkv",)))
                    for k in range(NKC):
                        for s2 in range(2):
                            c = tr.dma("sync",
                                       a.wqkvT[:, k * 3 * D + s2 * 1536:
                                               k * 3 * D + (s2 + 1) * 1536],
                                       wqkv_bf[s2 * 1536:(s2 + 1) * 1536,
                                               k * 128:(k + 1) * 128],
                                       S.d_w, transpose=True)
                    h["wq"] = c
                    tr.op("vector", "memset", a.onesb[:, :], 1.0)
                    tr.op("vector", "memset", a.epsb[:, :], LNEPS)
                    tr.op("vector", "memset", Stot[:, :], 0.0)
                    c0 = tr.op("vector", "memset", a.pS[:, :], 0.0,
                               inc=S.s_dv2)
                    trk = 0
                    for n in range(NCH):
                        P = n % 2
                        if n >= 2:
                            tr.wait("sync", S.s_dv2, h[("xdone", n - 2)])
                        c = tr.dma("sync", a.xres1[P][:, :],
                                   stream[n * 128:(n + 1) * 128, :], S.d_x)
                        h[("xld", n)] = c
                        tr.wait("vector", S.d_x, h[("xld", n)])
                        if n >= 2:
                            tr.wait("vector", S.s_ac2, h[("kexp", n - 2)])
                            tr.wait("vector", S.s_pe2, h[("smm", n - 2)])
                            tr.wait("vector", S.s_tr, h[("trlast", n - 2)])
                            tr.wait("vector", S.d_sp, h[("spst", n - 2)])
                        tr.opw("vector", "reduce_sum", a.r1[P][:, :],
                               a.xres1[P][:, :], axis=AX.X, inc=S.s_dv2)
                        tr.opw("vector", "tensor_scalar", a.nmean[P][:, :],
                               a.r1[P][:, :], -1.0 / D, None, ALU.mult,
                               inc=S.s_dv2)
                        c = tr.op("vector", "tensor_scalar", a.cst[P][:, :],
                                  a.xres1[P][:, :], a.nmean[P][:, :], None,
                                  ALU.add, inc=S.s_dv2)
                        h[("xdone", n)] = c
                        tr.opw("vector", "scalar_tensor_tensor",
                               a.sqst[P][:, :], a.cst[P][:, :], 1.0,
                               a.cst[P][:, :], ALU.mult, ALU.mult,
                               accum_out=a.ssq[P][:, :], inc=S.s_dv2)
                        tr.wait("scalar", S.s_dv2, S.s_dv2.count)
                        tr.opw("scalar", "activation", a.sd[P][:, :],
                               a.ssq[P][:, :], AF.Ln, scale=1.0 / D,
                               bias=a.epsb[:, :], inc=S.s_ac2)
                        c = tr.op("scalar", "activation", a.rstd[P][:, :],
                                  a.sd[P][:, :], AF.Exp, scale=-0.5,
                                  inc=S.s_ac2)
                        tr.wait("vector", S.s_ac2, c)
                        cxn = tr.op("vector", "tensor_scalar",
                                    a.xntm[P][:, :], a.cst[P][:, :],
                                    a.rstd[P][:, :], None, ALU.mult,
                                    inc=S.s_dv2)
                        tr.wait("tensor", S.s_dv2, cxn)
                        if n >= 2:
                            tr.wait("tensor", S.s_pe2, h[("qkvl", n - 2)])
                        for k in range(NKC):
                            slot = trk % 2
                            if trk >= 2:
                                tr.wait("tensor", S.s_tev, h[("tev", trk - 2)])
                            c = tr.op("tensor", "transpose", a.pT[slot][:, :],
                                      a.xntm[P][:, k * 128:(k + 1) * 128],
                                      identf[:, :], inc=S.s_tr)
                            h[("tr", trk)] = c
                            tr.wait("vector", S.s_tr, h[("tr", trk)])
                            c = tr.op("vector", "tensor_copy",
                                      a.xnbf1[P][:, k * 128:(k + 1) * 128],
                                      a.pT[slot][:, :], inc=S.s_tev)
                            h[("tev", trk)] = c
                            trk += 1
                        tr.wait("tensor", S.s_tev, h[("tev", trk - 1)])
                        if n == 0:
                            tr.wait("tensor", S.d_w, h["wq"])
                        for qs in range(6):
                            p = qs % 2
                            if n > 0 or qs >= 2:
                                tr.wait("tensor", S.s_dv2,
                                        h[("qev", n if qs >= 2 else n - 1,
                                           qs - 2 if qs >= 2 else qs + 4)])
                            for k in range(NKC):
                                c = tr.op("tensor", "matmul", a.pQ[p][:, :],
                                          a.xnbf1[P][:, k * 128:
                                                     (k + 1) * 128],
                                          a.wqkvT[:, k * 3 * D + qs * 512:
                                                  k * 3 * D + (qs + 1) * 512],
                                          start=(k == 0), stop=(k == NKC - 1),
                                          inc=S.s_pe2
                                          if k == NKC - 1 else None)
                            h[("qkv", n, qs)] = S.s_pe2.count
                            if qs == 5:
                                h[("qkvl", n)] = S.s_pe2.count
                            tr.wait("vector", S.s_pe2, h[("qkv", n, qs)])
                            if qs < 2:
                                dstap = a.qstage[P][:, qs * 512:
                                                   (qs + 1) * 512]
                            elif qs < 4:
                                dstap = a.kstage[P][:, (qs - 2) * 512:
                                                   (qs - 1) * 512]
                            else:
                                dstap = a.vbf[P][:, (qs - 4) * 512:
                                                (qs - 3) * 512]
                            c = tr.op("vector", "tensor_copy", dstap,
                                      a.pQ[p][:, :], inc=S.s_dv2)
                            h[("qev", n, qs)] = c
                        if n >= 2:
                            tr.wait("scalar", S.s_tr, h[("trlast", n - 2)])
                        tr.opw("vector", "reduce_max", a.qm16[P][:, :],
                               a.qstage[P][:, :].rearrange(
                                   "p (h c) -> p h c", c=HS),
                               axis=AX.X, negate=True, inc=S.s_dv2)
                        for hd in range(NH):
                            c = tr.op("vector", "tensor_scalar",
                                      a.qsm[P][:, hd * HS:(hd + 1) * HS],
                                      a.qstage[P][:, hd * HS:(hd + 1) * HS],
                                      a.qm16[P][:, hd:hd + 1], None, ALU.add,
                                      inc=S.s_dv2)
                        tr.wait("vector", S.s_dv2, c)
                        tr.wait("scalar", S.s_dv2, c)
                        c = tr.op("scalar", "activation", a.qsm[P][:, :],
                                  a.qsm[P][:, :], AF.Exp, inc=S.s_ac2)
                        tr.wait("vector", S.s_ac2, c)
                        tr.opw("vector", "reduce_sum", a.qsum16[P][:, :],
                               a.qsm[P][:, :].rearrange(
                                   "p (h c) -> p h c", c=HS),
                               axis=AX.X, inc=S.s_dv2)
                        tr.opw("vector", "reciprocal", a.qr16[P][:, :],
                               a.qsum16[P][:, :], inc=S.s_dv2)
                        for hd in range(NH):
                            c = tr.op("vector", "tensor_scalar",
                                      a.qsm[P][:, hd * HS:(hd + 1) * HS],
                                      a.qsm[P][:, hd * HS:(hd + 1) * HS],
                                      a.qr16[P][:, hd:hd + 1], 0.125,
                                      ALU.mult, ALU.mult, inc=S.s_dv2)
                        tr.wait("vector", S.s_dv2, c)
                        c = tr.op("vector", "nop", inc=S.s_dv2)
                        tr.wait("scalar", S.s_dv2, c)
                        c = tr.op("scalar", "activation", a.kexp[P][:, :],
                                  a.kstage[P][:, :], AF.Exp, inc=S.s_ac2)
                        h[("kexp", n)] = c
                        tr.wait("vector", S.s_ac2, c)
                        cvb = tr.op("vector", "tensor_copy",
                                    a.kexpb[P][:, :], a.kexp[P][:, :],
                                    inc=S.s_dv2)
                        tr.wait("tensor", S.s_dv2, cvb)
                        if n > 0:
                            tr.wait("tensor", S.s_dv2, h[("sdone", n - 1)])
                        else:
                            tr.wait("tensor", S.s_dv2, c0)
                        for hd in range(NH):
                            tr.op("tensor", "matmul",
                                  a.pS[0:64, hd * 128:hd * 128 + 64],
                                  a.kexpb[P][:, hd * HS:(hd + 1) * HS],
                                  a.vbf[P][:, hd * HS:(hd + 1) * HS],
                                  start=(hd % 4 == 0), stop=False,
                                  skip_group_check=True)
                            c = tr.op("tensor", "matmul",
                                      a.pS[0:64, hd * 128 + 64:
                                           hd * 128 + 65],
                                      a.kexpb[P][:, hd * HS:(hd + 1) * HS],
                                      a.onesb[:, :], start=False,
                                      stop=(hd % 4 == 3),
                                      skip_group_check=True,
                                      inc=S.s_pe2 if hd == NH - 1 else None)
                        h[("smm", n)] = S.s_pe2.count
                        for hd in range(NH):
                            for srcap, dstt in (
                                    (a.kstage[P][:, hd * HS:(hd + 1) * HS],
                                     a.kcm[P]),
                                    (a.qsm[P][:, hd * HS:(hd + 1) * HS],
                                     a.qscm[P])):
                                slot = trk % 2
                                if trk >= 2:
                                    tr.wait("tensor", S.s_tev,
                                            h[("tev", trk - 2)])
                                c = tr.op("tensor", "transpose",
                                          a.pT[slot][0:64, :], srcap,
                                          identf[:, :], inc=S.s_tr)
                                h[("tr", trk)] = c
                                tr.wait("vector", S.s_tr, h[("tr", trk)])
                                c = tr.op("vector", "tensor_copy",
                                          dstt[:, hd * 128:(hd + 1) * 128],
                                          a.pT[slot][0:64, :], inc=S.s_tev)
                                h[("tev", trk)] = c
                                trk += 1
                        cktr = S.s_tev.count
                        h[("trlast", n)] = S.s_tr.count
                        for hd in range(NH):
                            cr = tr.op("vector", "reduce_max",
                                       a.kmaxc[:, hd:hd + 1],
                                       a.kcm[P][:, hd * 128:(hd + 1) * 128],
                                       axis=AX.X,
                                       inc=S.s_dv2 if hd == NH - 1 else None)
                        tr.wait("vector", S.s_dv2, cr)
                        tr.opw("vector", "tensor_tensor", kmaxloc[:, :],
                               kmaxloc[:, :], a.kmaxc[:, :], ALU.max,
                               inc=S.s_dv2)
                        tr.wait("vector", S.s_pe2, h[("smm", n)])
                        if n >= 2:
                            tr.wait("vector", S.d_sp, h[("spst", n - 2)])
                        tr.opw("vector", "tensor_copy", a.schst[P][:, :],
                               a.pS[:, :], inc=S.s_dv2)
                        c = tr.op("vector", "tensor_tensor", Stot[:, :],
                                  Stot[:, :], a.schst[P][:, :], ALU.add,
                                  inc=S.s_dv2)
                        h[("sdone", n)] = c
                        tr.wait("gpsimd", S.s_tev, cktr)
                        tr.dma("gpsimd", kcm_d[n, :, :], a.kcm[P][:, :],
                               S.d_sp)
                        tr.dma("gpsimd", qscm_d[n, :, :], a.qscm[P][:, :],
                               S.d_sp)
                        tr.wait("gpsimd", S.s_dv2, h[("qev", n, 5)])
                        tr.dma("gpsimd", vtm_d[n * 128:(n + 1) * 128, :],
                               a.vbf[P][:, :], S.d_sp)
                        tr.wait("gpsimd", S.s_dv2, h[("sdone", n)])
                        c = tr.dma("gpsimd", schunks_d[n, :, :],
                                   a.schst[P][:, :], S.d_sp)
                        h[("spst", n)] = c
                    tr.wait("gpsimd", S.d_sp, S.d_sp.count)
                    cl = tr.op("vector", "nop", inc=S.s_dv2)
                    tr.wait("gpsimd", S.s_dv2, cl)
                    tr.dma("gpsimd", ccsrc_d[:, 0:16], kmaxloc[:, :], S.d_sp)
                    c = tr.dma("gpsimd", ccsrc_d[:, 16:CCW], Stot[:, :],
                               S.d_sp)
                    tr.wait("gpsimd", S.d_sp, c)
                    tr.op("gpsimd", "collective_compute", "AllGather",
                          ALU.bypass,
                          replica_groups=[list(range(NCORES))],
                          ins=[ccsrc_d.ap().opt()], outs=[ccdst_d.ap().opt()],
                          inc=S.cc_s)
                    tr.wait("gpsimd", S.cc_s, 1)

                run_phase(nc, allsems, emit_p2a)

            with ExitStack() as c3:
                b2 = NS()
                b2.woutT = sb(c3, "b_woutT", [128, NKC * D], BF16)
                dblb = lambda nm, shape, dt=F32: [
                    sb(c3, "b_" + nm + str(i), shape, dt) for i in range(2)]
                b2.gath = sb(c3, "b_gath", [64, CCW])
                b2.Scar = sb(c3, "b_Scar", [64, NH * 128])
                b2.Scarb = sb(c3, "b_Scarb", [64, NH * HS], BF16)
                b2.kcm = dblb("kcm", [64, NH * 128])
                b2.kexp = dblb("kexp", [64, NH * 128])
                b2.kexpb = dblb("kexpb", [64, NH * 128], BF16)
                b2.qscm = dblb("qscm", [64, NH * 128])
                b2.qtb = dblb("qtb", [64, NH * 128], BF16)
                b2.zfull = dblb("zfull", [64, NH * 128])
                b2.ztmp = dblb("ztmp", [64, NH * 128])
                b2.schk = dblb("schk", [64, NH * 128])
                b2.vchunk = dblb("vchunk", [128, D], BF16)
                b2.ofm = dblb("ofm", [128, D], BF16)
                b2.xres1 = dblb("xres1", [128, D])
                b2.abf = [sb(c3, "b_abf%d" % i, [128, 128], BF16)
                          for i in range(2)]
                b2.ones64 = sb(c3, "b_ones64", [64, 128])
                b2.pA = [ps(c3, "b_pA%d" % i, [128, 128]) for i in range(2)]
                b2.pO = ps(c3, "b_pO", [64, NH * 128])
                b2.pXO = [ps(c3, "b_pXO%d" % i, [128, 512]) for i in range(2)]

                def emit_p2b(tr):
                    h = tr.hist
                    tr.wait("sync", S.d_wc, xcounts.get(("wc_out",)))
                    for k in range(NKC):
                        c = tr.dma("sync", b2.woutT[:, k * D:(k + 1) * D],
                                   wout_bf[0:D, k * 128:(k + 1) * 128],
                                   S.d_w, transpose=True)
                    h["wo"] = c
                    tr.op("vector", "memset", b2.ones64[:, :], 1.0)
                    tr.op("vector", "memset", b2.Scar[:, :], 0.0)
                    tr.op("vector", "memset", kmaxloc[:, :], -1e38)
                    cm = tr.op("vector", "nop", inc=S.s_dv2)
                    for r in range(NCORES):
                        tr.wait("gpsimd", S.s_dv2,
                                cm if r == 0 else h[("gproc", r - 1)])
                        c = tr.dma("gpsimd", b2.gath[:, :],
                                   ccdst_d[r * 64:(r + 1) * 64, :], S.d_x)
                        h[("gld", r)] = c
                        tr.wait("vector", S.d_x, h[("gld", r)])
                        tr.opw("vector", "tensor_scalar", b2.kcm[0][:, 0:16],
                               b2.gath[:, 0:16], seqb[:, r:r + 1], None,
                               ALU.add, inc=S.s_dv2)
                        tr.opw("vector", "tensor_tensor", kmaxloc[:, :],
                               kmaxloc[:, :], b2.kcm[0][:, 0:16], ALU.max,
                               inc=S.s_dv2)
                        c = tr.opw("vector", "scalar_tensor_tensor",
                                   b2.Scar[:, :], b2.gath[:, 16:CCW],
                                   prefm[:, r:r + 1], b2.Scar[:, :],
                                   ALU.mult, ALU.add, inc=S.s_dv2)
                        h[("gproc", r)] = c
                    c = tr.op("vector", "nop", inc=S.s_dv2)
                    tr.wait("scalar", S.s_dv2, c)
                    c = tr.op("scalar", "activation", escale[:, :],
                              kmaxloc[:, :], AF.Exp, scale=-1.0, inc=S.s_ac2)
                    tr.wait("vector", S.s_ac2, c)
                    tr.opw("vector", "tensor_scalar", negm[:, :],
                           kmaxloc[:, :], -1.0, None, ALU.mult, inc=S.s_dv2)
                    for hd in range(NH):
                        crs = tr.op("vector", "tensor_scalar",
                                    b2.Scar[:, hd * 128:(hd + 1) * 128],
                                    b2.Scar[:, hd * 128:(hd + 1) * 128],
                                    escale[:, hd:hd + 1], None, ALU.mult,
                                    inc=S.s_dv2 if hd == NH - 1 else None)
                    tr.wait("vector", S.s_dv2, crs)
                    for hd in range(NH):
                        c = tr.op("vector", "tensor_copy",
                                  b2.Scarb[:, hd * HS:(hd + 1) * HS],
                                  b2.Scar[:, hd * 128:hd * 128 + 64],
                                  inc=S.s_dv2 if hd == NH - 1 else None)
                    h[("scb", -1)] = c

                    for n in range(NCH):
                        P = n % 2
                        if n == 0:
                            tr.wait("gpsimd", S.s_dv2,
                                    h[("gproc", NCORES - 1)])
                        if n >= 2:
                            tr.wait("gpsimd", S.s_dv2, h[("csc", n - 2)])
                        c = tr.dma("gpsimd", b2.kcm[P][:, :], kcm_d[n, :, :],
                                   S.d_x)
                        if n >= 2:
                            tr.wait("gpsimd", S.s_dv2, h[("qtb", n - 2)])
                        c = tr.dma("gpsimd", b2.qscm[P][:, :],
                                   qscm_d[n, :, :], S.d_x)
                        if n >= 2:
                            tr.wait("gpsimd", S.s_dv2, h[("sup", n - 2)])
                        c = tr.dma("gpsimd", b2.schk[P][:, :],
                                   schunks_d[n, :, :], S.d_x)
                        if n >= 2:
                            tr.wait("gpsimd", S.s_pe2, h[("omm", n - 2)])
                        c = tr.dma("gpsimd", b2.vchunk[P][:, :],
                                   vtm_d[n * 128:(n + 1) * 128, :], S.d_x)
                        if n >= 2:
                            tr.wait("gpsimd", S.d_st, h[("stc", n - 2)])
                        c = tr.dma("gpsimd", b2.xres1[P][:, :],
                                   stream[n * 128:(n + 1) * 128, :], S.d_x)
                        h[("ld", n)] = c
                        tr.wait("scalar", S.d_x, h[("ld", n)])
                        if n == 0:
                            tr.wait("scalar", S.s_dv2, h[("scb", -1)])
                        if n >= 2:
                            tr.wait("scalar", S.s_dv2, h[("csc", n - 2)])
                        c = tr.op("scalar", "activation", b2.kexp[P][:, :],
                                  b2.kcm[P][:, :], AF.Exp, inc=S.s_ac2)
                        h[("kexp", n)] = c
                        tr.wait("vector", S.s_ac2, h[("kexp", n)])
                        for hd in range(NH):
                            c = tr.op("vector", "tensor_scalar",
                                      b2.kexp[P][:, hd * 128:(hd + 1) * 128],
                                      b2.kexp[P][:, hd * 128:(hd + 1) * 128],
                                      escale[:, hd:hd + 1], None, ALU.mult,
                                      inc=S.s_dv2)
                        tr.wait("vector", S.s_dv2, c)
                        if n >= 2:
                            tr.wait("vector", S.s_pe2, h[("omm", n - 2)])
                        ckb = tr.op("vector", "tensor_copy",
                                    b2.kexpb[P][:, :], b2.kexp[P][:, :],
                                    inc=S.s_dv2)
                        for hd in range(NH):
                            csc = tr.op("vector", "tensor_tensor_scan",
                                        b2.zfull[P][:, hd * 128:
                                                    (hd + 1) * 128],
                                        b2.ones64[:, :],
                                        b2.kexp[P][:, hd * 128:
                                                   (hd + 1) * 128],
                                        b2.Scar[:, hd * 128 + 64:
                                                hd * 128 + 65],
                                        ALU.mult, ALU.add,
                                        inc=S.s_dv2 if hd == NH - 1 else None)
                        h[("csc", n)] = csc
                        tr.wait("vector", S.s_dv2, csc)
                        tr.opw("vector", "tensor_scalar", b2.ztmp[P][:, :],
                               b2.zfull[P][:, :], 1e-9, None, ALU.add,
                               inc=S.s_dv2)
                        tr.opw("vector", "reciprocal", b2.ztmp[P][:, :],
                               b2.ztmp[P][:, :], inc=S.s_dv2)
                        tr.wait("vector", S.d_x, h[("ld", n)])
                        cqt = tr.op("vector", "tensor_tensor",
                                    b2.qtb[P][:, :], b2.qscm[P][:, :],
                                    b2.ztmp[P][:, :], ALU.mult, inc=S.s_dv2)
                        h[("qtb", n)] = cqt
                        for hd in range(NH + 1):
                            if hd < NH:
                                slot = hd % 2
                                tr.wait("tensor", S.s_dv2, cqt)
                                if hd >= 2:
                                    tr.wait("tensor", S.s_dv2,
                                            h[("am", n, hd - 2)])
                                c = tr.op("tensor", "matmul",
                                          b2.pA[slot][:, :],
                                          b2.kexpb[P][:, hd * 128:
                                                      (hd + 1) * 128],
                                          b2.qtb[P][:, hd * 128:
                                                   (hd + 1) * 128],
                                          start=True, stop=True,
                                          skip_group_check=True, inc=S.s_pe2)
                                h[("amm", n, hd)] = c
                                tr.wait("vector", S.s_pe2, h[("amm", n, hd)])
                                c = tr.op("vector", "tensor_tensor",
                                          b2.abf[slot][:, :],
                                          b2.pA[slot][:, :],
                                          umaskf[:, :], ALU.mult, inc=S.s_dv2)
                                h[("am", n, hd)] = c
                            if hd >= 1:
                                hp = hd - 1
                                slotp = hp % 2
                                tr.wait("tensor", S.s_dv2, h[("am", n, hp)])
                                if n > 0 and hp == 0:
                                    tr.wait("tensor", S.s_dv2,
                                            h[("ofm", n - 1, NH - 1)])
                                tr.op("tensor", "matmul",
                                      b2.pO[0:64, hp * 128:(hp + 1) * 128],
                                      b2.Scarb[:, hp * HS:(hp + 1) * HS],
                                      b2.qtb[P][:, hp * 128:(hp + 1) * 128],
                                      start=(hp % 4 == 0), stop=False,
                                      skip_group_check=True)
                                c = tr.op("tensor", "matmul",
                                          b2.pO[0:64, hp * 128:
                                                (hp + 1) * 128],
                                          b2.vchunk[P][:, hp * HS:
                                                       (hp + 1) * HS],
                                          b2.abf[slotp][:, :],
                                          start=False, stop=(hp % 4 == 3),
                                          skip_group_check=True, inc=S.s_pe2)
                                h[("opair", n, hp)] = c
                                if hp % 4 == 3:
                                    tr.wait("vector", S.s_pe2,
                                            h[("opair", n, hp)])
                                    if hp == 3 and n >= 2:
                                        tr.wait("vector", S.s_pe2,
                                                h[("xomm", n - 2, 1)])
                                    for hq in range(hp - 3, hp + 1):
                                        sub = hq % 2
                                        kd = hq // 2
                                        c = tr.op("vector", "tensor_copy",
                                                  b2.ofm[P][sub * 64:
                                                            (sub + 1) * 64,
                                                            kd * 128:
                                                            (kd + 1) * 128],
                                                  b2.pO[0:64, hq * 128:
                                                        (hq + 1) * 128],
                                                  inc=S.s_dv2)
                                        h[("ofm", n, hq)] = c
                                        tr.op("vector",
                                              "scalar_tensor_tensor",
                                              b2.Scar[:, hq * 128:
                                                      (hq + 1) * 128],
                                              b2.schk[P][:, hq * 128:
                                                         (hq + 1) * 128],
                                              escale[:, hq:hq + 1],
                                              b2.Scar[:, hq * 128:
                                                      (hq + 1) * 128],
                                              ALU.mult, ALU.add)
                        h[("omm", n)] = S.s_pe2.count
                        cupd = tr.op("vector", "nop", inc=S.s_dv2)
                        tr.wait("vector", S.s_dv2, cupd)
                        c = tr.opw("vector", "tensor_copy",
                                   b2.Scarb[:, :].rearrange(
                                       "p (h c) -> p h c", c=HS),
                                   b2.Scar[:, :].rearrange(
                                       "p (h c) -> p h c",
                                       c=128)[:, :, 0:64],
                                   inc=S.s_dv2)
                        h[("sup", n)] = c
                        if n == 0:
                            tr.wait("tensor", S.d_w, h["wo"])
                        for half in range(2):
                            tr.wait("tensor", S.s_dv2, h[("ofm", n, NH - 1)])
                            if n > 0:
                                tr.wait("tensor", S.s_dv2,
                                        h[("xadd", n - 1, half)])
                            for kd in range(NKC):
                                c = tr.op("tensor", "matmul",
                                          b2.pXO[half][:, :],
                                          b2.ofm[P][:, kd * 128:
                                                    (kd + 1) * 128],
                                          b2.woutT[:, kd * D + half * 512:
                                                   kd * D + (half + 1) * 512],
                                          start=(kd == 0),
                                          stop=(kd == NKC - 1),
                                          inc=S.s_pe2
                                          if kd == NKC - 1 else None)
                            h[("xomm", n, half)] = S.s_pe2.count
                            tr.wait("vector", S.s_pe2, h[("xomm", n, half)])
                            c = tr.op("vector", "tensor_tensor",
                                      b2.xres1[P][:, half * 512:
                                                  (half + 1) * 512],
                                      b2.xres1[P][:, half * 512:
                                                  (half + 1) * 512],
                                      b2.pXO[half][:, :], ALU.add,
                                      inc=S.s_dv2)
                            h[("xadd", n, half)] = c
                        tr.wait("gpsimd", S.s_dv2, h[("xadd", n, 1)])
                        c = tr.dma("gpsimd", stream[n * 128:(n + 1) * 128, :],
                                   b2.xres1[P][:, :], S.d_st)
                        h[("stc", n)] = c
                    tr.wait("gpsimd", S.d_st, S.d_st.count)

                run_phase(nc, allsems, emit_p2b)


        if "p3" in phases:
            with ExitStack() as c4:
                f3 = make_ffn_tensors(c4, "f3_")
                run_phase(nc, allsems,
                          lambda tr: emit_ffn(tr, f3, stream, y_out))
        elif "copy" in phases:
            def emit_copy(tr):
                c = tr.dma("gpsimd", y_out[:, :], stream[:, :], S.d_st)
                if dump:
                    c = tr.dma("gpsimd", xg_dump[:, :], xg_d[:, :], S.d_st)
                    c = tr.dma("gpsimd", yd_dump[:, :],
                               y_d[0:E * CAP, :], S.d_st)
                    c = tr.dma("gpsimd", xn_dump[:, :], xnbf_d[:, :], S.d_st)
                tr.wait("gpsimd", S.d_st, S.d_st.count)

            run_phase(nc, allsems, emit_copy)

    return nc


def host_consts():
    identf = np.eye(128, dtype=np.float32)
    umask = np.triu(np.ones((128, 128), dtype=np.float32))
    slotb = np.broadcast_to(
        (np.arange(E, dtype=np.float32) * CAP - 1.0)[None, :],
        (128, E)).copy()
    prefm = []
    seqb = []
    for c in range(NCORES):
        pm = np.zeros((64, 8), dtype=np.float32)
        sb_ = np.full((64, 8), -1e38, dtype=np.float32)
        for r in range(NCORES):
            if r // 4 == c // 4:
                sb_[:, r] = 0.0
                if r % 4 < c % 4:
                    pm[:, r] = 1.0
        prefm.append(pm)
        seqb.append(sb_)
    return identf, umask, slotb, prefm, seqb


def kernel(L0_attn_x, L0_attn_W_qkv, L0_attn_W_out, L0_ffn_gate_w,
           L0_ffn_w1, L0_ffn_w3, L0_ffn_w2):
    x = np.ascontiguousarray(np.asarray(L0_attn_x, dtype=np.float32))
    wqkv = np.ascontiguousarray(np.asarray(L0_attn_W_qkv, dtype=np.float32))
    wout = np.ascontiguousarray(np.asarray(L0_attn_W_out, dtype=np.float32))
    gw = np.ascontiguousarray(np.asarray(L0_ffn_gate_w, dtype=np.float32))
    w1 = np.ascontiguousarray(np.asarray(L0_ffn_w1, dtype=np.float32))
    w3 = np.ascontiguousarray(np.asarray(L0_ffn_w3, dtype=np.float32))
    w2 = np.ascontiguousarray(np.asarray(L0_ffn_w2, dtype=np.float32))

    identf, umask, slotb, prefm, seqb = host_consts()
    nc = build_program()
    in_maps = []
    for c in range(NCORES):
        b = c // 4
        t0 = (c % 4) * TPC
        in_maps.append({
            "x": np.ascontiguousarray(x[b, t0:t0 + TPC, :]),
            "wqkv": wqkv, "wout": wout, "gw": gw,
            "w1": w1, "w3": w3, "w2": w2,
            "identf": identf, "umask": umask, "slotb": slotb,
            "prefm": prefm[c], "seqb": seqb[c],
        })
    res = run_bass_kernel_spmd(nc, in_maps, core_ids=list(range(NCORES)))
    out = np.empty((B, T, D), dtype=np.float32)
    for c in range(NCORES):
        b = c // 4
        t0 = (c % 4) * TPC
        out[b, t0:t0 + TPC, :] = np.asarray(res.results[c]["y"],
                                            dtype=np.float32)
    return out
